# version 13
# baseline (speedup 1.0000x reference)
import numpy as np
import concourse.bass as bass
import concourse.mybir as mybir
from concourse.bass_utils import run_bass_kernel_spmd

F32 = mybir.dt.float32
BF16 = mybir.dt.bfloat16
AF = mybir.ActivationFunctionType
ALU = mybir.AluOpType
ET = mybir.EngineType

D = 1024
KD = 8
TL = 2048
G = 512
NGRP = TL // G
L = 128
NCH = G // L
NH = 16
HD = 64
DFF0 = 4096
FK = 3584
DFF1 = 4096
NE = 8

INV_SQRT32 = float(1.0 / np.sqrt(32.0))


ENG_OF = {"te": "tensor", "ve": "vector", "se": "scalar", "sy": "sync", "gp": "gpsimd"}
NSY, NGP = 8, 6
CN_ENG = {"pe": "te", "dv": "ve", "ac": "se", "po": "gp"}
for _i in range(NSY):
    CN_ENG[f"da{_i}"] = "__dma__"
for _i in range(NGP):
    CN_ENG[f"db{_i}"] = "__dma__"


class Ctr:
    def __init__(self, nc, name):
        self.h = nc.alloc_semaphore(name)
        self.n = 0
        self.name = name


FULL_SYNC = False
FS_CTRS = None
SAME_ENGINE_SKIP = True


class BB:

    def __init__(self, nc):
        self.nc = nc
        self.c = {k: Ctr(nc, "sem_" + k) for k in ("pe", "dv", "ac", "po")}
        for i in range(NSY):
            self.c[f"da{i}"] = Ctr(nc, f"sem_da{i}")
        for i in range(NGP):
            self.c[f"db{i}"] = Ctr(nc, f"sem_db{i}")
        self._rr = {"sy": 0, "gp": 0}
        self.conflicts = {}
        self.marks = {}
        self.wm = {e: {} for e in ENG_OF}

    def conflict(self, a, b):
        self.conflicts.setdefault(a, set()).add(b)
        self.conflicts.setdefault(b, set()).add(a)

    def eng(self, e):
        return getattr(self.nc, ENG_OF[e])

    def _wait(self, e, ctr_name, val):
        if val <= 0:
            return
        if self.wm[e].get(ctr_name, 0) >= val:
            return
        self.eng(e).wait_ge(self.c[ctr_name].h, val)
        self.wm[e][ctr_name] = val

    def do(self, e, ctr, reads, writes, f, inc=1):
        need = {}
        if FULL_SYNC and (FULL_SYNC is True or e in FULL_SYNC):
            for _cn, _ctr in self.c.items():
                if FS_CTRS is None or _cn in FS_CTRS:
                    need[_cn] = _ctr.n
        for k in reads:
            for kk in (k, *self.conflicts.get(k, ())):
                m = self.marks.get(kk)
                if m and m["w"]:
                    cn, v = m["w"]
                    need[cn] = max(need.get(cn, 0), v)
        for k in writes:
            for kk in (k, *self.conflicts.get(k, ())):
                m = self.marks.get(kk)
                if m:
                    if m["w"]:
                        cn, v = m["w"]
                        need[cn] = max(need.get(cn, 0), v)
                    for cn, v in m["r"].items():
                        need[cn] = max(need.get(cn, 0), v)
        if SAME_ENGINE_SKIP:
            for cn in list(need):
                if CN_ENG[cn] == e:
                    need.pop(cn)
        for cn, v in sorted(need.items()):
            self._wait(e, cn, v)
        ins = f(self.eng(e))
        assert ins is not None, "emitter must return the last instruction"
        ins.then_inc(self.c[ctr].h, inc)
        self.c[ctr].n += inc
        cnt = self.c[ctr].n
        for k in writes:
            self.marks[k] = {"w": (ctr, cnt), "r": {}}
        for k in reads:
            m = self.marks.setdefault(k, {"w": None, "r": {}})
            m["r"][ctr] = max(m["r"].get(ctr, 0), cnt)
        return ins

    def dma(self, e, reads, writes, f):
        n = NSY if e == "sy" else NGP
        i = self._rr[e]
        self._rr[e] = (i + 1) % n
        cn = (f"da{i}" if e == "sy" else f"db{i}")
        self._wait(e, cn, self.c[cn].n)
        return self.do(e, cn, reads, writes, f, inc=16)

    def fence_all(self, e):
        for cn, ctr in self.c.items():
            self._wait(e, cn, ctr.n)



def _c(a):
    return np.ascontiguousarray(a)


def prep_host(p):
    h = {}
    _ = np.asarray(p["L0_ffn_Wg"])
    x = np.asarray(p["L0_attn_x"])
    h["x_fm"] = {}
    for b in range(4):
        h["x_fm"][(b, 0)] = _c(x[b, :2048].T)
        h["x_fm"][(b, 1)] = _c(x[b, 2048:].T)
    h["wqkvT"] = _c(np.asarray(p["L0_attn_w_qkv"]).T.reshape(8, 128, 3072)
                    .transpose(1, 0, 2))
    h["woT"] = _c(np.asarray(p["L0_attn_w_out"]).T.reshape(8, 128, 1024)
                  .transpose(1, 0, 2))
    om = np.asarray(p["L0_attn_omega"])
    h["omegaT"] = _c(om.transpose(2, 1, 0, 3).reshape(64, 1024))
    h["quads"] = _c(np.stack([np.asarray(p["L0_attn_quad_nodes"]),
                              np.asarray(p["L0_attn_quad_weights"])], 1))
    h["bqkv"] = _c(np.asarray(p["L0_attn_b_qkv"]).reshape(3, 1024))
    h["w1f0"] = _c(np.asarray(p["L0_ffn_w1"]).reshape(8, 128, 32, 128)
                   .transpose(2, 1, 0, 3))
    h["w2f0"] = _c(np.asarray(p["L0_ffn_w2"]).reshape(32, 128, 8, 128)
                   .transpose(2, 1, 0, 3))
    for k in ("tm_w1", "td_w1", "td_w2"):
        h[k] = _c(np.asarray(p["L1_attn_" + k]))
    for k in ("Wr", "Wk", "Wo", "Wr_ffn"):
        h[k] = _c(np.asarray(p["L1_attn_" + k]).reshape(8, 128, 8, 128)
                  .transpose(2, 1, 0, 3))
    for k in ("Wv", "Wg"):
        h[k] = _c(np.asarray(p["L1_attn_" + k]).reshape(8, 128, 1024)
                  .transpose(1, 0, 2))
    h["Wk_ffn"] = _c(np.asarray(p["L1_attn_Wk_ffn"]).reshape(8, 128, 28, 128)
                     .transpose(2, 1, 0, 3))
    h["Wv_ffn"] = _c(np.asarray(p["L1_attn_Wv_ffn"]).reshape(28, 128, 8, 128)
                     .transpose(2, 1, 0, 3))
    h["tm_w2"] = _c(np.asarray(p["L1_attn_tm_w2"]))
    h["rwT"] = _c(np.asarray(p["L1_ffn_router_w"]).T)
    h["m_b1"] = _c(np.asarray(p["L1_ffn_b1"]))
    vecs = []
    for name in PV_ORDER:
        if name == "u_flat":
            vecs.append(np.asarray(p["L1_attn_time_faaaa"]).reshape(1024))
        elif name == "b_out":
            vecs.append(np.asarray(p["L0_attn_b_out"]))
        elif name == "bq":
            vecs.append(np.asarray(p["L0_attn_b_qkv"])[0:1024])
        elif name == "bk":
            vecs.append(np.asarray(p["L0_attn_b_qkv"])[1024:2048])
        elif name.startswith("m_b2_"):
            e = int(name.split("_")[-1])
            vecs.append(np.asarray(p["L1_ffn_b2"])[e])
        elif name == "router_b8":
            rb = np.zeros(1024, np.float32)
            rb[:8] = np.asarray(p["L1_ffn_router_b"])
            vecs.append(rb)
        else:
            vecs.append(np.asarray(p["L1_attn_" + name]))
    h["pvecT"] = _c(np.stack(vecs, axis=1).astype(np.float32))
    h["rbL"] = _c(np.tile(np.asarray(p["L1_ffn_router_b"])[None, :], (128, 1)).astype(np.float32))
    h["m_b1L"] = _c(np.asarray(p["L1_ffn_b1"]).reshape(8, 64, 128).transpose(2, 1, 0).reshape(128, 512))
    h["m_w1T"] = [_c(np.asarray(p["L1_ffn_w1"])[ex].T.reshape(8, 128, 64, 128)
                     .transpose(2, 1, 0, 3)) for ex in range(8)]
    h["m_w2T"] = [_c(np.asarray(p["L1_ffn_w2"])[ex].T.reshape(32, 128, 8, 128)
                     .transpose(2, 1, 0, 3)) for ex in range(8)]
    h["lnxgrow"] = _c(np.tile(np.asarray(p["L1_attn_lnx_g"])[None, :], (128, 1)))
    h["lnxbrow"] = _c(np.tile(np.asarray(p["L1_attn_lnx_b"])[None, :], (128, 1)))
    qn = np.asarray(p["L0_attn_quad_nodes"])
    h["scolh"] = _c(qn[(np.arange(128) // 32) % 2].reshape(128, 1).astype(np.float32))
    return h


L1A_W_SHAPES = {
    "tm_w1": (D, 160), "td_w1": (D, 64), "td_w2": (64, D),
    "Wr": (KD, 128, KD, 128), "Wk": (KD, 128, KD, 128),
    "Wo": (KD, 128, KD, 128), "Wr_ffn": (KD, 128, KD, 128),
    "Wv": (128, KD, 1024), "Wg": (128, KD, 1024),
    "Wk_ffn": (28, 128, KD, 128), "Wv_ffn": (KD, 128, 28, 128),
}

PV_ORDER = [
    "b_out", "bq", "bk",
    "ln0_g", "ln0_b", "ln1_g", "ln1_b", "ln2_g", "ln2_b", "lnx_g", "lnx_b",
    "time_maa_x", "time_maa_w", "time_maa_k", "time_maa_v", "time_maa_r", "time_maa_g",
    "time_decay", "u_flat", "ffn_maa_k", "ffn_maa_r",
    "m_b2_0", "m_b2_1", "m_b2_2", "m_b2_3", "m_b2_4", "m_b2_5", "m_b2_6", "m_b2_7",
    "router_b8",
]
PV = {n: i for i, n in enumerate(PV_ORDER)}
NV = len(PV_ORDER)



def make_consts(nc):
    c = {}
    tri_incl = np.triu(np.ones((L, L), np.float32))
    tri_strict = np.triu(np.ones((L, L), np.float32), 1)
    c["m_incl"] = nc.inline_tensor(tri_incl, "m_incl")
    c["m_strict"] = nc.inline_tensor(tri_strict, "m_strict")
    c["ident_b"] = nc.inline_tensor(np.eye(128, dtype=np.float32), "ident_b")
    c["ident_f"] = nc.inline_tensor(np.eye(128, dtype=np.float32), "ident_f")
    hsel = np.zeros((128, 2), np.float32)
    hsel[0:64, 0] = 1.0
    hsel[64:128, 1] = 1.0
    c["hsel"] = nc.inline_tensor(hsel, "hsel")
    c["hselT"] = nc.inline_tensor(_c(hsel.T), "hselT")
    rsel = np.zeros((2, 128), np.float32)
    for pp in range(128):
        rsel[(pp // 32) % 2, pp] = 1.0
    c["rsel"] = nc.inline_tensor(rsel, "rsel")
    rselw = np.zeros((2, 1024), np.float32)
    for j in range(1024):
        rselw[(j // 32) % 2, j] = 1.0
    c["rselw"] = nc.inline_tensor(rselw, "rselw")
    c["ones_col"] = nc.inline_tensor(np.ones((128, 1), np.float32), "ones_col")
    c["ones_row"] = nc.inline_tensor(np.ones((1, 128), np.float32), "ones_row")
    return c



class MK:

    def __init__(self, launch, dbg=False, gw=False):
        self.launch = launch
        self.dbg = dbg
        self.gw = gw
        nc = bass.Bass("TRN2", target_bir_lowering=False, detect_race_conditions=False)
        self.nc = nc
        self.io = {}
        self.sb = {}
        self.ps = {}
        self.consts = make_consts(nc)

    def tap(self, B, name, key, ap, shape):
        if not self.dbg:
            return
        if isinstance(self.dbg, (set, frozenset)) and name not in self.dbg:
            return
        out = self.nc.dram_tensor("dbg_" + name, list(shape), F32, kind="ExternalOutput").ap()
        B.dma("gp", [key], [], lambda e: e.dma_start(out=out, in_=ap))

    def dram_in(self, name, shape):
        self.io[name] = self.nc.dram_tensor(name, list(shape), F32, kind="ExternalInput").ap()
        return self.io[name]

    def dram_out(self, name, shape):
        self.io[name] = self.nc.dram_tensor(name, list(shape), F32, kind="ExternalOutput").ap()
        return self.io[name]

    def sbuf(self, name, shape, dtype):
        t = self.nc.alloc_sbuf_tensor(name, list(shape), dtype)
        self.sb[name] = t
        return t

    def psum(self, name, shape, dtype=F32):
        t = self.nc.alloc_psum_tensor(name, list(shape), dtype)
        self.ps[name] = t
        return t

    def alloc_common(self, l0a=True):
        s = self.sbuf
        self.has_l0a = l0a
        s("xs", [128, KD, G if self.gw else TL], F32)
        s("xlnb", [128, KD, G], BF16)
        s("sq16", [128, G], BF16)
        s("xb16", [128, G], BF16)
        s("sc0", [128, G], F32)
        s("sc1", [128, G], F32)
        s("row0", [2, 1024], F32)
        if l0a:
            s("qf", [128, KD, G], BF16)
            s("kf", [128, KD, G], BF16)
            s("vtm", [128, NCH, 1024], BF16)
            s("kft", [128, KD, L], BF16)
            s("otm", [128, 1024], BF16)
        if l0a:
            s("S0", [128, 512], F32)
            s("S0b", [128, 512], BF16)
            s("ks", [128, 16], F32)
            s("ksb", [128, 16], BF16)
            s("rden", [128, 16], F32)
            s("denlog", [128, 256], F32)
            s("am", [128, L], BF16)
            s("am2", [128, L], BF16)
        s("row1", [1, 1024], F32)
        s("pv", [128, KD, NV], F32)
        s("quad", [2, 2], F32)
        s("qtmp", [2, 2], F32)
        s("scol", [128, 1], F32)
        s("rolec", [128, 1], F32)
        if l0a:
            s("omA", [128, 1024], BF16)
            s("omB", [128, 1024], BF16)
            s("bvrow", [128, 1024], BF16)
        if l0a:
            s("mI", [128, L], F32)
            s("idf", [128, 128], F32)
            s("hselT_f", [2, 128], F32)
        s("mS", [128, L], F32)
        s("idb", [128, 128], BF16)
        s("hsel_b", [128, 2], BF16)
        s("onecol_b", [128, 1], BF16)
        s("onerow_f", [1, 128], F32)
        s("onerow_b", [1, 128], BF16)
        s("epscol", [128, 1], F32)
        s("negone", [128, 1], F32)
        s("neghalf", [128, 1], F32)
        s("onecol_f", [128, 1], F32)
        if l0a:
            s("wqkv", [128, KD, 3072], BF16)
            if "wbig" not in self.sb:
                s("wbig", [128, KD, 1024], BF16)
        self.psum("P0", [128, 512])
        self.psum("P1", [128, 512])
        self.psum("P2", [128, 512])
        self.psum("P3", [128, 512])
        self.psum("P4", [128, 512])
        self.psum("P5", [128, 512])
        self.psum("P6", [128, 512])
        self.psum("PT", [128, 1024], BF16)

    def emit_setup(self, B, l0a=True):
        nc, sb, c = self.nc, self.sb, self.consts
        if l0a:
            B.dma("sy", [], ["mI"], lambda e: e.dma_start(out=sb["mI"][:], in_=c["m_incl"].ap()))
            B.dma("sy", [], ["idf"], lambda e: e.dma_start(out=sb["idf"][:], in_=c["ident_f"].ap()))
            B.dma("sy", [], ["hselT_f"], lambda e: e.dma_start(out=sb["hselT_f"][:], in_=c["hselT"].ap()))
        B.dma("sy", [], ["mS"], lambda e: e.dma_start(out=sb["mS"][:], in_=c["m_strict"].ap()))
        B.dma("gp", [], ["idb"], lambda e: e.dma_start(out=sb["idb"][:], in_=c["ident_b"].ap()))
        B.dma("gp", [], ["hsel_b"], lambda e: e.dma_start(out=sb["hsel_b"][:], in_=c["hsel"].ap()))
        B.dma("gp", [], ["onecol_b"], lambda e: e.dma_start(out=sb["onecol_b"][:], in_=c["ones_col"].ap()))
        B.dma("sy", [], ["onerow_f"], lambda e: e.dma_start(out=sb["onerow_f"][:], in_=c["ones_row"].ap()))
        B.dma("gp", [], ["onerow_b"], lambda e: e.dma_start(out=sb["onerow_b"][:], in_=c["ones_row"].ap()))
        B.do("ve", "dv", [], ["epscol"], lambda e: e.memset(sb["epscol"][:, :], 1e-5))
        B.do("ve", "dv", [], ["negone"], lambda e: e.memset(sb["negone"][:, :], -1.0))
        B.do("ve", "dv", [], ["neghalf"], lambda e: e.memset(sb["neghalf"][:, :], -0.5))
        B.do("ve", "dv", [], ["onecol_f"], lambda e: e.memset(sb["onecol_f"][:, :], 1.0))
        B.dma("sy", [], ["pv"],
             lambda e: e.dma_start(out=sb["pv"][:],
                                   in_=self.io["pvecT"].rearrange("(k p) v -> p k v", p=128)))
        B.dma("sy", [], ["quad"], lambda e: e.dma_start(out=sb["quad"][:], in_=self.io["quads"]))
        B.dma("sy", [], ["rolec"], lambda e: e.dma_start(out=sb["rolec"][:], in_=self.io["rolecol"]))
        B.dma("sy", [], ["scol"], lambda e: e.dma_start(out=sb["scol"][:], in_=self.io["scolh"]))
        if not l0a:
            return
        self.emit_build_omegas(B)
        self.pdm2_gate(B, "sy")
        B.dma("sy", [], ["wqkv"],
             lambda e: e.dma_start(out=sb["wqkv"][:], in_=self.io["scqkv"]))
        B.dma("sy", [], ["wbig"],
             lambda e: e.dma_start(out=sb["wbig"][:], in_=self.io["scwo"]))
        B.dma("sy", [], ["row1"],
             lambda e: e.dma_start(out=sb["row1"][0:1, :], in_=self.io["bqkv"][2:3, :]))
        for half, pn in ((0, "P4"), (1, "P5")):
            B.do("te", "pe", ["row1", "onerow_f"], ["p" + pn],
                 lambda e, half=half, pn=pn: e.matmul(
                     self.ps[pn][:, 0:512], sb["onerow_f"][0:1, :],
                     sb["row1"][0:1, half * 512:(half + 1) * 512], start=True, stop=True))
            B.do("ve", "dv", ["p" + pn], ["bvrow"],
                 lambda e, half=half, pn=pn: e.tensor_copy(
                     sb["bvrow"][:, half * 512:(half + 1) * 512], self.ps[pn][:, 0:512]))

    def emit_build_omegas(self, B):
        nc, sb, c = self.nc, self.sb, self.consts
        B.do("ve", "dv", ["quad"], ["qtmp"],
             lambda e: e.tensor_scalar(sb["qtmp"][:, 0:1], sb["quad"][:, 0:1], 0.0, 2.0, ALU.max, ALU.mult))
        B.do("se", "ac", ["qtmp"], ["qtmp"],
             lambda e: e.activation(sb["qtmp"][:, 0:1], sb["qtmp"][:, 0:1], AF.Sqrt))
        B.do("ve", "dv", ["quad"], ["qtmp"],
             lambda e: e.tensor_scalar(sb["qtmp"][:, 1:2], sb["quad"][:, 1:2], 0.0, None, ALU.max))
        B.do("se", "ac", ["qtmp"], ["qtmp"],
             lambda e: e.activation(sb["qtmp"][:, 1:2], sb["qtmp"][:, 1:2], AF.Sqrt))
        B.do("se", "ac", ["qtmp"], ["qtmp"],
             lambda e: e.activation(sb["qtmp"][:, 1:2], sb["qtmp"][:, 1:2], AF.Sqrt))
        B.do("ve", "dv", ["qtmp"], ["qtmp"],
             lambda e: e.tensor_scalar(sb["qtmp"][:, 1:2], sb["qtmp"][:, 1:2], INV_SQRT32, None, ALU.mult))
        B.dma("sy", [], ["sc0"],
             lambda e: e.dma_start(out=sb["sc0"][0:2, 0:512], in_=c["rselw"].ap()[:, 0:512]))
        B.dma("sy", [], ["sc1"],
             lambda e: e.dma_start(out=sb["sc1"][0:2, 0:512], in_=c["rselw"].ap()[:, 512:1024]))
        for nm, col in (("row0", 0), ("row1", 1)):
            for half in range(2):
                B.do("te", "pe", ["sc0", "sc1", "qtmp"], ["pP3"],
                     lambda e, col=col, half=half: e.matmul(
                         self.ps["P3"][0:1, 0:512],
                         sb["qtmp"][:, col:col + 1],
                         sb["sc0" if half == 0 else "sc1"][0:2, 0:512],
                         start=True, stop=True))
                B.do("ve", "dv", ["pP3"], [nm],
                     lambda e, nm=nm, half=half: e.tensor_copy(
                         sb[nm][0:1, half * 512:(half + 1) * 512], self.ps["P3"][0:1, 0:512]))
        for base in (0, 64):
            B.dma("sy", [], ["sc0"],
                 lambda e, base=base: e.dma_start(out=sb["sc0"][base:base + 64, :],
                                                  in_=self.io["omegaT"][:, 0:512]))
            B.dma("sy", [], ["sc1"],
                 lambda e, base=base: e.dma_start(out=sb["sc1"][base:base + 64, :],
                                                  in_=self.io["omegaT"][:, 512:1024]))
        for nm, rownm in (("omA", "row0"), ("omB", "row1")):
            for half, pn in ((0, "P4"), (1, "P5")):
                B.do("te", "pe", [rownm, "onerow_f"], ["p" + pn],
                     lambda e, half=half, pn=pn, rownm=rownm: e.matmul(
                         self.ps[pn][:, 0:512], sb["onerow_f"][0:1, :],
                         sb[rownm][0:1, half * 512:(half + 1) * 512], start=True, stop=True))
                B.do("ve", "dv", ["p" + pn, "sc0", "sc1"], [nm],
                     lambda e, half=half, pn=pn, nm=nm: e.tensor_mul(
                         sb[nm][:, half * 512:(half + 1) * 512],
                         sb["sc0" if half == 0 else "sc1"][:, 0:512],
                         self.ps[pn][:, 0:512]))
        self.tap(B, "scol", "scol", self.sb["scol"][:], [128, 1])
        self.tap(B, "omAv", "omA", self.sb["omA"][:], [128, 1024])
        self.tap(B, "omBv", "omB", self.sb["omB"][:], [128, 1024])
        self.tap(B, "carow", "row0", self.sb["row0"][0:1, :], [1, 1024])
        self.tap(B, "cbrow", "row1", self.sb["row1"][0:1, :], [1, 1024])

    def emit_load_stream(self, B, src_name):
        sb = self.sb
        B.dma("sy", [], ["xs"],
             lambda e: e.dma_start(out=sb["xs"][:],
                                   in_=self.io[src_name].rearrange("(k p) t -> p k t", p=128)))

    def emit_store_stream(self, B, dst_name):
        sb = self.sb
        B.dma("sy", ["xs"], [],
             lambda e: e.dma_start(out=self.io[dst_name].rearrange("(k p) t -> p k t", p=128),
                                   in_=sb["xs"][:]))

    def emit_ln(self, B, g0, gname="", dst="xlnb", eps_key="epscol", src=None,
                gcol=None, bcol=None):
        sb, ps = self.sb, self.ps
        xs = sb["xs"]
        if src is None:
            src_ap = lambda kd: xs[:, kd, g0:g0 + G]
            src_key = "xs"
        else:
            src_ap = lambda kd: sb[src][:, kd, :]
            src_key = src
        for kd in range(KD):
            if src is None:
                B.do("se", "ac", [src_key], ["xb16"],
                     lambda e, kd=kd: e.activation(sb["xb16"][:, :], src_ap(kd), AF.Copy))
                sum_src, sum_key = sb["xb16"], "xb16"
            else:
                sum_src, sum_key = None, src_key
            B.do("se", "ac", [src_key], ["sq16"],
                 lambda e, kd=kd: e.activation(sb["sq16"][:, :], src_ap(kd), AF.Square))
            B.do("te", "pe", [sum_key, "onecol_b"], ["pP2"],
                 lambda e, kd=kd: e.matmul(ps["P2"][0:1, 0:G], sb["onecol_b"][:, :],
                                           sum_src[:, :] if sum_src is not None else src_ap(kd),
                                           start=(kd == 0), stop=(kd == KD - 1)))
            B.do("te", "pe", ["sq16", "onecol_b"], ["pP3"],
                 lambda e, kd=kd: e.matmul(ps["P3"][0:1, 0:G], sb["onecol_b"][:, :], sb["sq16"][:, :],
                                           start=(kd == 0), stop=(kd == KD - 1)))
        r, r1 = sb["row0"], sb["row1"]
        B.do("ve", "dv", ["pP2"], ["row0"],
             lambda e: e.tensor_scalar(r[0:1, 0:G], ps["P2"][0:1, 0:G], 1.0 / D, None, ALU.mult))
        B.do("ve", "dv", ["pP3"], ["row0"],
             lambda e: e.tensor_scalar(r[0:1, G:2 * G], ps["P3"][0:1, 0:G], 1.0 / D, None, ALU.mult))
        B.do("se", "ac", ["row0"], ["row1"],
             lambda e: e.activation(r1[0:1, 0:G], r[0:1, 0:G], AF.Square))
        B.do("ve", "dv", ["row0", "row1"], ["row1"],
             lambda e: e.tensor_sub(r1[0:1, G:2 * G], r[0:1, G:2 * G], r1[0:1, 0:G]))
        B.do("se", "ac", ["row1", "epscol"], ["row1"],
             lambda e: e.activation(r1[0:1, G:2 * G], r1[0:1, G:2 * G], AF.Ln,
                                    bias=sb[eps_key][0:1, 0:1]))
        B.do("se", "ac", ["row1", "neghalf"], ["row1"],
             lambda e: e.activation(r1[0:1, G:2 * G], r1[0:1, G:2 * G], AF.Exp,
                                    scale=sb["neghalf"][0:1, 0:1]))
        B.do("ve", "dv", ["row1", "row0"], ["row0"],
             lambda e: e.tensor_mul(r[0:1, 0:G], r[0:1, 0:G], r1[0:1, G:2 * G]))
        B.do("te", "pe", ["row1", "onerow_f"], ["pP4"],
             lambda e: e.matmul(ps["P4"][:, 0:G], sb["onerow_f"][0:1, :], r1[0:1, G:2 * G],
                                start=True, stop=True))
        B.do("te", "pe", ["row0", "onerow_f"], ["pP5"],
             lambda e: e.matmul(ps["P5"][:, 0:G], sb["onerow_f"][0:1, :], r[0:1, 0:G],
                                start=True, stop=True))
        for kd in range(KD):
            B.do("ve", "dv", [src_key, "pP4"], ["sc0"],
                 lambda e, kd=kd: e.tensor_mul(sb["sc0"][:, :], src_ap(kd), ps["P4"][:, 0:G]))
            if gcol is None:
                B.do("ve", "dv", ["sc0", "pP5"], [dst],
                     lambda e, kd=kd: e.tensor_sub(sb[dst][:, kd, :], sb["sc0"][:, :], ps["P5"][:, 0:G]))
            else:
                B.do("ve", "dv", ["sc0", "pP5"], ["sc1"],
                     lambda e, kd=kd: e.tensor_sub(sb["sc1"][:, :], sb["sc0"][:, :], ps["P5"][:, 0:G]))
                B.do("ve", "dv", ["sc1", "pv"], [dst],
                     lambda e, kd=kd: e.tensor_scalar(
                         sb[dst][:, kd, :], sb["sc1"][:, :],
                         sb["pv"][:, kd, PV[gcol]:PV[gcol] + 1],
                         sb["pv"][:, kd, PV[bcol]:PV[bcol] + 1], ALU.mult, ALU.add))

    def emit_l0a(self, B, zero_state, mask_state_by_role, add_bias_cols=True,
                 src_name=None, dst_name=None):
        self._gw_io0 = (src_name, dst_name)
        """Full L0 attention block: xs <- xs + attn(LN(xs)). State in S0/ks.
        zero_state: memset states first. mask_state_by_role: multiply states by role col."""
        sb, ps = self.sb, self.ps
        if zero_state:
            B.do("ve", "dv", [], ["S0"], lambda e: e.memset(sb["S0"][:, :], 0.0))
            B.do("ve", "dv", [], ["S0b"], lambda e: e.memset(sb["S0b"][:, :], 0.0))
            B.do("ve", "dv", [], ["ks"], lambda e: e.memset(sb["ks"][:, :], 0.0))
            B.do("ve", "dv", [], ["ksb"], lambda e: e.memset(sb["ksb"][:, :], 0.0))
        if mask_state_by_role:
            B.do("ve", "dv", ["rolec", "S0"], ["S0"],
                 lambda e: e.tensor_scalar(sb["S0"][:, :], sb["S0"][:, :], sb["rolec"][:, 0:1], None, ALU.mult))
            B.do("ve", "dv", ["S0"], ["S0b"], lambda e: e.tensor_copy(sb["S0b"][:, :], sb["S0"][:, :]))
            B.do("ve", "dv", ["rolec", "ks"], ["ks"],
                 lambda e: e.tensor_scalar(sb["ks"][:, :], sb["ks"][:, :], sb["rolec"][:, 0:1], None, ALU.mult))
            B.do("se", "ac", ["ks"], ["ksb"], lambda e: e.copy(sb["ksb"][:, :], sb["ks"][:, :]))

        for g in range(NGRP):
            g0 = 0 if self.gw else g * G
            if self.gw:
                sn, dn = self._gw_io0
                B.dma("sy", [], ["xs"],
                      lambda e, g=g, sn=sn: e.dma_start(
                          out=sb["xs"][:],
                          in_=self.io[sn].rearrange("(k p) t -> p k t", p=128)[:, :, g * G:(g + 1) * G]))
            self.emit_ln(B, g0)
            self._l0a_group(B, g0, add_bias_cols)
            if self.gw and dn is not None:
                B.dma("sy", ["xs"], [],
                      lambda e, g=g, dn=dn: e.dma_start(
                          out=self.io[dn].rearrange("(k p) t -> p k t", p=128)[:, :, g * G:(g + 1) * G],
                          in_=sb["xs"][:]))

    def _l0a_group(self, B, g0, add_bias_cols):
        sb, ps = self.sb, self.ps

        for qk in range(2):
            dstf = "qf" if qk == 0 else "kf"
            for mt in range(KD):
                pn = "P0" if mt % 2 == 0 else "P1"
                wcol0 = qk * 1024 + mt * 128
                B.do("te", "pe", ["wqkv", "xlnb"], ["p" + pn],
                     lambda e, pn=pn, wcol0=wcol0: [
                         e.matmul(ps[pn][:, 0:G], sb["wqkv"][:, kd, wcol0:wcol0 + 128],
                                  sb["xlnb"][:, kd, :], start=(kd == 0), stop=(kd == KD - 1))
                         for kd in range(KD)][-1])
                bcol = PV["bq"] if qk == 0 else PV["bk"]
                B.do("ve", "dv", ["p" + pn, "pv"], ["sc0"],
                     lambda e, pn=pn, mt=mt, bcol=bcol: e.tensor_scalar(
                         sb["sc0"][:, :], ps[pn][:, 0:G],
                         sb["pv"][:, mt, bcol:bcol + 1], None, ALU.add))
                B.do("se", "ac", ["sc0"], ["sq16"],
                     lambda e: e.activation(sb["sq16"][:, :], sb["sc0"][:, :], AF.Square))
                B.do("te", "pe", ["sq16", "hsel_b"], ["pP2"],
                     lambda e: e.matmul(ps["P2"][0:2, 0:G], sb["hsel_b"][:, :], sb["sq16"][:, :],
                                        start=True, stop=True))
                B.do("ve", "dv", ["pP2"], ["row0"],
                     lambda e: e.tensor_scalar(sb["row0"][0:2, 0:G], ps["P2"][0:2, 0:G],
                                               1e-24, None, ALU.max))
                B.do("se", "ac", ["row0"], ["row0"],
                     lambda e: e.activation(sb["row0"][0:2, 0:G], sb["row0"][0:2, 0:G], AF.Ln))
                B.do("se", "ac", ["row0", "neghalf"], ["row0"],
                     lambda e: e.activation(sb["row0"][0:2, 0:G], sb["row0"][0:2, 0:G], AF.Exp,
                                            scale=sb["neghalf"][0:2, 0:1]))
                B.do("te", "pe", ["row0", "hselT_f"], ["pP3"],
                     lambda e: e.matmul(ps["P3"][:, 0:G], sb["hselT_f"][:, :], sb["row0"][0:2, 0:G],
                                        start=True, stop=True))
                B.do("ve", "dv", ["sc0", "pP3"], ["xb16"],
                     lambda e: e.tensor_mul(sb["xb16"][:, :], sb["sc0"][:, :], ps["P3"][:, 0:G]))
                for hh in range(2):
                    h = 2 * mt + hh
                    b0 = hh * 64
                    B.do("te", "pe", ["xb16", "omA"], ["pP4"],
                         lambda e, b0=b0, h=h: e.matmul(
                             ps["P4"][b0:b0 + 64, 0:G], sb["omA"][b0:b0 + 64, h * 64:(h + 1) * 64],
                             sb["xb16"][b0:b0 + 64, :], start=True, stop=True))
                    B.do("te", "pe", ["xb16", "omB"], ["pP5"],
                         lambda e, b0=b0, h=h: e.matmul(
                             ps["P5"][b0:b0 + 64, 0:G], sb["omB"][b0:b0 + 64, h * 64:(h + 1) * 64],
                             sb["xb16"][b0:b0 + 64, :], start=True, stop=True))
                B.do("ve", "dv", ["pP4", "scol"], ["sc1"],
                     lambda e: e.tensor_scalar(sb["sc1"][:, :], ps["P4"][:, 0:G],
                                               sb["scol"][:, 0:1], None, ALU.subtract))
                B.do("ve", "dv", ["sc1"], ["sc1"],
                     lambda e: e.tensor_scalar(sb["sc1"][:, :], sb["sc1"][:, :], -20.0, 20.0,
                                               ALU.max, ALU.min))
                B.do("se", "ac", ["sc1"], ["sc1"],
                     lambda e: e.activation(sb["sc1"][:, :], sb["sc1"][:, :], AF.Exp))
                B.do("se", "ac", ["pP5"], ["sq16"],
                     lambda e: e.activation(sb["sq16"][:, :], ps["P5"][:, 0:G], AF.Square))
                B.do("ve", "dv", ["sc1", "sq16"], [dstf],
                     lambda e, dstf=dstf, mt=mt: e.tensor_mul(sb[dstf][:, mt, :], sb["sc1"][:, :],
                                                              sb["sq16"][:, :]))
                if g0 == 0 and qk == 0 and mt == 0:
                    self.tap(B, "q0", "sc0", sb["sc0"][:], [128, G])
                    self.tap(B, "rn0", "row0", sb["row0"][0:2, 0:G], [2, G])
                    self.tap(B, "zn0", "xb16", sb["xb16"][:], [128, G])
                    self.tap(B, "e0", "sc1", sb["sc1"][:], [128, G])
                    self.tap(B, "sqB0", "sq16", sb["sq16"][:], [128, G])

        if g0 == 0:
            self.tap(B, "xlnb", "xlnb", sb["xlnb"][:], [128, KD, G])
            self.tap(B, "qf", "qf", sb["qf"][:], [128, KD, G])
            self.tap(B, "kf", "kf", sb["kf"][:], [128, KD, G])

        for tc in range(NCH):
            for nb in range(2):
                pn = "P0" if (tc * 2 + nb) % 2 == 0 else "P1"
                B.do("te", "pe", ["xlnb", "wqkv"], ["p" + pn],
                     lambda e, pn=pn, tc=tc, nb=nb: [
                         e.matmul(ps[pn][:, 0:512],
                                  sb["xlnb"][:, kd, tc * L:(tc + 1) * L],
                                  sb["wqkv"][:, kd, 2048 + nb * 512:2048 + (nb + 1) * 512],
                                  start=(kd == 0), stop=(kd == KD - 1))
                         for kd in range(KD)][-1])
                B.do("ve", "dv", ["p" + pn, "bvrow"], ["vtm"],
                     lambda e, pn=pn, tc=tc, nb=nb: e.tensor_add(
                         sb["vtm"][:, tc, nb * 512:(nb + 1) * 512],
                         ps[pn][:, 0:512], sb["bvrow"][:, nb * 512:(nb + 1) * 512]))

        if g0 == 0:
            self.tap(B, "vtm", "vtm", sb["vtm"][:], [128, NCH, 1024])

        for tc in range(NCH):
            t0 = tc * L
            for kd in range(KD):
                B.do("te", "pe", ["kf", "idb"], ["pPT"],
                     lambda e, kd=kd: e.transpose(ps["PT"][:, kd * 128:(kd + 1) * 128],
                                                  sb["kf"][:, kd, t0:t0 + L], sb["idb"][:, :]))
                B.do("ve", "dv", ["pPT"], ["kft"],
                     lambda e, kd=kd: e.tensor_copy(sb["kft"][:, kd, :],
                                                    ps["PT"][:, kd * 128:(kd + 1) * 128]))
            for h in range(NH):
                kd, b0 = h // 2, (h % 2) * 64
                pa = "P4" if h % 2 == 0 else "P5"
                an = "am" if h % 2 == 0 else "am2"
                B.do("te", "pe", ["kf", "qf"], ["p" + pa],
                     lambda e, pa=pa, kd=kd, b0=b0: e.matmul(
                         ps[pa][:, 0:L], sb["kf"][b0:b0 + 64, kd, t0:t0 + L],
                         sb["qf"][b0:b0 + 64, kd, t0:t0 + L], start=True, stop=True))
                B.do("ve", "dv", ["p" + pa, "mI"], [an],
                     lambda e, pa=pa, an=an: e.tensor_mul(sb[an][:, :], ps[pa][:, 0:L],
                                                          sb["mI"][:, :]))
                if g0 == 0 and tc == 0 and h == 0:
                    self.tap(B, "am0", "am", sb["am"][:], [128, L])
                po = "P0" if h < 8 else "P1"
                oc0 = (h % 8) * 64
                B.do("te", "pe", [an, "vtm"], ["p" + po],
                     lambda e, po=po, oc0=oc0, kd=kd, h=h, an=an: e.matmul(
                         ps[po][:, oc0:oc0 + 64], sb[an][:, :],
                         sb["vtm"][:, tc, h * 64:(h + 1) * 64], start=True, stop=False))
                B.do("te", "pe", ["qf", "S0b"], ["p" + po],
                     lambda e, po=po, oc0=oc0, kd=kd, b0=b0, h=h: e.matmul(
                         ps[po][:, oc0:oc0 + 64], sb["qf"][b0:b0 + 64, kd, t0:t0 + L],
                         sb["S0b"][b0:b0 + 64, (h // 2) * 64:(h // 2 + 1) * 64],
                         start=False, stop=True))
                B.do("te", "pe", [an, "onecol_b"], ["pP2"],
                     lambda e, h=h, an=an: e.matmul(ps["P2"][:, h:h + 1], sb[an][:, :],
                                                    sb["onecol_b"][:, :], start=True, stop=False))
                B.do("te", "pe", ["qf", "ksb"], ["pP2"],
                     lambda e, h=h, kd=kd, b0=b0: e.matmul(
                         ps["P2"][:, h:h + 1], sb["qf"][b0:b0 + 64, kd, t0:t0 + L],
                         sb["ksb"][b0:b0 + 64, h // 2:h // 2 + 1], start=False, stop=True))
                B.do("te", "pe", ["kft", "vtm"], ["pP6"],
                     lambda e, h=h, kd=kd, b0=b0: e.matmul(
                         ps["P6"][b0:b0 + 64, (h // 2) * 64:(h // 2 + 1) * 64],
                         sb["kft"][:, kd, b0:b0 + 64],
                         sb["vtm"][:, tc, h * 64:(h + 1) * 64], start=True, stop=True))
                B.do("te", "pe", ["kft", "onecol_b"], ["pP3"],
                     lambda e, h=h, kd=kd, b0=b0: e.matmul(
                         ps["P3"][b0:b0 + 64, h // 2:h // 2 + 1],
                         sb["kft"][:, kd, b0:b0 + 64],
                         sb["onecol_b"][:, :], start=True, stop=True))
            if g0 == 0 and tc == 0:
                B.do("ve", "dv", ["pP2"], ["sc0"],
                     lambda e: e.tensor_copy(sb["sc0"][:, 0:16], ps["P2"][:, 0:16]))
                self.tap(B, "den0", "sc0", sb["sc0"][:, 0:16], [128, 16])
            B.do("ve", "dv", ["pP2"], ["sc1"],
                 lambda e: e.tensor_scalar(sb["sc1"][:, 0:16], ps["P2"][:, 0:16], 1e-6, None, ALU.max))
            if g0 == 0 and tc == 0:
                self.tap(B, "denm0", "sc1", sb["sc1"][:, 0:16], [128, 16])
            B.do("se", "ac", ["sc1"], ["rden"],
                 lambda e: e.activation(sb["rden"][:, :], sb["sc1"][:, 0:16], AF.Ln))
            B.do("se", "ac", ["rden", "negone"], ["rden"],
                 lambda e: e.activation(sb["rden"][:, :], sb["rden"][:, :], AF.Exp,
                                        scale=sb["negone"][:, 0:1]))
            for h in range(NH):
                po = "P0" if h < 8 else "P1"
                oc0 = (h % 8) * 64
                B.do("ve", "dv", ["p" + po, "rden"], ["otm"],
                     lambda e, po=po, oc0=oc0, h=h: e.tensor_scalar(
                         sb["otm"][:, h * 64:(h + 1) * 64], ps[po][:, oc0:oc0 + 64],
                         sb["rden"][:, h:h + 1], None, ALU.mult))
            if g0 == 0 and tc == 0:
                B.do("ve", "dv", ["pP3"], ["sc0"],
                     lambda e: e.tensor_copy(sb["sc0"][:, 0:16], ps["P3"][:, 0:16]))
                self.tap(B, "p3c0", "sc0", sb["sc0"][:, 0:16], [128, 16])
            B.do("ve", "dv", ["pP6", "S0"], ["S0"],
                 lambda e: e.tensor_add(sb["S0"][:, :], sb["S0"][:, :], ps["P6"][:, 0:512]))
            B.do("ve", "dv", ["S0"], ["S0b"], lambda e: e.tensor_copy(sb["S0b"][:, :], sb["S0"][:, :]))
            B.do("ve", "dv", ["pP3", "ks"], ["ks"],
                 lambda e: e.tensor_add(sb["ks"][:, :], sb["ks"][:, :], ps["P3"][:, 0:16]))
            B.do("se", "ac", ["ks"], ["ksb"], lambda e: e.copy(sb["ksb"][:, :], sb["ks"][:, :]))
            if g0 == 0 and tc == 0:
                self.tap(B, "ks0post", "ks", sb["ks"][:], [128, 16])
                self.tap(B, "ksb0post", "ksb", sb["ksb"][:], [128, 16])
            if g0 == 0 and tc == 0:
                self.tap(B, "otm0", "otm", sb["otm"][:], [128, 1024])
                self.tap(B, "rden0", "rden", sb["rden"][:], [128, 16])
                self.tap(B, "S0c0", "S0", sb["S0"][:], [128, 512])
            if g0 == 0 and tc == 3:
                self.tap(B, "otm3", "otm", sb["otm"][:], [128, 1024])
                self.tap(B, "rden3", "rden", sb["rden"][:], [128, 16])
            if g0 == 0 and tc == 1:
                self.tap(B, "den1", "sc1", sb["sc1"][:, 0:16], [128, 16])
                self.tap(B, "otm1", "otm", sb["otm"][:], [128, 1024])
                self.tap(B, "ks1", "ks", sb["ks"][:], [128, 16])
                self.tap(B, "S0c1", "S0", sb["S0"][:], [128, 512])
            for kd in range(KD):
                B.do("te", "pe", ["otm", "idb"], ["pPT"],
                     lambda e, kd=kd: e.transpose(ps["PT"][:, kd * 128:(kd + 1) * 128],
                                                  sb["otm"][:, kd * 128:(kd + 1) * 128], sb["idb"][:, :]))
                B.do("ve", "dv", ["pPT"], ["xlnb"],
                     lambda e, kd=kd: e.tensor_copy(sb["xlnb"][:, kd, t0:t0 + L],
                                                    ps["PT"][:, kd * 128:(kd + 1) * 128]))

        if g0 == 0:
            self.tap(B, "afm0", "xlnb", sb["xlnb"][:], [128, KD, G])
        if g0 == (NGRP - 1) * G:
            self.tap(B, "denlog", "denlog", sb["denlog"][:], [128, 256])
        for mt in range(KD):
            pn = "P0" if mt % 2 == 0 else "P1"
            B.do("te", "pe", ["wbig", "xlnb"], ["p" + pn],
                 lambda e, pn=pn, mt=mt: [
                     e.matmul(ps[pn][:, 0:G], sb["wbig"][:, kd, mt * 128:(mt + 1) * 128],
                              sb["xlnb"][:, kd, :], start=(kd == 0), stop=(kd == KD - 1))
                     for kd in range(KD)][-1])
            B.do("ve", "dv", ["p" + pn, "pv", "xs"], ["xs"],
                 lambda e, pn=pn, mt=mt: e.scalar_tensor_tensor(
                     sb["xs"][:, mt, g0:g0 + G], ps[pn][:, 0:G],
                     sb["pv"][:, mt, PV["b_out"]:PV["b_out"] + 1],
                     sb["xs"][:, mt, g0:g0 + G], ALU.add, ALU.add))


    def alloc_predump(self, moe=True, l0f=True, l1a=False):
        nc = self.nc
        if l0f:
            self.io["scf1"] = nc.dram_tensor("scf1", [32, 128, KD, 128], BF16,
                                             kind="Internal").ap()
            self.io["scf2"] = nc.dram_tensor("scf2", [KD, 128, 32, 128], BF16,
                                             kind="Internal").ap()
        if moe:
            self.io["scw1"] = nc.dram_tensor("scw1", [NE, 64, 128, KD, 128], BF16,
                                             kind="Internal").ap()
            self.io["scw2"] = nc.dram_tensor("scw2", [NE, KD, 128, 32, 128], BF16,
                                             kind="Internal").ap()
        if l1a:
            for k in ("Wr", "Wk", "Wo", "Wr_ffn"):
                self.io["sc" + k] = nc.dram_tensor("sc" + k, [KD, 128, KD, 128],
                                                   BF16, kind="Internal").ap()
            for k in ("Wv", "Wg"):
                self.io["sc" + k] = nc.dram_tensor("sc" + k, [128, KD, 1024],
                                                   BF16, kind="Internal").ap()
            self.io["scWk_ffn"] = nc.dram_tensor("scWk_ffn", [28, 128, KD, 128],
                                                 BF16, kind="Internal").ap()
            self.io["scWv_ffn"] = nc.dram_tensor("scWv_ffn", [KD, 128, 28, 128],
                                                 BF16, kind="Internal").ap()
            self.io["sctm2"] = nc.dram_tensor("sctm2", [5, 32, 1024], BF16,
                                              kind="Internal").ap()
        self.pdm_sem = nc.alloc_semaphore("pdm")
        self.pdm_total = 0
        self.pdm_stage = {}
        self._pdm_waited = {}

    def alloc_predump_l0a(self):
        nc = self.nc
        self.io["scqkv"] = nc.dram_tensor("scqkv", [128, KD, 3 * D], BF16,
                                          kind="Internal").ap()
        self.io["scwo"] = nc.dram_tensor("scwo", [128, KD, D], BF16,
                                         kind="Internal").ap()
        self.pdm2_sem = nc.alloc_semaphore("pdm2")
        self.pdm2_total = 0
        self._pdm2_waited = False

    def emit_predump_l0a(self):
        nc, io = self.nc, self.io
        nc.gpsimd.dma_start(out=io["scqkv"], in_=io["wqkvT"]).then_inc(self.pdm2_sem, 16)
        nc.gpsimd.dma_start(out=io["scwo"], in_=io["woT"]).then_inc(self.pdm2_sem, 16)
        self.pdm2_total = 32

    def pdm2_gate(self, B, e):
        if getattr(self, "pdm2_total", 0) and not self._pdm2_waited:
            self._pdm2_waited = True
            B.eng(e).wait_ge(self.pdm2_sem, self.pdm2_total)

    def emit_predump(self, moe=True, l0f=True, l1a=False):
        nc, io = self.nc, self.io
        n = self.pdm_total // 16
        if l0f:
            for s in range(32):
                nc.gpsimd.dma_start(out=io["scf1"][s], in_=io["w1f0"][s]) \
                    .then_inc(self.pdm_sem, 16)
                n += 1
            for mt in range(KD):
                nc.gpsimd.dma_start(out=io["scf2"][mt], in_=io["w2f0"][mt]) \
                    .then_inc(self.pdm_sem, 16)
                n += 1
        if l1a:
            for k in ("Wr", "Wk", "Wo", "Wr_ffn"):
                for mt in range(KD):
                    nc.gpsimd.dma_start(out=io["sc" + k][mt], in_=io[k][mt]) \
                        .then_inc(self.pdm_sem, 16)
                    n += 1
            for k in ("Wv", "Wg"):
                nc.gpsimd.dma_start(out=io["sc" + k], in_=io[k]) \
                    .then_inc(self.pdm_sem, 16)
                n += 1
            for s in range(28):
                nc.gpsimd.dma_start(out=io["scWk_ffn"][s], in_=io["Wk_ffn"][s]) \
                    .then_inc(self.pdm_sem, 16)
                n += 1
            for mt in range(KD):
                nc.gpsimd.dma_start(out=io["scWv_ffn"][mt], in_=io["Wv_ffn"][mt]) \
                    .then_inc(self.pdm_sem, 16)
                n += 1
            for f in range(5):
                nc.gpsimd.dma_start(out=io["sctm2"][f], in_=io["tm_w2"][f]) \
                    .then_inc(self.pdm_sem, 16)
                n += 1
        if moe:
            for ex in range(NE):
                for s in range(64):
                    nc.gpsimd.dma_start(out=io["scw1"][ex, s], in_=io["m_w1T"][ex][s]) \
                        .then_inc(self.pdm_sem, 16)
                    n += 1
                for mt in range(KD):
                    nc.gpsimd.dma_start(out=io["scw2"][ex, mt], in_=io["m_w2T"][ex][mt]) \
                        .then_inc(self.pdm_sem, 16)
                    n += 1
        self.pdm_total = n * 16
        self.pdm_stage["moe" if moe else "base"] = self.pdm_total
        if not moe:
            self.pdm_stage.setdefault("base", self.pdm_total)

    def pdm_gate(self, B, e, stage="base"):
        tot = self.pdm_stage.get(stage, getattr(self, "pdm_total", 0))
        if tot and self._pdm_waited.get(e, -1) < tot:
            self._pdm_waited[e] = tot
            B.eng(e).wait_ge(self.pdm_sem, tot)

    W1RING = ("w1a", "w1b", "w1c", "w1d", "w1e", "w1f")

    def ensure_w1_ring(self):
        for nm in MK.W1RING[2:]:
            if nm not in self.sb:
                self.sbuf(nm, [128, KD, 128], BF16)

    def alloc_l0f(self):
        s = self.sbuf
        if "hbuf" not in self.sb:
            s("hbuf", [128, 32, G], BF16)
        s("w1a", [128, KD, 128], BF16)
        s("w1b", [128, KD, 128], BF16)
        s("w2a", [128, 32, 128], BF16)
        self.ensure_w1_ring()

    def emit_l0f(self, B, src_name=None, dst_name=None):
        sb, ps = self.sb, self.ps
        self.pdm_gate(B, "sy")
        for g in range(NGRP):
            g0 = 0 if self.gw else g * G
            if self.gw:
                B.dma("sy", [], ["xs"],
                      lambda e, g=g: e.dma_start(
                          out=sb["xs"][:],
                          in_=self.io[src_name].rearrange("(k p) t -> p k t", p=128)[:, :, g * G:(g + 1) * G]))
            self.emit_ln(B, g0)
            for j in range(32):
                wt = MK.W1RING[j % 6]
                B.dma("sy", [], [wt],
                      lambda e, wt=wt, j=j: e.dma_start(
                          out=sb[wt][:], in_=self.io["scf1"][j]))
                pn = ("P0", "P1", "P2", "P3")[j % 4]
                B.do("te", "pe", [wt, "xlnb"], ["p" + pn],
                     lambda e, pn=pn, wt=wt: [
                         e.matmul(ps[pn][:, 0:G], sb[wt][:, kd, :], sb["xlnb"][:, kd, :],
                                  start=(kd == 0), stop=(kd == KD - 1))
                         for kd in range(KD)][-1])
                B.do("se", "ac", ["p" + pn], ["hbuf"],
                     lambda e, pn=pn, j=j: e.activation(sb["hbuf"][:, j, :], ps[pn][:, 0:G], AF.Relu))
            wv2f = self.wbig_v(32, 128) if "wbig" in self.sb else None
            for mt in range(KD):
                use_big = (mt % 2 == 1) and wv2f is not None
                wk = "wbig" if use_big else "w2a"
                wap = wv2f if use_big else sb["w2a"][:]
                B.dma("sy", [], [wk],
                      lambda e, mt=mt, wap=wap: e.dma_start(
                          out=wap, in_=self.io["scf2"][mt]))
                pn = ("P0", "P1", "P2", "P3")[mt % 4]
                B.do("te", "pe", [wk, "hbuf"], ["p" + pn],
                     lambda e, pn=pn, wap=wap: [
                         e.matmul(ps[pn][:, 0:G], wap[:, k, :], sb["hbuf"][:, k, :],
                                  start=(k == 0), stop=(k == 31))
                         for k in range(32)][-1])
                B.do("ve", "dv", ["p" + pn, "xs"], ["xs"],
                     lambda e, pn=pn, mt=mt: e.tensor_add(
                         sb["xs"][:, mt, g0:g0 + G], sb["xs"][:, mt, g0:g0 + G], ps[pn][:, 0:G]))
            if self.gw:
                B.dma("sy", ["xs"], [],
                      lambda e, g=g: e.dma_start(
                          out=self.io[dst_name].rearrange("(k p) t -> p k t", p=128)[:, :, g * G:(g + 1) * G],
                          in_=sb["xs"][:]))

    def alloc_l1a(self):
        s = self.sbuf
        need = lambda n: n not in self.sb
        if need("hbuf"):
            s("hbuf", [128, 32, G], BF16)
        if need("vtm"):
            s("vtm", [128, NCH, 1024], BF16)
        if need("kft"):
            s("kft", [128, KD, L], BF16)
        if need("otm"):
            s("otm", [128, 1024], BF16)
        if need("am"):
            s("am", [128, L], BF16)
        if need("am2"):
            s("am2", [128, L], BF16)
        if need("qf"):
            s("qf", [128, KD, G], BF16)
        if need("kf"):
            s("kf", [128, KD, G], BF16)
        s("xbb", [128, KD, G], BF16)
        s("xab", [128, KD, G], BF16)
        s("dxb", [128, KD, G], BF16)
        s("mix", [128, KD, G], BF16)
        if need("gtm"):
            s("gtm", [128, NCH, 1024], BF16)
        s("loraf", [32, 5, G], BF16)
        s("t64", [64, G], BF16)
        s("S1", [128, 512], F32)
        s("S1b", [128, 512], BF16)
        s("dvc", [128, 16], BF16)
        s("dsc", [128, KD, NCH], F32)
        s("gnm", [128, 16], F32)
        s("gne", [128, 16], F32)
        s("grs", [128, 16], F32)
        s("xaprev", [128, KD], F32)
        s("xfprev", [128, KD], F32)
        s("scr", [128, L], F32)
        s("scr2", [128, L], F32)
        s("zcol", [128, 1], F32)
        s("tmw1s", [128, KD, 160], BF16)
        s("tmw2s", [32, 1024], BF16)
        s("tdw1s", [128, KD, 64], BF16)
        s("tdw2s", [64, 1024], BF16)
        if need("lnxg_r"):
            s("lnxg_r", [128, 1024], BF16)
            s("lnxb_r", [128, 1024], BF16)
        if need("wbig"):
            s("wbig", [128, KD, 1024], BF16)

    def wbig_v(self, a, b):
        return self.sb["wbig"][:].rearrange("p x y -> p (x y)")[:, 0:a * b].rearrange(
            "p (a b) -> p a b", a=a)

    def emit_l1a_setup(self, B):
        sb = self.sb
        B.do("ve", "dv", [], ["zcol"], lambda e: e.memset(sb["zcol"][:, :], 0.0))
        B.dma("gp", [], ["tmw1s"],
              lambda e: e.dma_start(out=sb["tmw1s"][:],
                                    in_=self.io["tm_w1"].rearrange("(k p) j -> p k j", p=128)))
        B.dma("gp", [], ["tdw1s"],
              lambda e: e.dma_start(out=sb["tdw1s"][:],
                                    in_=self.io["td_w1"].rearrange("(k p) j -> p k j", p=128)))
        B.dma("gp", [], ["tdw2s"],
              lambda e: e.dma_start(out=sb["tdw2s"][:], in_=self.io["td_w2"]))


    def emit_l1a(self, B, zero_state, mask_by_role, src_name=None, dst_name=None,
                 reduced=False):
        sb, ps = self.sb, self.ps
        self._gw_io = (src_name, dst_name)
        self.pdm_gate(B, "sy")
        B.dma("gp", [], ["lnxg_r"], lambda e: e.dma_start(out=sb["lnxg_r"][:], in_=self.io["lnxgrow"]))
        B.dma("gp", [], ["lnxb_r"], lambda e: e.dma_start(out=sb["lnxb_r"][:], in_=self.io["lnxbrow"]))
        EQD, EKG, KRAW, RUK = 0, 8, 16, 24
        if zero_state:
            B.do("ve", "dv", [], ["S1"], lambda e: e.memset(sb["S1"][:, :], 0.0))
            B.do("ve", "dv", [], ["S1b"], lambda e: e.memset(sb["S1b"][:, :], 0.0))
            B.do("ve", "dv", [], ["xaprev"], lambda e: e.memset(sb["xaprev"][:, :], 0.0))
            B.do("ve", "dv", [], ["xfprev"], lambda e: e.memset(sb["xfprev"][:, :], 0.0))
        if mask_by_role:
            B.do("ve", "dv", ["rolec", "S1"], ["S1"],
                 lambda e: e.tensor_scalar(sb["S1"][:, :], sb["S1"][:, :], sb["rolec"][:, 0:1], None, ALU.mult))
            B.do("ve", "dv", ["S1"], ["S1b"], lambda e: e.tensor_copy(sb["S1b"][:, :], sb["S1"][:, :]))
            B.do("ve", "dv", ["rolec", "xaprev"], ["xaprev"],
                 lambda e: e.tensor_scalar(sb["xaprev"][:, :], sb["xaprev"][:, :], sb["rolec"][:, 0:1], None, ALU.mult))
            B.do("ve", "dv", ["rolec", "xfprev"], ["xfprev"],
                 lambda e: e.tensor_scalar(sb["xfprev"][:, :], sb["xfprev"][:, :], sb["rolec"][:, 0:1], None, ALU.mult))
        for g in range(NGRP):
            self._l1a_group(B, g * G, EQD, EKG, KRAW, RUK, reduced=reduced)

    def _l1a_group(self, B, g0, EQD, EKG, KRAW, RUK, reduced=False):
        sb, ps = self.sb, self.ps
        io = self.io
        dst_name = None
        if self.gw:
            src_name, dst_name = self._gw_io
            gr = g0
            g0 = 0
            B.dma("sy", [], ["xs"],
                  lambda e: e.dma_start(
                      out=sb["xs"][:],
                      in_=io[src_name].rearrange("(k p) t -> p k t", p=128)[:, :, gr:gr + G]))

        self.emit_ln(B, g0)
        self.emit_ln(B, g0, src="xlnb", dst="xbb", gcol="ln0_g", bcol="ln0_b")
        self.emit_ln(B, g0, src="xbb", dst="xab", gcol="ln1_g", bcol="ln1_b")

        for kd in range(KD):
            B.do("ve", "dv", ["xab", "xaprev"], ["dxb"],
                 lambda e, kd=kd: e.tensor_sub(sb["dxb"][:, kd, 1:G], sb["xab"][:, kd, 0:G - 1],
                                               sb["xab"][:, kd, 1:G]))
            B.do("ve", "dv", ["xab", "xaprev"], ["dxb"],
                 lambda e, kd=kd: e.tensor_sub(sb["dxb"][:, kd, 0:1], sb["xaprev"][:, kd:kd + 1],
                                               sb["xab"][:, kd, 0:1]))
        for kd in range(KD):
            B.do("ve", "dv", ["xab"], ["xaprev"],
                 lambda e, kd=kd: e.tensor_copy(sb["xaprev"][:, kd:kd + 1], sb["xab"][:, kd, G - 1:G]))

        for kd in range(KD):
            B.do("ve", "dv", ["dxb", "pv", "xab"], ["mix"],
                 lambda e, kd=kd: e.scalar_tensor_tensor(
                     sb["mix"][:, kd, :], sb["dxb"][:, kd, :],
                     sb["pv"][:, kd, PV["time_maa_x"]:PV["time_maa_x"] + 1],
                     sb["xab"][:, kd, :], ALU.mult, ALU.add))
        for f in range(5):
            B.do("te", "pe", ["tmw1s", "mix"], ["pP2"],
                 lambda e, f=f: [
                     e.matmul(ps["P2"][0:32, 0:G], sb["tmw1s"][:, kd, f * 32:(f + 1) * 32],
                              sb["mix"][:, kd, :], start=(kd == 0), stop=(kd == KD - 1))
                     for kd in range(KD)][-1])
            B.do("se", "ac", ["pP2"], ["loraf"],
                 lambda e, f=f: e.activation(sb["loraf"][0:32, f, :], ps["P2"][0:32, 0:G], AF.Tanh))

        FNAMES = ["w", "k", "v", "r", "g"]
        for fi, fn in enumerate(FNAMES):
            B.dma("sy", [], ["tmw2s"],
                  lambda e, fi=fi: e.dma_start(out=sb["tmw2s"][:], in_=io["sctm2"][fi]))
            for kd in range(KD):
                pn = ("P0", "P1", "P2", "P3")[kd % 4]
                B.do("te", "pe", ["tmw2s", "loraf"], ["p" + pn],
                     lambda e, pn=pn, kd=kd, fi=fi: e.matmul(
                         ps[pn][:, 0:G], sb["tmw2s"][0:32, kd * 128:(kd + 1) * 128],
                         sb["loraf"][0:32, fi, :], start=True, stop=True))
                cn = "time_maa_" + fn
                B.do("ve", "dv", ["p" + pn, "pv", "dxb"], ["sc0"],
                     lambda e, pn=pn, kd=kd, cn=cn: e.scalar_tensor_tensor(
                         sb["sc0"][:, :], ps[pn][:, 0:G],
                         sb["pv"][:, kd, PV[cn]:PV[cn] + 1],
                         sb["dxb"][:, kd, :], ALU.add, ALU.mult))
                B.do("ve", "dv", ["sc0", "xab"], ["mix"],
                     lambda e, kd=kd: e.tensor_add(sb["mix"][:, kd, :], sb["sc0"][:, :],
                                                   sb["xab"][:, kd, :]))
            if fn == "w":
                B.do("te", "pe", ["tdw1s", "mix"], ["pP2"],
                     lambda e: [
                         e.matmul(ps["P2"][0:64, 0:G], sb["tdw1s"][:, kd, :], sb["mix"][:, kd, :],
                                  start=(kd == 0), stop=(kd == KD - 1))
                         for kd in range(KD)][-1])
                B.do("se", "ac", ["pP2"], ["t64"],
                     lambda e: e.activation(sb["t64"][0:64, :], ps["P2"][0:64, 0:G], AF.Tanh))
                for kd in range(KD):
                    pn = "P0" if kd % 2 == 0 else "P1"
                    B.do("te", "pe", ["tdw2s", "t64"], ["p" + pn],
                         lambda e, pn=pn, kd=kd: e.matmul(
                             ps[pn][:, 0:G], sb["tdw2s"][:, kd * 128:(kd + 1) * 128],
                             sb["t64"][0:64, :], start=True, stop=True))
                    B.do("ve", "dv", ["p" + pn, "pv"], ["sc0"],
                         lambda e, pn=pn, kd=kd: e.tensor_scalar(
                             sb["sc0"][:, :], ps[pn][:, 0:G],
                             sb["pv"][:, kd, PV["time_decay"]:PV["time_decay"] + 1], None, ALU.add))
                    B.do("se", "ac", ["sc0"], ["sc1"],
                         lambda e: e.activation(sb["sc1"][:, :], sb["sc0"][:, :], AF.Exp))
                    B.do("ve", "dv", ["sc1"], ["sc0"],
                         lambda e: e.tensor_tensor_scan(sb["sc0"][:, :], sb["sc1"][:, :],
                                                        sb["sc1"][:, :], 0.0, ALU.add, ALU.bypass))
                    for tc in range(NCH):
                        t0 = tc * L
                        basec = sb["zcol"][:, 0:1] if tc == 0 else sb["sc0"][:, t0 - 1:t0]
                        B.do("ve", "dv", ["sc0", "zcol"], ["scr"],
                             lambda e, t0=t0, basec=basec: e.tensor_scalar(
                                 sb["scr"][:, :], sb["sc0"][:, t0:t0 + L], basec, None, ALU.subtract))
                        B.do("se", "ac", ["scr"], ["hbuf"],
                             lambda e, kd=kd, tc=tc: e.activation(
                                 sb["hbuf"][:, EKG + kd, tc * L:(tc + 1) * L], sb["scr"][:, :], AF.Exp))
                        B.do("ve", "dv", ["scr", "sc1"], ["scr2"],
                             lambda e, t0=t0: e.tensor_sub(sb["scr2"][:, :], sb["scr"][:, :],
                                                           sb["sc1"][:, t0:t0 + L]))
                        B.do("se", "ac", ["scr2", "negone"], ["hbuf"],
                             lambda e, kd=kd, tc=tc: e.activation(
                                 sb["hbuf"][:, EQD + kd, tc * L:(tc + 1) * L], sb["scr2"][:, :],
                                 AF.Exp, scale=sb["negone"][:, 0:1]))
                        B.do("ve", "dv", ["sc0", "zcol"], ["dsc"],
                             lambda e, t0=t0, basec=basec, kd=kd, tc=tc: e.tensor_scalar(
                                 sb["dsc"][:, kd, tc:tc + 1], sb["sc0"][:, t0 + L - 1:t0 + L],
                                 basec, None, ALU.subtract))
                        B.do("se", "ac", ["dsc", "negone"], ["dsc"],
                             lambda e, kd=kd, tc=tc: e.activation(
                                 sb["dsc"][:, kd, tc:tc + 1], sb["dsc"][:, kd, tc:tc + 1],
                                 AF.Exp, scale=sb["negone"][:, 0:1]))
            elif fn == "k":
                for mt in range(KD):
                    wt = "w1a" if mt % 2 == 0 else "w1b"
                    B.dma("sy", [], [wt],
                          lambda e, mt=mt, wt=wt: e.dma_start(
                              out=sb[wt][:], in_=io["scWk"][mt]))
                    pn = ("P0", "P1", "P2", "P3")[mt % 4]
                    B.do("te", "pe", [wt, "mix"], ["p" + pn],
                         lambda e, pn=pn, wt=wt: [
                             e.matmul(ps[pn][:, 0:G], sb[wt][:, kd, :],
                                      sb["mix"][:, kd, :], start=(kd == 0), stop=(kd == KD - 1))
                             for kd in range(KD)][-1])
                    B.do("se", "ac", ["p" + pn], ["hbuf"],
                         lambda e, pn=pn, mt=mt: e.copy(sb["hbuf"][:, KRAW + mt, :], ps[pn][:, 0:G]))
                    B.do("ve", "dv", ["p" + pn, "hbuf"], ["kf"],
                         lambda e, pn=pn, mt=mt: e.tensor_mul(
                             sb["kf"][:, mt, :], ps[pn][:, 0:G], sb["hbuf"][:, EKG + mt, :]))
            elif fn == "v":
                B.dma("sy", [], ["wbig"],
                      lambda e: e.dma_start(out=sb["wbig"][:], in_=io["scWv"]))
                for tc in range(NCH):
                    for nb in range(2):
                        pn = "P0" if (tc * 2 + nb) % 2 == 0 else "P1"
                        B.do("te", "pe", ["wbig", "mix"], ["p" + pn],
                             lambda e, pn=pn, tc=tc, nb=nb: [
                                 e.matmul(ps[pn][:, 0:512], sb["mix"][:, kd, tc * L:(tc + 1) * L],
                                          sb["wbig"][:, kd, nb * 512:(nb + 1) * 512],
                                          start=(kd == 0), stop=(kd == KD - 1))
                                 for kd in range(KD)][-1])
                        B.do("ve", "dv", ["p" + pn], ["vtm"],
                             lambda e, pn=pn, tc=tc, nb=nb: e.tensor_copy(
                                 sb["vtm"][:, tc, nb * 512:(nb + 1) * 512], ps[pn][:, 0:512]))
            elif fn == "r":
                for mt in range(KD):
                    wt = "w1a" if mt % 2 == 0 else "w1b"
                    B.dma("sy", [], [wt],
                          lambda e, mt=mt, wt=wt: e.dma_start(
                              out=sb[wt][:], in_=io["scWr"][mt]))
                    pn = ("P0", "P1", "P2", "P3")[mt % 4]
                    B.do("te", "pe", [wt, "mix"], ["p" + pn],
                         lambda e, pn=pn, wt=wt: [
                             e.matmul(ps[pn][:, 0:G], sb[wt][:, kd, :],
                                      sb["mix"][:, kd, :], start=(kd == 0), stop=(kd == KD - 1))
                             for kd in range(KD)][-1])
                    B.do("ve", "dv", ["p" + pn, "hbuf"], ["qf"],
                         lambda e, pn=pn, mt=mt: e.tensor_mul(
                             sb["qf"][:, mt, :], ps[pn][:, 0:G], sb["hbuf"][:, EQD + mt, :]))
                    B.do("ve", "dv", ["p" + pn, "pv"], ["sc0"],
                         lambda e, pn=pn, mt=mt: e.tensor_scalar(
                             sb["sc0"][:, :], ps[pn][:, 0:G],
                             sb["pv"][:, mt, PV["u_flat"]:PV["u_flat"] + 1], None, ALU.mult))
                    B.do("ve", "dv", ["sc0", "hbuf"], ["hbuf"],
                         lambda e, mt=mt: e.tensor_mul(
                             sb["hbuf"][:, RUK + mt, :], sb["sc0"][:, :], sb["hbuf"][:, KRAW + mt, :]))
            elif fn == "g":
                B.dma("sy", [], ["wbig"],
                      lambda e: e.dma_start(out=sb["wbig"][:], in_=io["scWg"]))
                for tc in range(NCH):
                    for nb in range(2):
                        pn = "P0" if (tc * 2 + nb) % 2 == 0 else "P1"
                        B.do("te", "pe", ["wbig", "mix"], ["p" + pn],
                             lambda e, pn=pn, tc=tc, nb=nb: [
                                 e.matmul(ps[pn][:, 0:512], sb["mix"][:, kd, tc * L:(tc + 1) * L],
                                          sb["wbig"][:, kd, nb * 512:(nb + 1) * 512],
                                          start=(kd == 0), stop=(kd == KD - 1))
                                 for kd in range(KD)][-1])
                        B.do("se", "ac", ["p" + pn], ["gtm"],
                             lambda e, pn=pn, tc=tc, nb=nb: e.activation(
                                 sb["gtm"][:, tc, nb * 512:(nb + 1) * 512], ps[pn][:, 0:512], AF.Silu))

        for tc in range(NCH):
            t0 = tc * L
            out = (not reduced) or tc == NCH - 1
            for kd in range(KD):
                B.do("te", "pe", ["kf", "idb"], ["pPT"],
                     lambda e, kd=kd: e.transpose(ps["PT"][:, kd * 128:(kd + 1) * 128],
                                                  sb["kf"][:, kd, t0:t0 + L], sb["idb"][:, :]))
                B.do("ve", "dv", ["pPT"], ["kft"],
                     lambda e, kd=kd: e.tensor_copy(sb["kft"][:, kd, :],
                                                    ps["PT"][:, kd * 128:(kd + 1) * 128]))
            if out:
                for kd in range(KD):
                    B.do("te", "pe", ["hbuf", "hsel_b"], ["pP2"],
                         lambda e, kd=kd: e.matmul(
                             ps["P2"][:, 2 * kd:2 * kd + 2], sb["hbuf"][:, RUK + kd, t0:t0 + L],
                             sb["hsel_b"][:, :], start=True, stop=True))
                B.do("se", "ac", ["pP2"], ["dvc"],
                     lambda e: e.copy(sb["dvc"][:, :], ps["P2"][:, 0:16]))
            for h in range(NH):
                kd, b0 = h // 2, (h % 2) * 64
                pa = "P4" if h % 2 == 0 else "P5"
                an = "am" if h % 2 == 0 else "am2"
                po = "P0" if h < 8 else "P1"
                oc0 = (h % 8) * 64
                if out:
                    B.do("te", "pe", ["kf", "qf"], ["p" + pa],
                         lambda e, pa=pa, kd=kd, b0=b0: e.matmul(
                             ps[pa][:, 0:L], sb["kf"][b0:b0 + 64, kd, t0:t0 + L],
                             sb["qf"][b0:b0 + 64, kd, t0:t0 + L], start=True, stop=True))
                    B.do("ve", "dv", ["p" + pa, "mS"], [an],
                         lambda e, pa=pa, an=an: e.tensor_mul(sb[an][:, :], ps[pa][:, 0:L],
                                                              sb["mS"][:, :]))
                    B.do("te", "pe", [an, "vtm"], ["p" + po],
                         lambda e, po=po, oc0=oc0, h=h, an=an: e.matmul(
                             ps[po][:, oc0:oc0 + 64], sb[an][:, :],
                             sb["vtm"][:, tc, h * 64:(h + 1) * 64], start=True, stop=False))
                    B.do("te", "pe", ["qf", "S1b"], ["p" + po],
                         lambda e, po=po, oc0=oc0, kd=kd, b0=b0, h=h: e.matmul(
                             ps[po][:, oc0:oc0 + 64], sb["qf"][b0:b0 + 64, kd, t0:t0 + L],
                             sb["S1b"][b0:b0 + 64, (h // 2) * 64:(h // 2 + 1) * 64],
                             start=False, stop=True))
                B.do("te", "pe", ["kft", "vtm"], ["pP6"],
                     lambda e, h=h, kd=kd, b0=b0: e.matmul(
                         ps["P6"][b0:b0 + 64, (h // 2) * 64:(h // 2 + 1) * 64],
                         sb["kft"][:, kd, b0:b0 + 64],
                         sb["vtm"][:, tc, h * 64:(h + 1) * 64], start=True, stop=True))
            if out:
                for h in range(NH):
                    po = "P0" if h < 8 else "P1"
                    oc0 = (h % 8) * 64
                    B.do("ve", "dv", ["p" + po, "dvc", "vtm"], ["otm"],
                         lambda e, po=po, oc0=oc0, h=h: e.scalar_tensor_tensor(
                             sb["otm"][:, h * 64:(h + 1) * 64], sb["vtm"][:, tc, h * 64:(h + 1) * 64],
                             sb["dvc"][:, h:h + 1], ps[po][:, oc0:oc0 + 64], ALU.mult, ALU.add))
            B.do("ve", "dv", ["pP6", "S1"], ["S1"],
                 lambda e: e.tensor_add(sb["S1"][:, :], sb["S1"][:, :], ps["P6"][:, 0:512]))
            for h in range(NH):
                kd, b0 = h // 2, (h % 2) * 64
                B.do("ve", "dv", ["dsc", "S1"], ["S1"],
                     lambda e, kd=kd, b0=b0, h=h: e.tensor_scalar(
                         sb["S1"][b0:b0 + 64, (h // 2) * 64:(h // 2 + 1) * 64],
                         sb["S1"][b0:b0 + 64, (h // 2) * 64:(h // 2 + 1) * 64],
                         sb["dsc"][b0:b0 + 64, kd, tc:tc + 1], None, ALU.mult))
            B.do("ve", "dv", ["S1"], ["S1b"], lambda e: e.tensor_copy(sb["S1b"][:, :], sb["S1"][:, :]))
            if not out:
                continue
            for h in range(NH):
                B.do("se", "ac", ["otm"], ["sq16"],
                     lambda e, h=h: e.activation(sb["sq16"][:, 0:64], sb["otm"][:, h * 64:(h + 1) * 64],
                                                 AF.Square))
                B.do("ve", "dv", ["otm"], ["gnm"],
                     lambda e, h=h: e.reduce_sum(sb["gnm"][:, h:h + 1], sb["otm"][:, h * 64:(h + 1) * 64],
                                                 axis=mybir.AxisListType.X))
                B.do("ve", "dv", ["sq16"], ["gne"],
                     lambda e, h=h: e.reduce_sum(sb["gne"][:, h:h + 1], sb["sq16"][:, 0:64],
                                                 axis=mybir.AxisListType.X))
            B.do("ve", "dv", ["gnm"], ["gnm"],
                 lambda e: e.tensor_scalar(sb["gnm"][:, :], sb["gnm"][:, :], 1.0 / 64, None, ALU.mult))
            B.do("ve", "dv", ["gne"], ["gne"],
                 lambda e: e.tensor_scalar(sb["gne"][:, :], sb["gne"][:, :], 1.0 / 64, None, ALU.mult))
            B.do("se", "ac", ["gnm"], ["grs"],
                 lambda e: e.activation(sb["grs"][:, :], sb["gnm"][:, :], AF.Square))
            B.do("ve", "dv", ["gne", "grs"], ["grs"],
                 lambda e: e.tensor_sub(sb["grs"][:, :], sb["gne"][:, :], sb["grs"][:, :]))
            B.do("se", "ac", ["grs", "epscol"], ["grs"],
                 lambda e: e.activation(sb["grs"][:, :], sb["grs"][:, :], AF.Ln,
                                        bias=sb["epscol"][:, 0:1]))
            B.do("se", "ac", ["grs", "neghalf"], ["grs"],
                 lambda e: e.activation(sb["grs"][:, :], sb["grs"][:, :], AF.Exp,
                                        scale=sb["neghalf"][:, 0:1]))
            for h in range(NH):
                B.do("ve", "dv", ["gnm", "grs", "otm"], ["otm"],
                     lambda e, h=h: e.tensor_scalar(
                         sb["otm"][:, h * 64:(h + 1) * 64], sb["otm"][:, h * 64:(h + 1) * 64],
                         sb["gnm"][:, h:h + 1], sb["grs"][:, h:h + 1], ALU.subtract, ALU.mult))
            B.do("ve", "dv", ["otm", "lnxg_r"], ["sc0"],
                 lambda e: e.tensor_mul(sb["sc0"][:, 0:512], sb["otm"][:, 0:512], sb["lnxg_r"][:, 0:512]))
            B.do("ve", "dv", ["otm", "lnxg_r"], ["sc1"],
                 lambda e: e.tensor_mul(sb["sc1"][:, 0:512], sb["otm"][:, 512:1024], sb["lnxg_r"][:, 512:1024]))
            B.do("ve", "dv", ["sc0", "lnxb_r"], ["sc0"],
                 lambda e: e.tensor_add(sb["sc0"][:, 0:512], sb["sc0"][:, 0:512], sb["lnxb_r"][:, 0:512]))
            B.do("ve", "dv", ["sc1", "lnxb_r"], ["sc1"],
                 lambda e: e.tensor_add(sb["sc1"][:, 0:512], sb["sc1"][:, 0:512], sb["lnxb_r"][:, 512:1024]))
            B.do("ve", "dv", ["sc0", "gtm"], ["otm"],
                 lambda e: e.tensor_mul(sb["otm"][:, 0:512], sb["sc0"][:, 0:512], sb["gtm"][:, tc, 0:512]))
            B.do("ve", "dv", ["sc1", "gtm"], ["otm"],
                 lambda e: e.tensor_mul(sb["otm"][:, 512:1024], sb["sc1"][:, 0:512], sb["gtm"][:, tc, 512:1024]))
            for kd in range(KD):
                B.do("te", "pe", ["otm", "idb"], ["pPT"],
                     lambda e, kd=kd: e.transpose(ps["PT"][:, kd * 128:(kd + 1) * 128],
                                                  sb["otm"][:, kd * 128:(kd + 1) * 128], sb["idb"][:, :]))
                B.do("ve", "dv", ["pPT"], ["dxb"],
                     lambda e, kd=kd: e.tensor_copy(sb["dxb"][:, kd, t0:t0 + L],
                                                    ps["PT"][:, kd * 128:(kd + 1) * 128]))

        c0, cw = ((NCH - 1) * L, L) if reduced else (0, G)
        for mt in range(KD):
            wt = "w1a" if mt % 2 == 0 else "w1b"
            B.dma("sy", [], [wt],
                  lambda e, mt=mt, wt=wt: e.dma_start(out=sb[wt][:], in_=io["scWo"][mt]))
            pn = "P0" if mt % 2 == 0 else "P1"
            B.do("te", "pe", [wt, "dxb"], ["p" + pn],
                 lambda e, pn=pn, wt=wt: [
                     e.matmul(ps[pn][:, 0:cw], sb[wt][:, kd, :],
                              sb["dxb"][:, kd, c0:c0 + cw], start=(kd == 0), stop=(kd == KD - 1))
                     for kd in range(KD)][-1])
            B.do("ve", "dv", ["p" + pn, "xbb"], ["xlnb"],
                 lambda e, pn=pn, mt=mt: e.tensor_add(sb["xlnb"][:, mt, c0:c0 + cw],
                                                      ps[pn][:, 0:cw],
                                                      sb["xbb"][:, mt, c0:c0 + cw]))

        if reduced:
            self.emit_ln(B, g0, src="xlnb", dst="xab", gcol="ln2_g", bcol="ln2_b")
            for kd in range(KD):
                B.do("ve", "dv", ["xab"], ["xfprev"],
                     lambda e, kd=kd: e.tensor_copy(sb["xfprev"][:, kd:kd + 1],
                                                    sb["xab"][:, kd, G - 1:G]))
            return
        self.emit_ln(B, g0, src="xlnb", dst="xab", gcol="ln2_g", bcol="ln2_b")
        for kd in range(KD):
            B.do("ve", "dv", ["xab", "xfprev"], ["mix"],
                 lambda e, kd=kd: e.tensor_sub(sb["mix"][:, kd, 1:G], sb["xab"][:, kd, 0:G - 1],
                                               sb["xab"][:, kd, 1:G]))
            B.do("ve", "dv", ["xab", "xfprev"], ["mix"],
                 lambda e, kd=kd: e.tensor_sub(sb["mix"][:, kd, 0:1], sb["xfprev"][:, kd:kd + 1],
                                               sb["xab"][:, kd, 0:1]))
        for kd in range(KD):
            B.do("ve", "dv", ["xab"], ["xfprev"],
                 lambda e, kd=kd: e.tensor_copy(sb["xfprev"][:, kd:kd + 1], sb["xab"][:, kd, G - 1:G]))
        for kd in range(KD):
            B.do("ve", "dv", ["mix", "pv", "xab"], ["qf"],
                 lambda e, kd=kd: e.scalar_tensor_tensor(
                     sb["qf"][:, kd, :], sb["mix"][:, kd, :],
                     sb["pv"][:, kd, PV["ffn_maa_r"]:PV["ffn_maa_r"] + 1],
                     sb["xab"][:, kd, :], ALU.mult, ALU.add))
        for mt in range(KD):
            wt = "w1a" if mt % 2 == 0 else "w1b"
            B.dma("sy", [], [wt],
                  lambda e, mt=mt, wt=wt: e.dma_start(out=sb[wt][:], in_=io["scWr_ffn"][mt]))
            pn = "P0" if mt % 2 == 0 else "P1"
            B.do("te", "pe", [wt, "qf"], ["p" + pn],
                 lambda e, pn=pn, wt=wt: [
                     e.matmul(ps[pn][:, 0:G], sb[wt][:, kd, :],
                              sb["qf"][:, kd, :], start=(kd == 0), stop=(kd == KD - 1))
                     for kd in range(KD)][-1])
            B.do("se", "ac", ["p" + pn], ["kf"],
                 lambda e, pn=pn, mt=mt: e.activation(sb["kf"][:, mt, :], ps[pn][:, 0:G], AF.Sigmoid))
            if mt == 0:
                for kd in range(KD):
                    B.do("ve", "dv", ["mix", "pv", "xab"], ["xbb"],
                         lambda e, kd=kd: e.scalar_tensor_tensor(
                             sb["xbb"][:, kd, :], sb["mix"][:, kd, :],
                             sb["pv"][:, kd, PV["ffn_maa_k"]:PV["ffn_maa_k"] + 1],
                             sb["xab"][:, kd, :], ALU.mult, ALU.add))
        for j in range(FK // 128):
            wt = "w1a" if j % 2 == 0 else "w1b"
            B.dma("sy", [], [wt],
                  lambda e, j=j, wt=wt: e.dma_start(out=sb[wt][:], in_=io["scWk_ffn"][j]))
            pn = ("P0", "P1", "P2", "P3")[j % 4]
            B.do("te", "pe", [wt, "xbb"], ["p" + pn],
                 lambda e, pn=pn, wt=wt: [
                     e.matmul(ps[pn][:, 0:G], sb[wt][:, kd, :], sb["xbb"][:, kd, :],
                              start=(kd == 0), stop=(kd == KD - 1))
                     for kd in range(KD)][-1])
            B.do("se", "ac", ["p" + pn], ["xb16"],
                 lambda e, pn=pn: e.activation(sb["xb16"][:, :], ps[pn][:, 0:G], AF.Relu))
            B.do("se", "ac", ["xb16"], ["hbuf"],
                 lambda e, j=j: e.activation(sb["hbuf"][:, j, :], sb["xb16"][:, :], AF.Square))
        wv2 = self.wbig_v(28, 128)
        for mt in range(KD):
            use_big = mt % 2 == 1
            wk = "wbig" if use_big else "w2a"
            wap = wv2 if use_big else sb["w2a"][:, 0:28, :]
            B.dma("sy", [], [wk],
                  lambda e, mt=mt, wap=wap: e.dma_start(out=wap, in_=io["scWv_ffn"][mt]))
            pn = ("P0", "P1", "P2", "P3")[mt % 4]
            B.do("te", "pe", [wk, "hbuf"], ["p" + pn],
                 lambda e, pn=pn, wap=wap: [
                     e.matmul(ps[pn][:, 0:G], wap[:, k, :], sb["hbuf"][:, k, :],
                              start=(k == 0), stop=(k == 27))
                     for k in range(28)][-1])
            B.do("ve", "dv", ["p" + pn, "kf"], ["sc0"],
                 lambda e, pn=pn, mt=mt: e.tensor_mul(sb["sc0"][:, :], ps[pn][:, 0:G],
                                                      sb["kf"][:, mt, :]))
            B.do("ve", "dv", ["sc0", "xs"], ["xs"],
                 lambda e, mt=mt: e.tensor_add(sb["xs"][:, mt, g0:g0 + G], sb["xs"][:, mt, g0:g0 + G],
                                               sb["sc0"][:, :]))
            B.do("ve", "dv", ["xlnb", "xs"], ["xs"],
                 lambda e, mt=mt: e.tensor_add(sb["xs"][:, mt, g0:g0 + G], sb["xs"][:, mt, g0:g0 + G],
                                               sb["xlnb"][:, mt, :]))
        if self.gw and dst_name is not None:
            B.dma("sy", ["xs"], [],
                  lambda e: e.dma_start(
                      out=io[dst_name].rearrange("(k p) t -> p k t", p=128)[:, :, gr:gr + G],
                      in_=sb["xs"][:]))


    def alloc_l1f(self):
        s = self.sbuf
        need = lambda n: n not in self.sb
        if need("hbuf"):
            s("hbuf", [128, 32, G], BF16)
        if need("w1a"):
            s("w1a", [128, KD, 128], BF16)
            s("w1b", [128, KD, 128], BF16)
        if need("w2a"):
            s("w2a", [128, 32, 128], BF16)
        self.ensure_w1_ring()
        if need("idf"):
            s("idf", [128, 128], F32)
        need = lambda n: n not in self.sb
        if need("accf"):
            s("accf", [128, KD, G], F32)
        s("wgt32", [128, NCH, 8], F32)
        s("lgt", [128, 8], F32)
        s("lgc", [128, 8], F32)
        s("mk1", [128, 8], F32)
        s("mk2", [128, 8], F32)
        s("mx8", [128, 8], F32)
        s("lgt2", [128, 8], F32)
        s("mxb", [128, 8], F32)
        s("mk1b", [128, 8], F32)
        s("mk2b", [128, 8], F32)
        if need("gater"):
            s("gater", [1, G], F32)
        if need("gb"):
            s("gb", [128, G], BF16)
        s("rws", [128, KD, 8], BF16)
        if need("b1s"):
            s("b1s", [128, 64, 8], F32)
        s("rbL_s", [128, 8], F32)

    def emit_l1f_setup(self, B):
        sb = self.sb
        B.dma("gp", [], ["rws"],
              lambda e: e.dma_start(out=sb["rws"][:],
                                    in_=self.io["rwT"].rearrange("(k p) j -> p k j", p=128)))
        B.dma("sy", [], ["rbL_s"], lambda e: e.dma_start(out=sb["rbL_s"][:], in_=self.io["rbL"]))
        if self.has_l0a is False:
            B.dma("sy", [], ["idf"], lambda e: e.dma_start(out=sb["idf"][:], in_=self.consts["ident_f"].ap()))

    def emit_l1f(self, B, src_name=None, dst_name=None):
        sb, ps = self.sb, self.ps
        io = self.io
        self.pdm_gate(B, "sy", stage="moe")
        B.dma("sy", [], ["b1s"],
              lambda e: e.dma_start(out=sb["b1s"][:],
                                    in_=io["m_b1L"].rearrange("p (j e) -> p j e", e=8)))
        for g in range(NGRP):
            gr = g * G
            g0 = 0 if self.gw else gr
            if self.gw:
                B.dma("sy", [], ["xs"],
                      lambda e, gr=gr: e.dma_start(
                          out=sb["xs"][:],
                          in_=io[src_name].rearrange("(k p) t -> p k t", p=128)[:, :, gr:gr + G]))
            self.emit_ln(B, g0)
            for tc in range(NCH):
                B.do("te", "pe", ["xlnb", "rws"], ["pP2"],
                     lambda e, tc=tc: [
                         e.matmul(ps["P2"][:, 0:8], sb["xlnb"][:, kd, tc * L:(tc + 1) * L],
                                  sb["rws"][:, kd, :], start=(kd == 0), stop=(kd == KD - 1))
                         for kd in range(KD)][-1])
                B.do("ve", "dv", ["pP2", "rbL_s"], ["lgt"],
                     lambda e: e.tensor_add(sb["lgt"][:, :], ps["P2"][:, 0:8], sb["rbL_s"][:, :]))
                B.do("se", "ac", ["lgt"], ["lgt2"],
                     lambda e: e.copy(sb["lgt2"][:, :], sb["lgt"][:, :]))
                B.do("ve", "dv", ["lgt2"], ["mx8"],
                     lambda e: e.tensor_max(sb["mx8"][:, 0:4], sb["lgt2"][:, 0:4], sb["lgt2"][:, 4:8]))
                B.do("se", "ac", ["mx8"], ["mxb"],
                     lambda e: e.copy(sb["mxb"][:, 0:4], sb["mx8"][:, 0:4]))
                B.do("ve", "dv", ["mxb"], ["mx8"],
                     lambda e: e.tensor_max(sb["mx8"][:, 4:6], sb["mxb"][:, 0:2], sb["mxb"][:, 2:4]))
                B.do("se", "ac", ["mx8"], ["mxb"],
                     lambda e: e.copy(sb["mxb"][:, 4:6], sb["mx8"][:, 4:6]))
                B.do("ve", "dv", ["mxb"], ["mx8"],
                     lambda e: e.tensor_max(sb["mx8"][:, 6:7], sb["mxb"][:, 4:5], sb["mxb"][:, 5:6]))
                B.do("se", "ac", ["mx8"], ["mxb"],
                     lambda e: e.copy(sb["mxb"][:, 6:7], sb["mx8"][:, 6:7]))
                B.do("ve", "dv", ["lgt2", "mxb"], ["mk1"],
                     lambda e: e.tensor_scalar(sb["mk1"][:, :], sb["lgt2"][:, :], sb["mxb"][:, 6:7],
                                               None, ALU.is_equal))
                B.do("se", "ac", ["mk1"], ["mk1b"],
                     lambda e: e.copy(sb["mk1b"][:, :], sb["mk1"][:, :]))
                B.do("ve", "dv", ["mk1b", "lgt2"], ["mk2"],
                     lambda e: e.scalar_tensor_tensor(sb["mk2"][:, :], sb["mk1b"][:, :], -1e30,
                                                      sb["lgt2"][:, :], ALU.mult, ALU.add))
                B.do("se", "ac", ["mk2"], ["mk2b"],
                     lambda e: e.copy(sb["mk2b"][:, :], sb["mk2"][:, :]))
                B.do("ve", "dv", ["mk2b"], ["mx8"],
                     lambda e: e.tensor_max(sb["mx8"][:, 0:4], sb["mk2b"][:, 0:4], sb["mk2b"][:, 4:8]))
                B.do("se", "ac", ["mx8"], ["mxb"],
                     lambda e: e.copy(sb["mxb"][:, 0:4], sb["mx8"][:, 0:4]))
                B.do("ve", "dv", ["mxb"], ["mx8"],
                     lambda e: e.tensor_max(sb["mx8"][:, 4:6], sb["mxb"][:, 0:2], sb["mxb"][:, 2:4]))
                B.do("se", "ac", ["mx8"], ["mxb"],
                     lambda e: e.copy(sb["mxb"][:, 4:6], sb["mx8"][:, 4:6]))
                B.do("ve", "dv", ["mxb"], ["mx8"],
                     lambda e: e.tensor_max(sb["mx8"][:, 7:8], sb["mxb"][:, 4:5], sb["mxb"][:, 5:6]))
                B.do("se", "ac", ["mx8"], ["mxb"],
                     lambda e: e.copy(sb["mxb"][:, 7:8], sb["mx8"][:, 7:8]))
                B.do("ve", "dv", ["mxb"], ["lgc"],
                     lambda e: e.tensor_scalar(sb["lgc"][:, 0:1], sb["mxb"][:, 7:8],
                                               sb["mxb"][:, 6:7], None, ALU.subtract))
                B.do("se", "ac", ["lgc"], ["lgc"],
                     lambda e: e.activation(sb["lgc"][:, 4:5], sb["lgc"][:, 0:1], AF.Exp))
                B.do("se", "ac", ["lgc", "onecol_f"], ["lgc"],
                     lambda e: e.activation(sb["lgc"][:, 5:6], sb["lgc"][:, 4:5], AF.Ln,
                                            bias=sb["onecol_f"][:, 0:1]))
                B.do("se", "ac", ["lgc", "negone"], ["lgc"],
                     lambda e: e.activation(sb["lgc"][:, 5:6], sb["lgc"][:, 5:6], AF.Exp,
                                            scale=sb["negone"][:, 0:1]))
                B.do("ve", "dv", ["lgc"], ["lgc"],
                     lambda e: e.tensor_mul(sb["lgc"][:, 6:7], sb["lgc"][:, 4:5], sb["lgc"][:, 5:6]))
                B.do("ve", "dv", ["mk2b", "mxb"], ["mk2"],
                     lambda e: e.tensor_scalar(sb["mk2"][:, :], sb["mk2b"][:, :], sb["mxb"][:, 7:8],
                                               None, ALU.is_equal))
                B.do("se", "ac", ["mk2", "lgc"], ["mk2b"],
                     lambda e: e.mul(sb["mk2b"][:, :], sb["mk2"][:, :], sb["lgc"][:, 6:7]))
                B.do("ve", "dv", ["mk1b", "lgc", "mk2b"], ["wgt32"],
                     lambda e, tc=tc: e.scalar_tensor_tensor(
                         sb["wgt32"][:, tc, :], sb["mk1b"][:, :], sb["lgc"][:, 5:6],
                         sb["mk2b"][:, :], ALU.mult, ALU.add))
                if tc == 0 and g == 0:
                    self.tap(B, "lgt_0", "lgt", sb["lgt"][:], [128, 8])
                    self.tap(B, "mk1_0", "mk1", sb["mk1"][:], [128, 8])
                    self.tap(B, "mk2_0", "mk2", sb["mk2"][:], [128, 8])
                    self.tap(B, "lgc_0", "lgc", sb["lgc"][:], [128, 8])
            if g == 0:
                self.tap(B, "wgt0", "wgt32", sb["wgt32"][:], [128, NCH, 8])
                self.tap(B, "xln3", "xlnb", sb["xlnb"][:], [128, KD, G])
            for ex in range(NE):
                for tc in range(NCH):
                    B.do("te", "pe", ["wgt32", "idf"], ["pP3"],
                         lambda e, tc=tc, ex=ex: e.transpose(
                             ps["P3"][0:1, 0:128], sb["wgt32"][:, tc, ex:ex + 1], sb["idf"][:, :]))
                    B.do("ve", "dv", ["pP3"], ["gater"],
                         lambda e, tc=tc: e.tensor_copy(sb["gater"][0:1, tc * L:(tc + 1) * L],
                                                        ps["P3"][0:1, 0:128]))
                B.do("te", "pe", ["gater", "onerow_f"], ["pP3"],
                     lambda e: e.matmul(ps["P3"][:, 0:G], sb["onerow_f"][0:1, :], sb["gater"][0:1, :],
                                        start=True, stop=True))
                B.do("ve", "dv", ["pP3"], ["gb"],
                     lambda e: e.tensor_copy(sb["gb"][:, :], ps["P3"][:, 0:G]))
                if g == 0 and ex == 0:
                    self.tap(B, "gb0", "gb", sb["gb"][:], [128, G])
                for j in range(32):
                    wt = MK.W1RING[j % 6]
                    B.dma("sy", [], [wt],
                          lambda e, j=j, ex=ex, wt=wt: e.dma_start(
                              out=sb[wt][:], in_=io["scw1"][ex, j]))
                    pn = ("P0", "P1", "P2")[j % 3]
                    B.do("te", "pe", [wt, "xlnb"], ["p" + pn],
                         lambda e, pn=pn, wt=wt: [
                             e.matmul(ps[pn][:, 0:G], sb[wt][:, kd, :], sb["xlnb"][:, kd, :],
                                      start=(kd == 0), stop=(kd == KD - 1))
                             for kd in range(KD)][-1])
                    B.do("se", "ac", ["p" + pn, "b1s"], ["hbuf"],
                         lambda e, pn=pn, j=j, ex=ex: e.activation(
                             sb["hbuf"][:, j, :], ps[pn][:, 0:G], AF.Silu,
                             bias=sb["b1s"][:, j, ex:ex + 1]))
                for j in range(32):
                    wt = MK.W1RING[(j + 32) % 6]
                    B.dma("sy", [], [wt],
                          lambda e, j=j, ex=ex, wt=wt: e.dma_start(
                              out=sb[wt][:], in_=io["scw1"][ex, j + 32]))
                    pn = ("P0", "P1", "P2")[j % 3]
                    B.do("te", "pe", [wt, "xlnb"], ["p" + pn],
                         lambda e, pn=pn, wt=wt: [
                             e.matmul(ps[pn][:, 0:G], sb[wt][:, kd, :], sb["xlnb"][:, kd, :],
                                      start=(kd == 0), stop=(kd == KD - 1))
                             for kd in range(KD)][-1])
                    B.do("ve", "dv", ["p" + pn, "b1s", "hbuf"], ["hbuf"],
                         lambda e, pn=pn, j=j, ex=ex: e.scalar_tensor_tensor(
                             sb["hbuf"][:, j, :], ps[pn][:, 0:G],
                             sb["b1s"][:, j + 32, ex:ex + 1],
                             sb["hbuf"][:, j, :], ALU.add, ALU.mult))
                wv2f = self.wbig_v(32, 128)
                for mt in range(KD):
                    wk = "w2a" if mt % 2 == 0 else "wbig"
                    wap = sb["w2a"][:] if mt % 2 == 0 else wv2f
                    B.dma("sy", [], [wk],
                          lambda e, mt=mt, ex=ex, wap=wap: e.dma_start(
                              out=wap, in_=io["scw2"][ex, mt]))
                    pn = ("P0", "P1", "P2")[mt % 3]
                    B.do("te", "pe", [wk, "hbuf"], ["p" + pn],
                         lambda e, pn=pn, wap=wap: [
                             e.matmul(ps[pn][:, 0:G], wap[:, k, :], sb["hbuf"][:, k, :],
                                      start=(k == 0), stop=(k == 31))
                             for k in range(31 + 1)][-1])
                    bn = "m_b2_%d" % ex
                    if ex == 0:
                        B.do("ve", "dv", ["p" + pn, "pv", "gb"], ["accf"],
                             lambda e, pn=pn, mt=mt, bn=bn: e.scalar_tensor_tensor(
                                 sb["accf"][:, mt, :], ps[pn][:, 0:G],
                                 sb["pv"][:, mt, PV[bn]:PV[bn] + 1], sb["gb"][:, :],
                                 ALU.add, ALU.mult))
                    else:
                        B.do("ve", "dv", ["p" + pn, "pv", "gb"], ["sc1"],
                             lambda e, pn=pn, mt=mt, bn=bn: e.scalar_tensor_tensor(
                                 sb["sc1"][:, :], ps[pn][:, 0:G],
                                 sb["pv"][:, mt, PV[bn]:PV[bn] + 1], sb["gb"][:, :],
                                 ALU.add, ALU.mult))
                        B.do("ve", "dv", ["sc1", "accf"], ["accf"],
                             lambda e, mt=mt: e.tensor_add(sb["accf"][:, mt, :],
                                                           sb["accf"][:, mt, :],
                                                           sb["sc1"][:, :]))
            if g == 0:
                self.tap(B, "acc_fin", "accf", sb["accf"][:], [128, KD, G])
            for mt in range(KD):
                B.do("ve", "dv", ["accf", "xs"], ["xs"],
                     lambda e, mt=mt, g0=g0: e.tensor_add(sb["xs"][:, mt, g0:g0 + G],
                                                          sb["xs"][:, mt, g0:g0 + G],
                                                          sb["accf"][:, mt, :]))
            if self.gw:
                B.dma("sy", ["xs"], [],
                      lambda e, gr=gr: e.dma_start(
                          out=io[dst_name].rearrange("(k p) t -> p k t", p=128)[:, :, gr:gr + G],
                          in_=sb["xs"][:]))



    def alloc_l0f_fused(self):
        s = self.sbuf
        if "w1a" not in self.sb:
            s("w1a", [128, KD, 128], BF16)
            s("w1b", [128, KD, 128], BF16)
        if "w2a" not in self.sb:
            s("w2a", [128, 32, 128], BF16)

    def emit_reload_wqkv(self, B):
        self.emit_build_omegas(B)
        sb = self.sb
        B.dma("sy", [], ["wqkv"],
              lambda e: e.dma_start(out=sb["wqkv"][:], in_=self.io["scqkv"]))
        B.dma("sy", [], ["wbig"],
              lambda e: e.dma_start(out=sb["wbig"][:], in_=self.io["scwo"]))


def build_l1a_launch(spine=False):
    mk = MK("l1a", gw=True)
    nc = mk.nc
    mk.dram_in("xmn", [D, TL])
    if spine:
        mk.dram_in("xsp", [D, TL])
    mk.dram_in("role", [1, 1])
    mk.dram_in("rolecol", [128, 1])
    mk.dram_in("scolh", [128, 1])
    mk.dram_in("quads", [2, 2])
    mk.dram_in("pvecT", [D, NV])
    for k in ("tm_w1", "td_w1", "td_w2", "Wr", "Wk", "Wv", "Wg", "Wo",
              "Wk_ffn", "Wv_ffn", "Wr_ffn"):
        mk.dram_in(k, list(L1A_W_SHAPES[k]))
    mk.dram_in("tm_w2", [5, 32, D])
    mk.dram_in("lnxgrow", [128, D])
    mk.dram_in("lnxbrow", [128, D])
    mk.alloc_predump(moe=False, l0f=False, l1a=True)
    mk.dram_out("x3mn", [D, TL])
    mk.alloc_common(l0a=False)
    mk.alloc_l1a()
    if "w1a" not in mk.sb:
        mk.sbuf("w1a", [128, KD, 128], BF16)
        mk.sbuf("w1b", [128, KD, 128], BF16)
    if "w2a" not in mk.sb:
        mk.sbuf("w2a", [128, 32, 128], BF16)
    with nc.body("prog", valid_engines=[ET.Pool, ET.Activation, ET.PE, ET.DVE, ET.SP]) as _:
        B = BB(nc)
        mk.emit_setup(B, l0a=False)
        mk.emit_l1a_setup(B)
        mk.emit_predump(moe=False, l0f=False, l1a=True)
        if spine:
            mk.emit_l1a(B, zero_state=True, mask_by_role=False, src_name="xsp", dst_name=None)
            mk.emit_l1a(B, zero_state=False, mask_by_role=True, src_name="xmn", dst_name="x3mn")
        else:
            mk.emit_l1a(B, zero_state=True, mask_by_role=False, src_name="xmn", dst_name="x3mn")
        B.fence_all("sy")
    return mk


def run_l1a_test(p):
    h = prep_host(p)
    mk = build_l1a_launch()
    in_maps = []
    for c in range(8):
        b, half = c // 2, c % 2
        m = {"xmn": h["x_fm"][(b, half)], "role": np.array([[float(half)]], np.float32),
             "rolecol": np.full((128, 1), float(half), np.float32), "scolh": h["scolh"],
             "quads": h["quads"], "pvecT": h["pvecT"], "tm_w2": h["tm_w2"],
             "lnxgrow": h["lnxgrow"], "lnxbrow": h["lnxbrow"]}
        for k in ("tm_w1", "td_w1", "td_w2", "Wr", "Wk", "Wv", "Wg", "Wo",
                  "Wk_ffn", "Wv_ffn", "Wr_ffn"):
            m[k] = h[k]
        in_maps.append(m)
    return run_bass_kernel_spmd(mk.nc, in_maps, core_ids=list(range(8)))


def build_l1f_launch(dbg=False):
    mk = MK("l1f", gw=True, dbg=dbg)
    nc = mk.nc
    mk.dram_in("xmn", [D, TL])
    mk.dram_in("role", [1, 1])
    mk.dram_in("rolecol", [128, 1])
    mk.dram_in("scolh", [128, 1])
    mk.dram_in("quads", [2, 2])
    mk.dram_in("pvecT", [D, NV])
    mk.dram_in("rwT", [D, 8])
    mk.dram_in("rbL", [128, 8])
    mk.dram_in("m_b1L", [128, 512])
    mk.io["m_w1T"] = [mk.nc.dram_tensor("m_w1T_%d" % ex, [64, 128, KD, 128], F32,
                                        kind="ExternalInput").ap() for ex in range(NE)]
    mk.io["m_w2T"] = [mk.nc.dram_tensor("m_w2T_%d" % ex, [KD, 128, 32, 128], F32,
                                        kind="ExternalInput").ap() for ex in range(NE)]
    mk.alloc_predump(moe=True, l0f=False)
    mk.dram_out("x4mn", [D, TL])
    mk.alloc_common(l0a=False)
    if "wbig" not in mk.sb:
        mk.sbuf("wbig", [128, 32, 128], BF16)
    mk.alloc_l1f()
    with nc.body("prog", valid_engines=[ET.Pool, ET.Activation, ET.PE, ET.DVE, ET.SP]) as _:
        B = BB(nc)
        mk.emit_setup(B, l0a=False)
        mk.emit_l1f_setup(B)
        mk.emit_predump(moe=True, l0f=False)
        mk.emit_l1f(B, src_name="xmn", dst_name="x4mn")
        B.fence_all("sy")
    return mk


def run_l1f_test(p, dbg=False):
    h = prep_host(p)
    mk = build_l1f_launch(dbg)
    in_maps = []
    for c in range(8):
        b, half = c // 2, c % 2
        m = {"xmn": h["x_fm"][(b, half)], "role": np.array([[float(half)]], np.float32),
             "rolecol": np.full((128, 1), float(half), np.float32), "scolh": h["scolh"],
             "quads": h["quads"], "pvecT": h["pvecT"], "rwT": h["rwT"],
             "rbL": h["rbL"], "m_b1L": h["m_b1L"]}
        for ex in range(NE):
            m["m_w1T_%d" % ex] = h["m_w1T"][ex]
            m["m_w2T_%d" % ex] = h["m_w2T"][ex]
        in_maps.append(m)
    return run_bass_kernel_spmd(mk.nc, in_maps, core_ids=list(range(8)))



def build_l0f_launch(spine=False):
    mk = MK("l0f")
    nc = mk.nc
    mk.dram_in("xmn", [D, TL])
    mk.dram_in("role", [1, 1])
    mk.dram_in("rolecol", [128, 1])
    mk.dram_in("scolh", [128, 1])
    mk.dram_in("quads", [2, 2])
    mk.dram_in("pvecT", [D, NV])
    mk.dram_in("w1f0", [32, 128, KD, 128])
    mk.dram_in("w2f0", [KD, 128, 32, 128])
    mk.alloc_predump(moe=False, l0f=True)
    mk.dram_out("x2mn", [D, TL])
    if spine:
        mk.dram_in("xsp", [D, TL])
        mk.dram_out("x2sp", [D, TL])
    mk.alloc_common(l0a=False)
    mk.alloc_l0f()
    with nc.body("prog", valid_engines=[ET.Pool, ET.Activation, ET.PE, ET.DVE, ET.SP]) as _:
        B = BB(nc)
        mk.emit_setup(B, l0a=False)
        mk.emit_predump(moe=False, l0f=True)
        if spine:
            mk.emit_load_stream(B, "xsp")
            mk.emit_l0f(B)
            mk.emit_store_stream(B, "x2sp")
        mk.emit_load_stream(B, "xmn")
        mk.emit_l0f(B)
        mk.emit_store_stream(B, "x2mn")
        B.fence_all("sy")
    return mk


def run_l0f_test(p):
    h = prep_host(p)
    mk = build_l0f_launch()
    in_maps = []
    for c in range(8):
        b, half = c // 2, c % 2
        m = {"xmn": h["x_fm"][(b, half)], "role": np.array([[float(half)]], np.float32),
             "rolecol": np.full((128, 1), float(half), np.float32), "scolh": h["scolh"],
             "quads": h["quads"], "pvecT": h["pvecT"],
             "w1f0": h["w1f0"], "w2f0": h["w2f0"]}
        in_maps.append(m)
    return run_bass_kernel_spmd(mk.nc, in_maps, core_ids=list(range(8)))




def build_l0a_launch(spine_and_main, dbg=False):
    mk = MK("l0a", dbg=dbg)
    nc = mk.nc
    mk.dram_in("xmn", [D, TL])
    if spine_and_main:
        mk.dram_in("xsp", [D, TL])
    mk.dram_in("role", [1, 1])
    mk.dram_in("rolecol", [128, 1])
    mk.dram_in("scolh", [128, 1])
    mk.dram_in("wqkvT", [128, KD, 3 * D])
    mk.dram_in("woT", [128, KD, D])
    mk.dram_in("omegaT", [64, 1024])
    mk.dram_in("quads", [2, 2])
    mk.dram_in("bqkv", [3, 1024])
    mk.dram_in("pvecT", [D, NV])
    mk.alloc_predump_l0a()
    mk.dram_out("x1mn", [D, TL])
    if spine_and_main:
        mk.dram_out("x1sp", [D, TL])
    mk.alloc_common()
    with nc.body("prog", valid_engines=[ET.Pool, ET.Activation, ET.PE, ET.DVE, ET.SP]) as _:
        B = BB(nc)
        mk.emit_predump_l0a()
        mk.emit_setup(B)
        if spine_and_main:
            mk.emit_load_stream(B, "xsp")
            mk.emit_l0a(B, zero_state=True, mask_state_by_role=False)
            mk.emit_store_stream(B, "x1sp")
            mk.emit_load_stream(B, "xmn")
            mk.emit_l0a(B, zero_state=False, mask_state_by_role=True)
            mk.emit_store_stream(B, "x1mn")
        else:
            mk.emit_load_stream(B, "xmn")
            mk.emit_l0a(B, zero_state=True, mask_state_by_role=False)
            mk.emit_store_stream(B, "x1mn")
        B.fence_all("sy")
    return mk


def run_l0a_test(p, spine_and_main=False, dbg=False):
    h = prep_host(p)
    mk = build_l0a_launch(spine_and_main, dbg=dbg)
    in_maps = []
    for c in range(8):
        b, half = c // 2, c % 2
        m = {"xmn": h["x_fm"][(b, half)], "role": np.array([[float(half)]], np.float32),
             "rolecol": np.full((128, 1), float(half), np.float32), "scolh": h["scolh"],
             "wqkvT": h["wqkvT"], "woT": h["woT"], "omegaT": h["omegaT"],
             "quads": h["quads"], "bqkv": h["bqkv"], "pvecT": h["pvecT"]}
        if spine_and_main:
            m["xsp"] = h["x_fm"][(b, 0)]
        in_maps.append(m)
    res = run_bass_kernel_spmd(mk.nc, in_maps, core_ids=list(range(8)))
    return res




def build_fused(dbg_streams=False):
    mk = MK("fused", gw=True)
    nc = mk.nc
    mk.dram_in("xmn", [D, TL])
    mk.dram_in("xsp", [D, TL])
    mk.dram_in("role", [1, 1])
    mk.dram_in("rolecol", [128, 1])
    mk.dram_in("scolh", [128, 1])
    mk.dram_in("quads", [2, 2])
    mk.dram_in("pvecT", [D, NV])
    mk.dram_in("wqkvT", [128, KD, 3 * D])
    mk.dram_in("woT", [128, KD, D])
    mk.dram_in("omegaT", [64, 1024])
    mk.dram_in("bqkv", [3, 1024])
    mk.dram_in("w1f0", [32, 128, KD, 128])
    mk.dram_in("w2f0", [KD, 128, 32, 128])
    for k in ("tm_w1", "td_w1", "td_w2", "Wr", "Wk", "Wv", "Wg", "Wo",
              "Wk_ffn", "Wv_ffn", "Wr_ffn"):
        mk.dram_in(k, list(L1A_W_SHAPES[k]))
    mk.dram_in("tm_w2", [5, 32, D])
    mk.dram_in("lnxgrow", [128, D])
    mk.dram_in("lnxbrow", [128, D])
    mk.dram_in("rwT", [D, 8])
    mk.dram_in("rbL", [128, 8])
    mk.dram_in("m_b1L", [128, 512])
    mk.io["m_w1T"] = [nc.dram_tensor("m_w1T_%d" % ex, [64, 128, KD, 128], F32,
                                     kind="ExternalInput").ap() for ex in range(NE)]
    mk.io["m_w2T"] = [nc.dram_tensor("m_w2T_%d" % ex, [KD, 128, 32, 128], F32,
                                     kind="ExternalInput").ap() for ex in range(NE)]
    mk.alloc_predump(moe=True, l0f=True, l1a=True)
    mk.alloc_predump_l0a()
    mk.dram_out("x4mn", [D, TL])
    kind = "ExternalOutput" if dbg_streams else "Internal"
    mk.io["s1"] = nc.dram_tensor("s1", [D, TL], F32, kind=kind).ap()
    mk.io["s2"] = nc.dram_tensor("s2", [D, TL], F32, kind=kind).ap()
    mk.io["s3"] = nc.dram_tensor("s3", [D, TL], F32, kind=kind).ap()

    mk.alloc_common(l0a=True)
    addr = nc.lookup_mls(mk.sb["wqkv"]).memorylocations[0].addr
    mk.sb["hbuf"] = nc.alloc_sbuf_tensor_at("hbuf", [128, 32, G], BF16, offset=addr)
    mk.sb["accf"] = nc.alloc_sbuf_tensor_at("accf", [128, KD, G], F32, offset=addr + 32768)
    mk.sbuf("w1a", [128, KD, 128], BF16)
    mk.sbuf("w1b", [128, KD, 128], BF16)
    mk.sbuf("w2a", [128, 32, 128], BF16)
    addr2 = nc.lookup_mls(mk.sb["w2a"]).memorylocations[0].addr
    mk.sb["gtm"] = nc.alloc_sbuf_tensor_at("gtm", [128, NCH, 1024], BF16, offset=addr2)
    aA = nc.lookup_mls(mk.sb["omA"]).memorylocations[0].addr
    aB = nc.lookup_mls(mk.sb["omB"]).memorylocations[0].addr
    mk.sb["lnxg_r"] = nc.alloc_sbuf_tensor_at("lnxg_r", [128, 1024], BF16, offset=aA)
    mk.sb["lnxb_r"] = nc.alloc_sbuf_tensor_at("lnxb_r", [128, 1024], BF16, offset=aB)
    mk.alloc_l1a()
    mk.alloc_l0f_fused()
    aK = nc.lookup_mls(mk.sb["kft"]).memorylocations[0].addr
    aO = nc.lookup_mls(mk.sb["otm"]).memorylocations[0].addr
    aL = nc.lookup_mls(mk.sb["loraf"]).memorylocations[0].addr
    mk.sb["gater"] = nc.alloc_sbuf_tensor_at("gater", [1, G], F32, offset=aK)
    mk.sb["gb"] = nc.alloc_sbuf_tensor_at("gb", [128, G], BF16, offset=aO)
    mk.sb["b1s"] = nc.alloc_sbuf_tensor_at("b1s", [128, 64, 8], F32, offset=aL)
    aV = nc.lookup_mls(mk.sb["vtm"]).memorylocations[0].addr
    for _i, _nm in enumerate(("w1c", "w1d", "w1e", "w1f")):
        mk.sb[_nm] = nc.alloc_sbuf_tensor_at(_nm, [128, KD, 128], BF16,
                                             offset=aV + _i * 2048)
    mk.alloc_l1f()
    with nc.body("prog", valid_engines=[ET.Pool, ET.Activation, ET.PE, ET.DVE, ET.SP]) as _:
        B = BB(nc)
        B.conflict("hbuf", "wqkv")
        B.conflict("accf", "wqkv")
        B.conflict("gtm", "w2a")
        B.conflict("lnxg_r", "omA")
        B.conflict("lnxb_r", "omB")
        B.conflict("gater", "kft")
        B.conflict("gb", "otm")
        B.conflict("b1s", "loraf")
        for _nm in ("w1c", "w1d", "w1e", "w1f"):
            B.conflict(_nm, "vtm")
        with nc.named_scope("setup"):
            mk.emit_predump_l0a()
            mk.emit_setup(B, l0a=True)
            mk.emit_l1a_setup(B)
            mk.emit_l1f_setup(B)
            mk.emit_predump(moe=False, l0f=True, l1a=True)
        with nc.named_scope("sL0A"):
            mk.emit_l0a(B, zero_state=True, mask_state_by_role=False,
                        src_name="xsp", dst_name="s1")
        with nc.named_scope("sL0F"):
            mk.emit_l0f(B, src_name="s1", dst_name="s2")
        with nc.named_scope("sL1A"):
            mk.emit_l1a(B, zero_state=True, mask_by_role=False, src_name="s2", dst_name=None,
                        reduced=True)
        with nc.named_scope("reload"):
            mk.emit_reload_wqkv(B)
            mk.emit_predump(moe=True, l0f=False, l1a=False)
        with nc.named_scope("mL0A"):
            mk.emit_l0a(B, zero_state=False, mask_state_by_role=True,
                        src_name="xmn", dst_name="s1")
        with nc.named_scope("mL0F"):
            mk.emit_l0f(B, src_name="s1", dst_name="s2")
        with nc.named_scope("mL1A"):
            mk.emit_l1a(B, zero_state=False, mask_by_role=True, src_name="s2", dst_name="s3")
        with nc.named_scope("mL1F"):
            mk.emit_l1f(B, src_name="s3", dst_name="x4mn")
        B.fence_all("sy")
    return mk


_PROGS = {}
FUSED = True


def _get_progs():
    if "p" not in _PROGS:
        _PROGS["p"] = (build_l0a_launch(True), build_l0f_launch(True),
                       build_l1a_launch(True), build_l1f_launch())
    return _PROGS["p"]


def _get_fused():
    if "f" not in _PROGS:
        _PROGS["f"] = build_fused()
    return _PROGS["f"]


def kernel(**inputs):
    h = prep_host(inputs)
    if FUSED:
        mk = _get_fused()
        im = []
        for c in range(8):
            b, half = c // 2, c % 2
            m = {"xmn": h["x_fm"][(b, half)], "xsp": h["x_fm"][(b, 0)],
                 "role": np.array([[float(half)]], np.float32),
                 "rolecol": np.full((128, 1), float(half), np.float32),
                 "scolh": h["scolh"], "quads": h["quads"], "pvecT": h["pvecT"],
                 "wqkvT": h["wqkvT"], "woT": h["woT"], "omegaT": h["omegaT"],
                 "bqkv": h["bqkv"], "w1f0": h["w1f0"], "w2f0": h["w2f0"],
                 "tm_w2": h["tm_w2"], "lnxgrow": h["lnxgrow"], "lnxbrow": h["lnxbrow"],
                 "rwT": h["rwT"], "rbL": h["rbL"], "m_b1L": h["m_b1L"]}
            for k in ("tm_w1", "td_w1", "td_w2", "Wr", "Wk", "Wv", "Wg", "Wo",
                      "Wk_ffn", "Wv_ffn", "Wr_ffn"):
                m[k] = h[k]
            for ex in range(NE):
                m["m_w1T_%d" % ex] = h["m_w1T"][ex]
                m["m_w2T_%d" % ex] = h["m_w2T"][ex]
            im.append(m)
        rr = run_bass_kernel_spmd(mk.nc, im, core_ids=list(range(8))).results
        out = np.zeros((4, 4096, 1024), np.float32)
        for c in range(8):
            b, half = c // 2, c % 2
            out[b, half * 2048:(half + 1) * 2048] = rr[c]["x4mn"].T
        return out
    mka, mkf0, mk1a, mk1f = _get_progs()
    base = []
    for c in range(8):
        b, half = c // 2, c % 2
        base.append({"role": np.array([[float(half)]], np.float32),
                     "rolecol": np.full((128, 1), float(half), np.float32),
                     "scolh": h["scolh"], "quads": h["quads"], "pvecT": h["pvecT"]})
    im = []
    for c in range(8):
        b, half = c // 2, c % 2
        m = dict(base[c])
        m.update({"xmn": h["x_fm"][(b, half)], "xsp": h["x_fm"][(b, 0)],
                  "wqkvT": h["wqkvT"], "woT": h["woT"], "omegaT": h["omegaT"],
                  "bqkv": h["bqkv"]})
        im.append(m)
    r1 = run_bass_kernel_spmd(mka.nc, im, core_ids=list(range(8))).results
    im = []
    for c in range(8):
        m = dict(base[c])
        m.update({"xmn": r1[c]["x1mn"], "xsp": r1[c]["x1sp"],
                  "w1f0": h["w1f0"], "w2f0": h["w2f0"]})
        im.append(m)
    r2 = run_bass_kernel_spmd(mkf0.nc, im, core_ids=list(range(8))).results
    im = []
    for c in range(8):
        m = dict(base[c])
        m.update({"xmn": r2[c]["x2mn"], "xsp": r2[c]["x2sp"], "tm_w2": h["tm_w2"],
                  "lnxgrow": h["lnxgrow"], "lnxbrow": h["lnxbrow"]})
        for k in ("tm_w1", "td_w1", "td_w2", "Wr", "Wk", "Wv", "Wg", "Wo",
                  "Wk_ffn", "Wv_ffn", "Wr_ffn"):
            m[k] = h[k]
        im.append(m)
    r3 = run_bass_kernel_spmd(mk1a.nc, im, core_ids=list(range(8))).results
    im = []
    for c in range(8):
        m = dict(base[c])
        m.update({"xmn": r3[c]["x3mn"], "rwT": h["rwT"], "rbL": h["rbL"],
                  "m_b1L": h["m_b1L"]})
        for ex in range(NE):
            m["m_w1T_%d" % ex] = h["m_w1T"][ex]
            m["m_w2T_%d" % ex] = h["m_w2T"][ex]
        im.append(m)
    r4 = run_bass_kernel_spmd(mk1f.nc, im, core_ids=list(range(8))).results
    out = np.zeros((4, 4096, 1024), np.float32)
    for c in range(8):
        b, half = c // 2, c % 2
        out[b, half * 2048:(half + 1) * 2048] = r4[c]["x4mn"].T
    return out





# version 14 (final)
# speedup vs baseline: 1.0111x; 1.0111x over previous
import numpy as np
import concourse.bass as bass
import concourse.mybir as mybir
from concourse.bass_utils import run_bass_kernel_spmd

F32 = mybir.dt.float32
BF16 = mybir.dt.bfloat16
AF = mybir.ActivationFunctionType
ALU = mybir.AluOpType
ET = mybir.EngineType

D = 1024
KD = 8
TL = 2048
G = 512
NGRP = TL // G
L = 128
NCH = G // L
NH = 16
HD = 64
DFF0 = 4096
FK = 3584
DFF1 = 4096
NE = 8

INV_SQRT32 = float(1.0 / np.sqrt(32.0))


ENG_OF = {"te": "tensor", "ve": "vector", "se": "scalar", "sy": "sync", "gp": "gpsimd"}
NSY, NGP = 8, 6
CN_ENG = {"pe": "te", "dv": "ve", "ac": "se", "po": "gp"}
for _i in range(NSY):
    CN_ENG[f"da{_i}"] = "__dma__"
for _i in range(NGP):
    CN_ENG[f"db{_i}"] = "__dma__"


class Ctr:
    def __init__(self, nc, name):
        self.h = nc.alloc_semaphore(name)
        self.n = 0
        self.name = name


FULL_SYNC = False
FS_CTRS = None
SAME_ENGINE_SKIP = True


class BB:

    def __init__(self, nc):
        self.nc = nc
        self.c = {k: Ctr(nc, "sem_" + k) for k in ("pe", "dv", "ac", "po")}
        for i in range(NSY):
            self.c[f"da{i}"] = Ctr(nc, f"sem_da{i}")
        for i in range(NGP):
            self.c[f"db{i}"] = Ctr(nc, f"sem_db{i}")
        self._rr = {"sy": 0, "gp": 0}
        self.conflicts = {}
        self.marks = {}
        self.wm = {e: {} for e in ENG_OF}

    def conflict(self, a, b):
        self.conflicts.setdefault(a, set()).add(b)
        self.conflicts.setdefault(b, set()).add(a)

    def eng(self, e):
        return getattr(self.nc, ENG_OF[e])

    def _wait(self, e, ctr_name, val):
        if val <= 0:
            return
        if self.wm[e].get(ctr_name, 0) >= val:
            return
        self.eng(e).wait_ge(self.c[ctr_name].h, val)
        self.wm[e][ctr_name] = val

    def do(self, e, ctr, reads, writes, f, inc=1):
        need = {}
        if FULL_SYNC and (FULL_SYNC is True or e in FULL_SYNC):
            for _cn, _ctr in self.c.items():
                if FS_CTRS is None or _cn in FS_CTRS:
                    need[_cn] = _ctr.n
        for k in reads:
            for kk in (k, *self.conflicts.get(k, ())):
                m = self.marks.get(kk)
                if m and m["w"]:
                    cn, v = m["w"]
                    need[cn] = max(need.get(cn, 0), v)
        for k in writes:
            for kk in (k, *self.conflicts.get(k, ())):
                m = self.marks.get(kk)
                if m:
                    if m["w"]:
                        cn, v = m["w"]
                        need[cn] = max(need.get(cn, 0), v)
                    for cn, v in m["r"].items():
                        need[cn] = max(need.get(cn, 0), v)
        if SAME_ENGINE_SKIP:
            for cn in list(need):
                if CN_ENG[cn] == e:
                    need.pop(cn)
        for cn, v in sorted(need.items()):
            self._wait(e, cn, v)
        ins = f(self.eng(e))
        assert ins is not None, "emitter must return the last instruction"
        ins.then_inc(self.c[ctr].h, inc)
        self.c[ctr].n += inc
        cnt = self.c[ctr].n
        for k in writes:
            self.marks[k] = {"w": (ctr, cnt), "r": {}}
        for k in reads:
            m = self.marks.setdefault(k, {"w": None, "r": {}})
            m["r"][ctr] = max(m["r"].get(ctr, 0), cnt)
        return ins

    def dma(self, e, reads, writes, f):
        n = NSY if e == "sy" else NGP
        i = self._rr[e]
        self._rr[e] = (i + 1) % n
        cn = (f"da{i}" if e == "sy" else f"db{i}")
        self._wait(e, cn, self.c[cn].n)
        return self.do(e, cn, reads, writes, f, inc=16)

    def fence_all(self, e):
        for cn, ctr in self.c.items():
            self._wait(e, cn, ctr.n)



def _c(a):
    return np.ascontiguousarray(a)


def prep_host(p):
    h = {}
    _ = np.asarray(p["L0_ffn_Wg"])
    x = np.asarray(p["L0_attn_x"])
    h["x_fm"] = {}
    for b in range(4):
        h["x_fm"][(b, 0)] = _c(x[b, :2048].T)
        h["x_fm"][(b, 1)] = _c(x[b, 2048:].T)
    h["wqkvT"] = _c(np.asarray(p["L0_attn_w_qkv"]).T.reshape(8, 128, 3072)
                    .transpose(1, 0, 2))
    h["woT"] = _c(np.asarray(p["L0_attn_w_out"]).T.reshape(8, 128, 1024)
                  .transpose(1, 0, 2))
    om = np.asarray(p["L0_attn_omega"])
    h["omegaT"] = _c(om.transpose(2, 1, 0, 3).reshape(64, 1024))
    h["quads"] = _c(np.stack([np.asarray(p["L0_attn_quad_nodes"]),
                              np.asarray(p["L0_attn_quad_weights"])], 1))
    h["bqkv"] = _c(np.asarray(p["L0_attn_b_qkv"]).reshape(3, 1024))
    h["w1f0"] = _c(np.asarray(p["L0_ffn_w1"]).reshape(8, 128, 32, 128)
                   .transpose(2, 1, 0, 3))
    h["w2f0"] = _c(np.asarray(p["L0_ffn_w2"]).reshape(32, 128, 8, 128)
                   .transpose(2, 1, 0, 3))
    for k in ("tm_w1", "td_w1", "td_w2"):
        h[k] = _c(np.asarray(p["L1_attn_" + k]))
    for k in ("Wr", "Wk", "Wo", "Wr_ffn"):
        h[k] = _c(np.asarray(p["L1_attn_" + k]).reshape(8, 128, 8, 128)
                  .transpose(2, 1, 0, 3))
    for k in ("Wv", "Wg"):
        h[k] = _c(np.asarray(p["L1_attn_" + k]).reshape(8, 128, 1024)
                  .transpose(1, 0, 2))
    h["Wk_ffn"] = _c(np.asarray(p["L1_attn_Wk_ffn"]).reshape(8, 128, 28, 128)
                     .transpose(2, 1, 0, 3))
    h["Wv_ffn"] = _c(np.asarray(p["L1_attn_Wv_ffn"]).reshape(28, 128, 8, 128)
                     .transpose(2, 1, 0, 3))
    h["tm_w2"] = _c(np.asarray(p["L1_attn_tm_w2"]))
    h["rwT"] = _c(np.asarray(p["L1_ffn_router_w"]).T)
    h["m_b1"] = _c(np.asarray(p["L1_ffn_b1"]))
    vecs = []
    for name in PV_ORDER:
        if name == "u_flat":
            vecs.append(np.asarray(p["L1_attn_time_faaaa"]).reshape(1024))
        elif name == "b_out":
            vecs.append(np.asarray(p["L0_attn_b_out"]))
        elif name == "bq":
            vecs.append(np.asarray(p["L0_attn_b_qkv"])[0:1024])
        elif name == "bk":
            vecs.append(np.asarray(p["L0_attn_b_qkv"])[1024:2048])
        elif name.startswith("m_b2_"):
            e = int(name.split("_")[-1])
            vecs.append(np.asarray(p["L1_ffn_b2"])[e])
        elif name == "router_b8":
            rb = np.zeros(1024, np.float32)
            rb[:8] = np.asarray(p["L1_ffn_router_b"])
            vecs.append(rb)
        else:
            vecs.append(np.asarray(p["L1_attn_" + name]))
    h["pvecT"] = _c(np.stack(vecs, axis=1).astype(np.float32))
    h["rbL"] = _c(np.tile(np.asarray(p["L1_ffn_router_b"])[None, :], (128, 1)).astype(np.float32))
    h["m_b1L"] = _c(np.asarray(p["L1_ffn_b1"]).reshape(8, 64, 128).transpose(2, 1, 0).reshape(128, 512))
    h["m_w1T"] = [_c(np.asarray(p["L1_ffn_w1"])[ex].T.reshape(8, 128, 64, 128)
                     .transpose(2, 1, 0, 3)) for ex in range(8)]
    h["m_w2T"] = [_c(np.asarray(p["L1_ffn_w2"])[ex].T.reshape(32, 128, 8, 128)
                     .transpose(2, 1, 0, 3)) for ex in range(8)]
    h["lnxgrow"] = _c(np.tile(np.asarray(p["L1_attn_lnx_g"])[None, :], (128, 1)))
    h["lnxbrow"] = _c(np.tile(np.asarray(p["L1_attn_lnx_b"])[None, :], (128, 1)))
    qn = np.asarray(p["L0_attn_quad_nodes"])
    h["scolh"] = _c(qn[(np.arange(128) // 32) % 2].reshape(128, 1).astype(np.float32))
    return h


L1A_W_SHAPES = {
    "tm_w1": (D, 160), "td_w1": (D, 64), "td_w2": (64, D),
    "Wr": (KD, 128, KD, 128), "Wk": (KD, 128, KD, 128),
    "Wo": (KD, 128, KD, 128), "Wr_ffn": (KD, 128, KD, 128),
    "Wv": (128, KD, 1024), "Wg": (128, KD, 1024),
    "Wk_ffn": (28, 128, KD, 128), "Wv_ffn": (KD, 128, 28, 128),
}

PV_ORDER = [
    "b_out", "bq", "bk",
    "ln0_g", "ln0_b", "ln1_g", "ln1_b", "ln2_g", "ln2_b", "lnx_g", "lnx_b",
    "time_maa_x", "time_maa_w", "time_maa_k", "time_maa_v", "time_maa_r", "time_maa_g",
    "time_decay", "u_flat", "ffn_maa_k", "ffn_maa_r",
    "m_b2_0", "m_b2_1", "m_b2_2", "m_b2_3", "m_b2_4", "m_b2_5", "m_b2_6", "m_b2_7",
    "router_b8",
]
PV = {n: i for i, n in enumerate(PV_ORDER)}
NV = len(PV_ORDER)



def make_consts(nc):
    c = {}
    tri_incl = np.triu(np.ones((L, L), np.float32))
    tri_strict = np.triu(np.ones((L, L), np.float32), 1)
    c["m_incl"] = nc.inline_tensor(tri_incl, "m_incl")
    c["m_strict"] = nc.inline_tensor(tri_strict, "m_strict")
    c["ident_b"] = nc.inline_tensor(np.eye(128, dtype=np.float32), "ident_b")
    c["ident_f"] = nc.inline_tensor(np.eye(128, dtype=np.float32), "ident_f")
    hsel = np.zeros((128, 2), np.float32)
    hsel[0:64, 0] = 1.0
    hsel[64:128, 1] = 1.0
    c["hsel"] = nc.inline_tensor(hsel, "hsel")
    c["hselT"] = nc.inline_tensor(_c(hsel.T), "hselT")
    rsel = np.zeros((2, 128), np.float32)
    for pp in range(128):
        rsel[(pp // 32) % 2, pp] = 1.0
    c["rsel"] = nc.inline_tensor(rsel, "rsel")
    rselw = np.zeros((2, 1024), np.float32)
    for j in range(1024):
        rselw[(j // 32) % 2, j] = 1.0
    c["rselw"] = nc.inline_tensor(rselw, "rselw")
    c["ones_col"] = nc.inline_tensor(np.ones((128, 1), np.float32), "ones_col")
    c["ones_row"] = nc.inline_tensor(np.ones((1, 128), np.float32), "ones_row")
    return c



class MK:

    def __init__(self, launch, dbg=False, gw=False):
        self.launch = launch
        self.dbg = dbg
        self.gw = gw
        nc = bass.Bass("TRN2", target_bir_lowering=False, detect_race_conditions=False)
        self.nc = nc
        self.io = {}
        self.sb = {}
        self.ps = {}
        self.consts = make_consts(nc)

    def tap(self, B, name, key, ap, shape):
        if not self.dbg:
            return
        if isinstance(self.dbg, (set, frozenset)) and name not in self.dbg:
            return
        out = self.nc.dram_tensor("dbg_" + name, list(shape), F32, kind="ExternalOutput").ap()
        B.dma("gp", [key], [], lambda e: e.dma_start(out=out, in_=ap))

    def dram_in(self, name, shape):
        self.io[name] = self.nc.dram_tensor(name, list(shape), F32, kind="ExternalInput").ap()
        return self.io[name]

    def dram_out(self, name, shape):
        self.io[name] = self.nc.dram_tensor(name, list(shape), F32, kind="ExternalOutput").ap()
        return self.io[name]

    def sbuf(self, name, shape, dtype):
        t = self.nc.alloc_sbuf_tensor(name, list(shape), dtype)
        self.sb[name] = t
        return t

    def psum(self, name, shape, dtype=F32):
        t = self.nc.alloc_psum_tensor(name, list(shape), dtype)
        self.ps[name] = t
        return t

    def alloc_common(self, l0a=True):
        s = self.sbuf
        self.has_l0a = l0a
        s("xs", [128, KD, G if self.gw else TL], F32)
        s("xlnb", [128, KD, G], BF16)
        s("sq16", [128, G], BF16)
        s("xb16", [128, G], BF16)
        s("sc0", [128, G], F32)
        s("sc1", [128, G], F32)
        s("row0", [2, 1024], F32)
        if l0a:
            s("qf", [128, KD, G], BF16)
            s("kf", [128, KD, G], BF16)
            s("vtm", [128, NCH, 1024], BF16)
            s("kft", [128, KD, L], BF16)
            s("otm", [128, 1024], BF16)
        if l0a:
            s("S0", [128, 512], F32)
            s("S0b", [128, 512], BF16)
            s("ks", [128, 16], F32)
            s("ksb", [128, 16], BF16)
            s("rden", [128, 16], F32)
            s("denlog", [128, 256], F32)
            s("am", [128, L], BF16)
            s("am2", [128, L], BF16)
        s("row1", [1, 1024], F32)
        s("pv", [128, KD, NV], F32)
        s("quad", [2, 2], F32)
        s("qtmp", [2, 2], F32)
        s("scol", [128, 1], F32)
        s("rolec", [128, 1], F32)
        if l0a:
            s("omA", [128, 1024], BF16)
            s("omB", [128, 1024], BF16)
            s("bvrow", [128, 1024], BF16)
        if l0a:
            s("mI", [128, L], F32)
            s("idf", [128, 128], F32)
            s("hselT_f", [2, 128], F32)
        s("mS", [128, L], F32)
        s("idb", [128, 128], BF16)
        s("hsel_b", [128, 2], BF16)
        s("onecol_b", [128, 1], BF16)
        s("onerow_f", [1, 128], F32)
        s("onerow_b", [1, 128], BF16)
        s("epscol", [128, 1], F32)
        s("negone", [128, 1], F32)
        s("neghalf", [128, 1], F32)
        s("onecol_f", [128, 1], F32)
        if l0a:
            s("wqkv", [128, KD, 3072], BF16)
            if "wbig" not in self.sb:
                s("wbig", [128, KD, 1024], BF16)
        self.psum("P0", [128, 512])
        self.psum("P1", [128, 512])
        self.psum("P2", [128, 512])
        self.psum("P3", [128, 512])
        self.psum("P4", [128, 512])
        self.psum("P5", [128, 512])
        self.psum("P6", [128, 512])
        self.psum("PT", [128, 1024], BF16)

    def emit_setup(self, B, l0a=True):
        nc, sb, c = self.nc, self.sb, self.consts
        if l0a:
            B.dma("sy", [], ["mI"], lambda e: e.dma_start(out=sb["mI"][:], in_=c["m_incl"].ap()))
            B.dma("sy", [], ["idf"], lambda e: e.dma_start(out=sb["idf"][:], in_=c["ident_f"].ap()))
            B.dma("sy", [], ["hselT_f"], lambda e: e.dma_start(out=sb["hselT_f"][:], in_=c["hselT"].ap()))
        B.dma("sy", [], ["mS"], lambda e: e.dma_start(out=sb["mS"][:], in_=c["m_strict"].ap()))
        B.dma("gp", [], ["idb"], lambda e: e.dma_start(out=sb["idb"][:], in_=c["ident_b"].ap()))
        B.dma("gp", [], ["hsel_b"], lambda e: e.dma_start(out=sb["hsel_b"][:], in_=c["hsel"].ap()))
        B.dma("gp", [], ["onecol_b"], lambda e: e.dma_start(out=sb["onecol_b"][:], in_=c["ones_col"].ap()))
        B.dma("sy", [], ["onerow_f"], lambda e: e.dma_start(out=sb["onerow_f"][:], in_=c["ones_row"].ap()))
        B.dma("gp", [], ["onerow_b"], lambda e: e.dma_start(out=sb["onerow_b"][:], in_=c["ones_row"].ap()))
        B.do("ve", "dv", [], ["epscol"], lambda e: e.memset(sb["epscol"][:, :], 1e-5))
        B.do("ve", "dv", [], ["negone"], lambda e: e.memset(sb["negone"][:, :], -1.0))
        B.do("ve", "dv", [], ["neghalf"], lambda e: e.memset(sb["neghalf"][:, :], -0.5))
        B.do("ve", "dv", [], ["onecol_f"], lambda e: e.memset(sb["onecol_f"][:, :], 1.0))
        B.dma("sy", [], ["pv"],
             lambda e: e.dma_start(out=sb["pv"][:],
                                   in_=self.io["pvecT"].rearrange("(k p) v -> p k v", p=128)))
        B.dma("sy", [], ["quad"], lambda e: e.dma_start(out=sb["quad"][:], in_=self.io["quads"]))
        B.dma("sy", [], ["rolec"], lambda e: e.dma_start(out=sb["rolec"][:], in_=self.io["rolecol"]))
        B.dma("sy", [], ["scol"], lambda e: e.dma_start(out=sb["scol"][:], in_=self.io["scolh"]))
        if not l0a:
            return
        self.emit_build_omegas(B)
        self.pdm2_gate(B, "sy")
        B.dma("sy", [], ["wqkv"],
             lambda e: e.dma_start(out=sb["wqkv"][:], in_=self.io["scqkv"]))
        B.dma("sy", [], ["wbig"],
             lambda e: e.dma_start(out=sb["wbig"][:], in_=self.io["scwo"]))
        B.dma("sy", [], ["row1"],
             lambda e: e.dma_start(out=sb["row1"][0:1, :], in_=self.io["bqkv"][2:3, :]))
        for half, pn in ((0, "P4"), (1, "P5")):
            B.do("te", "pe", ["row1", "onerow_f"], ["p" + pn],
                 lambda e, half=half, pn=pn: e.matmul(
                     self.ps[pn][:, 0:512], sb["onerow_f"][0:1, :],
                     sb["row1"][0:1, half * 512:(half + 1) * 512], start=True, stop=True))
            B.do("ve", "dv", ["p" + pn], ["bvrow"],
                 lambda e, half=half, pn=pn: e.tensor_copy(
                     sb["bvrow"][:, half * 512:(half + 1) * 512], self.ps[pn][:, 0:512]))

    def emit_build_omegas(self, B):
        nc, sb, c = self.nc, self.sb, self.consts
        B.do("ve", "dv", ["quad"], ["qtmp"],
             lambda e: e.tensor_scalar(sb["qtmp"][:, 0:1], sb["quad"][:, 0:1], 0.0, 2.0, ALU.max, ALU.mult))
        B.do("se", "ac", ["qtmp"], ["qtmp"],
             lambda e: e.activation(sb["qtmp"][:, 0:1], sb["qtmp"][:, 0:1], AF.Sqrt))
        B.do("ve", "dv", ["quad"], ["qtmp"],
             lambda e: e.tensor_scalar(sb["qtmp"][:, 1:2], sb["quad"][:, 1:2], 0.0, None, ALU.max))
        B.do("se", "ac", ["qtmp"], ["qtmp"],
             lambda e: e.activation(sb["qtmp"][:, 1:2], sb["qtmp"][:, 1:2], AF.Sqrt))
        B.do("se", "ac", ["qtmp"], ["qtmp"],
             lambda e: e.activation(sb["qtmp"][:, 1:2], sb["qtmp"][:, 1:2], AF.Sqrt))
        B.do("ve", "dv", ["qtmp"], ["qtmp"],
             lambda e: e.tensor_scalar(sb["qtmp"][:, 1:2], sb["qtmp"][:, 1:2], INV_SQRT32, None, ALU.mult))
        B.dma("sy", [], ["sc0"],
             lambda e: e.dma_start(out=sb["sc0"][0:2, 0:512], in_=c["rselw"].ap()[:, 0:512]))
        B.dma("sy", [], ["sc1"],
             lambda e: e.dma_start(out=sb["sc1"][0:2, 0:512], in_=c["rselw"].ap()[:, 512:1024]))
        for nm, col in (("row0", 0), ("row1", 1)):
            for half in range(2):
                B.do("te", "pe", ["sc0", "sc1", "qtmp"], ["pP3"],
                     lambda e, col=col, half=half: e.matmul(
                         self.ps["P3"][0:1, 0:512],
                         sb["qtmp"][:, col:col + 1],
                         sb["sc0" if half == 0 else "sc1"][0:2, 0:512],
                         start=True, stop=True))
                B.do("ve", "dv", ["pP3"], [nm],
                     lambda e, nm=nm, half=half: e.tensor_copy(
                         sb[nm][0:1, half * 512:(half + 1) * 512], self.ps["P3"][0:1, 0:512]))
        for base in (0, 64):
            B.dma("sy", [], ["sc0"],
                 lambda e, base=base: e.dma_start(out=sb["sc0"][base:base + 64, :],
                                                  in_=self.io["omegaT"][:, 0:512]))
            B.dma("sy", [], ["sc1"],
                 lambda e, base=base: e.dma_start(out=sb["sc1"][base:base + 64, :],
                                                  in_=self.io["omegaT"][:, 512:1024]))
        for nm, rownm in (("omA", "row0"), ("omB", "row1")):
            for half, pn in ((0, "P4"), (1, "P5")):
                B.do("te", "pe", [rownm, "onerow_f"], ["p" + pn],
                     lambda e, half=half, pn=pn, rownm=rownm: e.matmul(
                         self.ps[pn][:, 0:512], sb["onerow_f"][0:1, :],
                         sb[rownm][0:1, half * 512:(half + 1) * 512], start=True, stop=True))
                B.do("ve", "dv", ["p" + pn, "sc0", "sc1"], [nm],
                     lambda e, half=half, pn=pn, nm=nm: e.tensor_mul(
                         sb[nm][:, half * 512:(half + 1) * 512],
                         sb["sc0" if half == 0 else "sc1"][:, 0:512],
                         self.ps[pn][:, 0:512]))
        self.tap(B, "scol", "scol", self.sb["scol"][:], [128, 1])
        self.tap(B, "omAv", "omA", self.sb["omA"][:], [128, 1024])
        self.tap(B, "omBv", "omB", self.sb["omB"][:], [128, 1024])
        self.tap(B, "carow", "row0", self.sb["row0"][0:1, :], [1, 1024])
        self.tap(B, "cbrow", "row1", self.sb["row1"][0:1, :], [1, 1024])

    def emit_load_stream(self, B, src_name):
        sb = self.sb
        B.dma("sy", [], ["xs"],
             lambda e: e.dma_start(out=sb["xs"][:],
                                   in_=self.io[src_name].rearrange("(k p) t -> p k t", p=128)))

    def emit_store_stream(self, B, dst_name):
        sb = self.sb
        B.dma("sy", ["xs"], [],
             lambda e: e.dma_start(out=self.io[dst_name].rearrange("(k p) t -> p k t", p=128),
                                   in_=sb["xs"][:]))

    def emit_ln(self, B, g0, gname="", dst="xlnb", eps_key="epscol", src=None,
                gcol=None, bcol=None):
        sb, ps = self.sb, self.ps
        xs = sb["xs"]
        if src is None:
            src_ap = lambda kd: xs[:, kd, g0:g0 + G]
            src_key = "xs"
        else:
            src_ap = lambda kd: sb[src][:, kd, :]
            src_key = src
        for kd in range(KD):
            if src is None:
                B.do("se", "ac", [src_key], ["xb16"],
                     lambda e, kd=kd: e.activation(sb["xb16"][:, :], src_ap(kd), AF.Copy))
                sum_src, sum_key = sb["xb16"], "xb16"
            else:
                sum_src, sum_key = None, src_key
            B.do("se", "ac", [src_key], ["sq16"],
                 lambda e, kd=kd: e.activation(sb["sq16"][:, :], src_ap(kd), AF.Square))
            B.do("te", "pe", [sum_key, "onecol_b"], ["pP2"],
                 lambda e, kd=kd: e.matmul(ps["P2"][0:1, 0:G], sb["onecol_b"][:, :],
                                           sum_src[:, :] if sum_src is not None else src_ap(kd),
                                           start=(kd == 0), stop=(kd == KD - 1)))
            B.do("te", "pe", ["sq16", "onecol_b"], ["pP3"],
                 lambda e, kd=kd: e.matmul(ps["P3"][0:1, 0:G], sb["onecol_b"][:, :], sb["sq16"][:, :],
                                           start=(kd == 0), stop=(kd == KD - 1)))
        r, r1 = sb["row0"], sb["row1"]
        B.do("ve", "dv", ["pP2"], ["row0"],
             lambda e: e.tensor_scalar(r[0:1, 0:G], ps["P2"][0:1, 0:G], 1.0 / D, None, ALU.mult))
        B.do("ve", "dv", ["pP3"], ["row0"],
             lambda e: e.tensor_scalar(r[0:1, G:2 * G], ps["P3"][0:1, 0:G], 1.0 / D, None, ALU.mult))
        B.do("se", "ac", ["row0"], ["row1"],
             lambda e: e.activation(r1[0:1, 0:G], r[0:1, 0:G], AF.Square))
        B.do("ve", "dv", ["row0", "row1"], ["row1"],
             lambda e: e.tensor_sub(r1[0:1, G:2 * G], r[0:1, G:2 * G], r1[0:1, 0:G]))
        B.do("se", "ac", ["row1", "epscol"], ["row1"],
             lambda e: e.activation(r1[0:1, G:2 * G], r1[0:1, G:2 * G], AF.Ln,
                                    bias=sb[eps_key][0:1, 0:1]))
        B.do("se", "ac", ["row1", "neghalf"], ["row1"],
             lambda e: e.activation(r1[0:1, G:2 * G], r1[0:1, G:2 * G], AF.Exp,
                                    scale=sb["neghalf"][0:1, 0:1]))
        B.do("ve", "dv", ["row1", "row0"], ["row0"],
             lambda e: e.tensor_mul(r[0:1, 0:G], r[0:1, 0:G], r1[0:1, G:2 * G]))
        B.do("te", "pe", ["row1", "onerow_f"], ["pP4"],
             lambda e: e.matmul(ps["P4"][:, 0:G], sb["onerow_f"][0:1, :], r1[0:1, G:2 * G],
                                start=True, stop=True))
        B.do("te", "pe", ["row0", "onerow_f"], ["pP5"],
             lambda e: e.matmul(ps["P5"][:, 0:G], sb["onerow_f"][0:1, :], r[0:1, 0:G],
                                start=True, stop=True))
        for kd in range(KD):
            B.do("ve", "dv", [src_key, "pP4"], ["sc0"],
                 lambda e, kd=kd: e.tensor_mul(sb["sc0"][:, :], src_ap(kd), ps["P4"][:, 0:G]))
            if gcol is None:
                B.do("ve", "dv", ["sc0", "pP5"], [dst],
                     lambda e, kd=kd: e.tensor_sub(sb[dst][:, kd, :], sb["sc0"][:, :], ps["P5"][:, 0:G]))
            else:
                B.do("ve", "dv", ["sc0", "pP5"], ["sc1"],
                     lambda e, kd=kd: e.tensor_sub(sb["sc1"][:, :], sb["sc0"][:, :], ps["P5"][:, 0:G]))
                B.do("ve", "dv", ["sc1", "pv"], [dst],
                     lambda e, kd=kd: e.tensor_scalar(
                         sb[dst][:, kd, :], sb["sc1"][:, :],
                         sb["pv"][:, kd, PV[gcol]:PV[gcol] + 1],
                         sb["pv"][:, kd, PV[bcol]:PV[bcol] + 1], ALU.mult, ALU.add))

    def emit_l0a(self, B, zero_state, mask_state_by_role, add_bias_cols=True,
                 src_name=None, dst_name=None):
        self._gw_io0 = (src_name, dst_name)
        """Full L0 attention block: xs <- xs + attn(LN(xs)). State in S0/ks.
        zero_state: memset states first. mask_state_by_role: multiply states by role col."""
        sb, ps = self.sb, self.ps
        if zero_state:
            B.do("ve", "dv", [], ["S0"], lambda e: e.memset(sb["S0"][:, :], 0.0))
            B.do("ve", "dv", [], ["S0b"], lambda e: e.memset(sb["S0b"][:, :], 0.0))
            B.do("ve", "dv", [], ["ks"], lambda e: e.memset(sb["ks"][:, :], 0.0))
            B.do("ve", "dv", [], ["ksb"], lambda e: e.memset(sb["ksb"][:, :], 0.0))
        if mask_state_by_role:
            B.do("ve", "dv", ["rolec", "S0"], ["S0"],
                 lambda e: e.tensor_scalar(sb["S0"][:, :], sb["S0"][:, :], sb["rolec"][:, 0:1], None, ALU.mult))
            B.do("ve", "dv", ["S0"], ["S0b"], lambda e: e.tensor_copy(sb["S0b"][:, :], sb["S0"][:, :]))
            B.do("ve", "dv", ["rolec", "ks"], ["ks"],
                 lambda e: e.tensor_scalar(sb["ks"][:, :], sb["ks"][:, :], sb["rolec"][:, 0:1], None, ALU.mult))
            B.do("se", "ac", ["ks"], ["ksb"], lambda e: e.copy(sb["ksb"][:, :], sb["ks"][:, :]))

        for g in range(NGRP):
            g0 = 0 if self.gw else g * G
            if self.gw:
                sn, dn = self._gw_io0
                B.dma("sy", [], ["xs"],
                      lambda e, g=g, sn=sn: e.dma_start(
                          out=sb["xs"][:],
                          in_=self.io[sn].rearrange("(k p) t -> p k t", p=128)[:, :, g * G:(g + 1) * G]))
            self.emit_ln(B, g0)
            self._l0a_group(B, g0, add_bias_cols)
            if self.gw and dn is not None:
                B.dma("sy", ["xs"], [],
                      lambda e, g=g, dn=dn: e.dma_start(
                          out=self.io[dn].rearrange("(k p) t -> p k t", p=128)[:, :, g * G:(g + 1) * G],
                          in_=sb["xs"][:]))

    def _l0a_group(self, B, g0, add_bias_cols):
        sb, ps = self.sb, self.ps

        for qk in range(2):
            dstf = "qf" if qk == 0 else "kf"
            for mt in range(KD):
                pn = "P0" if mt % 2 == 0 else "P1"
                wcol0 = qk * 1024 + mt * 128
                B.do("te", "pe", ["wqkv", "xlnb"], ["p" + pn],
                     lambda e, pn=pn, wcol0=wcol0: [
                         e.matmul(ps[pn][:, 0:G], sb["wqkv"][:, kd, wcol0:wcol0 + 128],
                                  sb["xlnb"][:, kd, :], start=(kd == 0), stop=(kd == KD - 1))
                         for kd in range(KD)][-1])
                bcol = PV["bq"] if qk == 0 else PV["bk"]
                B.do("ve", "dv", ["p" + pn, "pv"], ["sc0"],
                     lambda e, pn=pn, mt=mt, bcol=bcol: e.tensor_scalar(
                         sb["sc0"][:, :], ps[pn][:, 0:G],
                         sb["pv"][:, mt, bcol:bcol + 1], None, ALU.add))
                B.do("se", "ac", ["sc0"], ["sq16"],
                     lambda e: e.activation(sb["sq16"][:, :], sb["sc0"][:, :], AF.Square))
                B.do("te", "pe", ["sq16", "hsel_b"], ["pP2"],
                     lambda e: e.matmul(ps["P2"][0:2, 0:G], sb["hsel_b"][:, :], sb["sq16"][:, :],
                                        start=True, stop=True))
                B.do("ve", "dv", ["pP2"], ["row0"],
                     lambda e: e.tensor_scalar(sb["row0"][0:2, 0:G], ps["P2"][0:2, 0:G],
                                               1e-24, None, ALU.max))
                B.do("se", "ac", ["row0"], ["row0"],
                     lambda e: e.activation(sb["row0"][0:2, 0:G], sb["row0"][0:2, 0:G], AF.Ln))
                B.do("se", "ac", ["row0", "neghalf"], ["row0"],
                     lambda e: e.activation(sb["row0"][0:2, 0:G], sb["row0"][0:2, 0:G], AF.Exp,
                                            scale=sb["neghalf"][0:2, 0:1]))
                B.do("te", "pe", ["row0", "hselT_f"], ["pP3"],
                     lambda e: e.matmul(ps["P3"][:, 0:G], sb["hselT_f"][:, :], sb["row0"][0:2, 0:G],
                                        start=True, stop=True))
                B.do("ve", "dv", ["sc0", "pP3"], ["xb16"],
                     lambda e: e.tensor_mul(sb["xb16"][:, :], sb["sc0"][:, :], ps["P3"][:, 0:G]))
                for hh in range(2):
                    h = 2 * mt + hh
                    b0 = hh * 64
                    B.do("te", "pe", ["xb16", "omA"], ["pP4"],
                         lambda e, b0=b0, h=h: e.matmul(
                             ps["P4"][b0:b0 + 64, 0:G], sb["omA"][b0:b0 + 64, h * 64:(h + 1) * 64],
                             sb["xb16"][b0:b0 + 64, :], start=True, stop=True))
                    B.do("te", "pe", ["xb16", "omB"], ["pP5"],
                         lambda e, b0=b0, h=h: e.matmul(
                             ps["P5"][b0:b0 + 64, 0:G], sb["omB"][b0:b0 + 64, h * 64:(h + 1) * 64],
                             sb["xb16"][b0:b0 + 64, :], start=True, stop=True))
                B.do("ve", "dv", ["pP4", "scol"], ["sc1"],
                     lambda e: e.tensor_scalar(sb["sc1"][:, :], ps["P4"][:, 0:G],
                                               sb["scol"][:, 0:1], None, ALU.subtract))
                B.do("ve", "dv", ["sc1"], ["sc1"],
                     lambda e: e.tensor_scalar(sb["sc1"][:, :], sb["sc1"][:, :], -20.0, 20.0,
                                               ALU.max, ALU.min))
                B.do("se", "ac", ["sc1"], ["sc1"],
                     lambda e: e.activation(sb["sc1"][:, :], sb["sc1"][:, :], AF.Exp))
                B.do("se", "ac", ["pP5"], ["sq16"],
                     lambda e: e.activation(sb["sq16"][:, :], ps["P5"][:, 0:G], AF.Square))
                B.do("ve", "dv", ["sc1", "sq16"], [dstf],
                     lambda e, dstf=dstf, mt=mt: e.tensor_mul(sb[dstf][:, mt, :], sb["sc1"][:, :],
                                                              sb["sq16"][:, :]))
                if g0 == 0 and qk == 0 and mt == 0:
                    self.tap(B, "q0", "sc0", sb["sc0"][:], [128, G])
                    self.tap(B, "rn0", "row0", sb["row0"][0:2, 0:G], [2, G])
                    self.tap(B, "zn0", "xb16", sb["xb16"][:], [128, G])
                    self.tap(B, "e0", "sc1", sb["sc1"][:], [128, G])
                    self.tap(B, "sqB0", "sq16", sb["sq16"][:], [128, G])

        if g0 == 0:
            self.tap(B, "xlnb", "xlnb", sb["xlnb"][:], [128, KD, G])
            self.tap(B, "qf", "qf", sb["qf"][:], [128, KD, G])
            self.tap(B, "kf", "kf", sb["kf"][:], [128, KD, G])

        for tc in range(NCH):
            for nb in range(2):
                pn = "P0" if (tc * 2 + nb) % 2 == 0 else "P1"
                B.do("te", "pe", ["xlnb", "wqkv"], ["p" + pn],
                     lambda e, pn=pn, tc=tc, nb=nb: [
                         e.matmul(ps[pn][:, 0:512],
                                  sb["xlnb"][:, kd, tc * L:(tc + 1) * L],
                                  sb["wqkv"][:, kd, 2048 + nb * 512:2048 + (nb + 1) * 512],
                                  start=(kd == 0), stop=(kd == KD - 1))
                         for kd in range(KD)][-1])
                B.do("ve", "dv", ["p" + pn, "bvrow"], ["vtm"],
                     lambda e, pn=pn, tc=tc, nb=nb: e.tensor_add(
                         sb["vtm"][:, tc, nb * 512:(nb + 1) * 512],
                         ps[pn][:, 0:512], sb["bvrow"][:, nb * 512:(nb + 1) * 512]))

        if g0 == 0:
            self.tap(B, "vtm", "vtm", sb["vtm"][:], [128, NCH, 1024])

        for tc in range(NCH):
            t0 = tc * L
            for kd in range(KD):
                B.do("te", "pe", ["kf", "idb"], ["pPT"],
                     lambda e, kd=kd: e.transpose(ps["PT"][:, kd * 128:(kd + 1) * 128],
                                                  sb["kf"][:, kd, t0:t0 + L], sb["idb"][:, :]))
                B.do("ve", "dv", ["pPT"], ["kft"],
                     lambda e, kd=kd: e.tensor_copy(sb["kft"][:, kd, :],
                                                    ps["PT"][:, kd * 128:(kd + 1) * 128]))
            for h in range(NH):
                kd, b0 = h // 2, (h % 2) * 64
                pa = "P4" if h % 2 == 0 else "P5"
                an = "am" if h % 2 == 0 else "am2"
                B.do("te", "pe", ["kf", "qf"], ["p" + pa],
                     lambda e, pa=pa, kd=kd, b0=b0: e.matmul(
                         ps[pa][:, 0:L], sb["kf"][b0:b0 + 64, kd, t0:t0 + L],
                         sb["qf"][b0:b0 + 64, kd, t0:t0 + L], start=True, stop=True))
                B.do("ve", "dv", ["p" + pa, "mI"], [an],
                     lambda e, pa=pa, an=an: e.tensor_mul(sb[an][:, :], ps[pa][:, 0:L],
                                                          sb["mI"][:, :]))
                if g0 == 0 and tc == 0 and h == 0:
                    self.tap(B, "am0", "am", sb["am"][:], [128, L])
                po = "P0" if h < 8 else "P1"
                oc0 = (h % 8) * 64
                B.do("te", "pe", [an, "vtm"], ["p" + po],
                     lambda e, po=po, oc0=oc0, kd=kd, h=h, an=an: e.matmul(
                         ps[po][:, oc0:oc0 + 64], sb[an][:, :],
                         sb["vtm"][:, tc, h * 64:(h + 1) * 64], start=True, stop=False))
                B.do("te", "pe", ["qf", "S0b"], ["p" + po],
                     lambda e, po=po, oc0=oc0, kd=kd, b0=b0, h=h: e.matmul(
                         ps[po][:, oc0:oc0 + 64], sb["qf"][b0:b0 + 64, kd, t0:t0 + L],
                         sb["S0b"][b0:b0 + 64, (h // 2) * 64:(h // 2 + 1) * 64],
                         start=False, stop=True))
                B.do("te", "pe", [an, "onecol_b"], ["pP2"],
                     lambda e, h=h, an=an: e.matmul(ps["P2"][:, h:h + 1], sb[an][:, :],
                                                    sb["onecol_b"][:, :], start=True, stop=False))
                B.do("te", "pe", ["qf", "ksb"], ["pP2"],
                     lambda e, h=h, kd=kd, b0=b0: e.matmul(
                         ps["P2"][:, h:h + 1], sb["qf"][b0:b0 + 64, kd, t0:t0 + L],
                         sb["ksb"][b0:b0 + 64, h // 2:h // 2 + 1], start=False, stop=True))
                B.do("te", "pe", ["kft", "vtm"], ["pP6"],
                     lambda e, h=h, kd=kd, b0=b0: e.matmul(
                         ps["P6"][b0:b0 + 64, (h // 2) * 64:(h // 2 + 1) * 64],
                         sb["kft"][:, kd, b0:b0 + 64],
                         sb["vtm"][:, tc, h * 64:(h + 1) * 64], start=True, stop=True))
                B.do("te", "pe", ["kft", "onecol_b"], ["pP3"],
                     lambda e, h=h, kd=kd, b0=b0: e.matmul(
                         ps["P3"][b0:b0 + 64, h // 2:h // 2 + 1],
                         sb["kft"][:, kd, b0:b0 + 64],
                         sb["onecol_b"][:, :], start=True, stop=True))
            if g0 == 0 and tc == 0:
                B.do("ve", "dv", ["pP2"], ["sc0"],
                     lambda e: e.tensor_copy(sb["sc0"][:, 0:16], ps["P2"][:, 0:16]))
                self.tap(B, "den0", "sc0", sb["sc0"][:, 0:16], [128, 16])
            B.do("ve", "dv", ["pP2"], ["sc1"],
                 lambda e: e.tensor_scalar(sb["sc1"][:, 0:16], ps["P2"][:, 0:16], 1e-6, None, ALU.max))
            if g0 == 0 and tc == 0:
                self.tap(B, "denm0", "sc1", sb["sc1"][:, 0:16], [128, 16])
            B.do("se", "ac", ["sc1"], ["rden"],
                 lambda e: e.activation(sb["rden"][:, :], sb["sc1"][:, 0:16], AF.Ln))
            B.do("se", "ac", ["rden", "negone"], ["rden"],
                 lambda e: e.activation(sb["rden"][:, :], sb["rden"][:, :], AF.Exp,
                                        scale=sb["negone"][:, 0:1]))
            for h in range(NH):
                po = "P0" if h < 8 else "P1"
                oc0 = (h % 8) * 64
                B.do("ve", "dv", ["p" + po, "rden"], ["otm"],
                     lambda e, po=po, oc0=oc0, h=h: e.tensor_scalar(
                         sb["otm"][:, h * 64:(h + 1) * 64], ps[po][:, oc0:oc0 + 64],
                         sb["rden"][:, h:h + 1], None, ALU.mult))
            if g0 == 0 and tc == 0:
                B.do("ve", "dv", ["pP3"], ["sc0"],
                     lambda e: e.tensor_copy(sb["sc0"][:, 0:16], ps["P3"][:, 0:16]))
                self.tap(B, "p3c0", "sc0", sb["sc0"][:, 0:16], [128, 16])
            B.do("ve", "dv", ["pP6", "S0"], ["S0"],
                 lambda e: e.tensor_add(sb["S0"][:, :], sb["S0"][:, :], ps["P6"][:, 0:512]))
            B.do("ve", "dv", ["S0"], ["S0b"], lambda e: e.tensor_copy(sb["S0b"][:, :], sb["S0"][:, :]))
            B.do("ve", "dv", ["pP3", "ks"], ["ks"],
                 lambda e: e.tensor_add(sb["ks"][:, :], sb["ks"][:, :], ps["P3"][:, 0:16]))
            B.do("se", "ac", ["ks"], ["ksb"], lambda e: e.copy(sb["ksb"][:, :], sb["ks"][:, :]))
            if g0 == 0 and tc == 0:
                self.tap(B, "ks0post", "ks", sb["ks"][:], [128, 16])
                self.tap(B, "ksb0post", "ksb", sb["ksb"][:], [128, 16])
            if g0 == 0 and tc == 0:
                self.tap(B, "otm0", "otm", sb["otm"][:], [128, 1024])
                self.tap(B, "rden0", "rden", sb["rden"][:], [128, 16])
                self.tap(B, "S0c0", "S0", sb["S0"][:], [128, 512])
            if g0 == 0 and tc == 3:
                self.tap(B, "otm3", "otm", sb["otm"][:], [128, 1024])
                self.tap(B, "rden3", "rden", sb["rden"][:], [128, 16])
            if g0 == 0 and tc == 1:
                self.tap(B, "den1", "sc1", sb["sc1"][:, 0:16], [128, 16])
                self.tap(B, "otm1", "otm", sb["otm"][:], [128, 1024])
                self.tap(B, "ks1", "ks", sb["ks"][:], [128, 16])
                self.tap(B, "S0c1", "S0", sb["S0"][:], [128, 512])
            for kd in range(KD):
                B.do("te", "pe", ["otm", "idb"], ["pPT"],
                     lambda e, kd=kd: e.transpose(ps["PT"][:, kd * 128:(kd + 1) * 128],
                                                  sb["otm"][:, kd * 128:(kd + 1) * 128], sb["idb"][:, :]))
                B.do("ve", "dv", ["pPT"], ["xlnb"],
                     lambda e, kd=kd: e.tensor_copy(sb["xlnb"][:, kd, t0:t0 + L],
                                                    ps["PT"][:, kd * 128:(kd + 1) * 128]))

        if g0 == 0:
            self.tap(B, "afm0", "xlnb", sb["xlnb"][:], [128, KD, G])
        if g0 == (NGRP - 1) * G:
            self.tap(B, "denlog", "denlog", sb["denlog"][:], [128, 256])
        for mt in range(KD):
            pn = "P0" if mt % 2 == 0 else "P1"
            B.do("te", "pe", ["wbig", "xlnb"], ["p" + pn],
                 lambda e, pn=pn, mt=mt: [
                     e.matmul(ps[pn][:, 0:G], sb["wbig"][:, kd, mt * 128:(mt + 1) * 128],
                              sb["xlnb"][:, kd, :], start=(kd == 0), stop=(kd == KD - 1))
                     for kd in range(KD)][-1])
            B.do("ve", "dv", ["p" + pn, "pv", "xs"], ["xs"],
                 lambda e, pn=pn, mt=mt: e.scalar_tensor_tensor(
                     sb["xs"][:, mt, g0:g0 + G], ps[pn][:, 0:G],
                     sb["pv"][:, mt, PV["b_out"]:PV["b_out"] + 1],
                     sb["xs"][:, mt, g0:g0 + G], ALU.add, ALU.add))


    def alloc_predump(self, moe=True, l0f=True, l1a=False):
        nc = self.nc
        if l0f:
            self.io["scf1"] = nc.dram_tensor("scf1", [32, 128, KD, 128], BF16,
                                             kind="Internal").ap()
            self.io["scf2"] = nc.dram_tensor("scf2", [KD, 128, 32, 128], BF16,
                                             kind="Internal").ap()
        if moe:
            self.io["scw1"] = nc.dram_tensor("scw1", [NE, 64, 128, KD, 128], BF16,
                                             kind="Internal").ap()
            self.io["scw2"] = nc.dram_tensor("scw2", [NE, KD, 128, 32, 128], BF16,
                                             kind="Internal").ap()
        if l1a:
            for k in ("Wr", "Wk", "Wo", "Wr_ffn"):
                self.io["sc" + k] = nc.dram_tensor("sc" + k, [KD, 128, KD, 128],
                                                   BF16, kind="Internal").ap()
            for k in ("Wv", "Wg"):
                self.io["sc" + k] = nc.dram_tensor("sc" + k, [128, KD, 1024],
                                                   BF16, kind="Internal").ap()
            self.io["scWk_ffn"] = nc.dram_tensor("scWk_ffn", [28, 128, KD, 128],
                                                 BF16, kind="Internal").ap()
            self.io["scWv_ffn"] = nc.dram_tensor("scWv_ffn", [KD, 128, 28, 128],
                                                 BF16, kind="Internal").ap()
            self.io["sctm2"] = nc.dram_tensor("sctm2", [5, 32, 1024], BF16,
                                              kind="Internal").ap()
        self.pdm_sem = nc.alloc_semaphore("pdm")
        self.pdm_total = 0
        self.pdm_stage = {}
        self._pdm_waited = {}

    def alloc_predump_l0a(self):
        nc = self.nc
        self.io["scqkv"] = nc.dram_tensor("scqkv", [128, KD, 3 * D], BF16,
                                          kind="Internal").ap()
        self.io["scwo"] = nc.dram_tensor("scwo", [128, KD, D], BF16,
                                         kind="Internal").ap()
        self.pdm2_sem = nc.alloc_semaphore("pdm2")
        self.pdm2_total = 0
        self._pdm2_waited = False

    def emit_predump_l0a(self):
        nc, io = self.nc, self.io
        nc.gpsimd.dma_start(out=io["scqkv"], in_=io["wqkvT"]).then_inc(self.pdm2_sem, 16)
        nc.gpsimd.dma_start(out=io["scwo"], in_=io["woT"]).then_inc(self.pdm2_sem, 16)
        self.pdm2_total = 32

    def pdm2_gate(self, B, e):
        if getattr(self, "pdm2_total", 0) and not self._pdm2_waited:
            self._pdm2_waited = True
            B.eng(e).wait_ge(self.pdm2_sem, self.pdm2_total)

    def emit_predump(self, moe=True, l0f=True, l1a=False):
        nc, io = self.nc, self.io
        n = self.pdm_total // 16
        if l0f:
            for s in range(32):
                nc.gpsimd.dma_start(out=io["scf1"][s], in_=io["w1f0"][s]) \
                    .then_inc(self.pdm_sem, 16)
                n += 1
            for mt in range(KD):
                nc.gpsimd.dma_start(out=io["scf2"][mt], in_=io["w2f0"][mt]) \
                    .then_inc(self.pdm_sem, 16)
                n += 1
        if l1a:
            for k in ("Wr", "Wk", "Wo", "Wr_ffn"):
                for mt in range(KD):
                    nc.gpsimd.dma_start(out=io["sc" + k][mt], in_=io[k][mt]) \
                        .then_inc(self.pdm_sem, 16)
                    n += 1
            for k in ("Wv", "Wg"):
                nc.gpsimd.dma_start(out=io["sc" + k], in_=io[k]) \
                    .then_inc(self.pdm_sem, 16)
                n += 1
            for s in range(28):
                nc.gpsimd.dma_start(out=io["scWk_ffn"][s], in_=io["Wk_ffn"][s]) \
                    .then_inc(self.pdm_sem, 16)
                n += 1
            for mt in range(KD):
                nc.gpsimd.dma_start(out=io["scWv_ffn"][mt], in_=io["Wv_ffn"][mt]) \
                    .then_inc(self.pdm_sem, 16)
                n += 1
            for f in range(5):
                nc.gpsimd.dma_start(out=io["sctm2"][f], in_=io["tm_w2"][f]) \
                    .then_inc(self.pdm_sem, 16)
                n += 1
        if moe:
            for ex in range(NE):
                for s in range(64):
                    nc.gpsimd.dma_start(out=io["scw1"][ex, s], in_=io["m_w1T"][ex][s]) \
                        .then_inc(self.pdm_sem, 16)
                    n += 1
                for mt in range(KD):
                    nc.gpsimd.dma_start(out=io["scw2"][ex, mt], in_=io["m_w2T"][ex][mt]) \
                        .then_inc(self.pdm_sem, 16)
                    n += 1
        self.pdm_total = n * 16
        self.pdm_stage["moe" if moe else "base"] = self.pdm_total
        if not moe:
            self.pdm_stage.setdefault("base", self.pdm_total)

    def pdm_gate(self, B, e, stage="base"):
        tot = self.pdm_stage.get(stage, getattr(self, "pdm_total", 0))
        if tot and self._pdm_waited.get(e, -1) < tot:
            self._pdm_waited[e] = tot
            B.eng(e).wait_ge(self.pdm_sem, tot)

    W1RING = ("w1a", "w1b", "w1c", "w1d", "w1e", "w1f")

    def ensure_w1_ring(self):
        for nm in MK.W1RING[2:]:
            if nm not in self.sb:
                self.sbuf(nm, [128, KD, 128], BF16)

    def alloc_l0f(self):
        s = self.sbuf
        if "hbuf" not in self.sb:
            s("hbuf", [128, 32, G], BF16)
        s("w1a", [128, KD, 128], BF16)
        s("w1b", [128, KD, 128], BF16)
        s("w2a", [128, 32, 128], BF16)
        self.ensure_w1_ring()

    def emit_l0f(self, B, src_name=None, dst_name=None):
        sb, ps = self.sb, self.ps
        self.pdm_gate(B, "sy")
        for g in range(NGRP):
            g0 = 0 if self.gw else g * G
            if self.gw:
                B.dma("sy", [], ["xs"],
                      lambda e, g=g: e.dma_start(
                          out=sb["xs"][:],
                          in_=self.io[src_name].rearrange("(k p) t -> p k t", p=128)[:, :, g * G:(g + 1) * G]))
            self.emit_ln(B, g0)
            for j in range(32):
                wt = MK.W1RING[j % 6]
                B.dma("sy", [], [wt],
                      lambda e, wt=wt, j=j: e.dma_start(
                          out=sb[wt][:], in_=self.io["scf1"][j]))
                pn = ("P0", "P1", "P2", "P3")[j % 4]
                B.do("te", "pe", [wt, "xlnb"], ["p" + pn],
                     lambda e, pn=pn, wt=wt: [
                         e.matmul(ps[pn][:, 0:G], sb[wt][:, kd, :], sb["xlnb"][:, kd, :],
                                  start=(kd == 0), stop=(kd == KD - 1))
                         for kd in range(KD)][-1])
                B.do("se", "ac", ["p" + pn], ["hbuf"],
                     lambda e, pn=pn, j=j: e.activation(sb["hbuf"][:, j, :], ps[pn][:, 0:G], AF.Relu))
            wv2f = self.wbig_v(32, 128) if "wbig" in self.sb else None
            for mt in range(KD):
                use_big = (mt % 2 == 1) and wv2f is not None
                wk = "wbig" if use_big else "w2a"
                wap = wv2f if use_big else sb["w2a"][:]
                B.dma("sy", [], [wk],
                      lambda e, mt=mt, wap=wap: e.dma_start(
                          out=wap, in_=self.io["scf2"][mt]))
                pn = ("P0", "P1", "P2", "P3")[mt % 4]
                B.do("te", "pe", [wk, "hbuf"], ["p" + pn],
                     lambda e, pn=pn, wap=wap: [
                         e.matmul(ps[pn][:, 0:G], wap[:, k, :], sb["hbuf"][:, k, :],
                                  start=(k == 0), stop=(k == 31))
                         for k in range(32)][-1])
                B.do("ve", "dv", ["p" + pn, "xs"], ["xs"],
                     lambda e, pn=pn, mt=mt: e.tensor_add(
                         sb["xs"][:, mt, g0:g0 + G], sb["xs"][:, mt, g0:g0 + G], ps[pn][:, 0:G]))
            if self.gw:
                B.dma("sy", ["xs"], [],
                      lambda e, g=g: e.dma_start(
                          out=self.io[dst_name].rearrange("(k p) t -> p k t", p=128)[:, :, g * G:(g + 1) * G],
                          in_=sb["xs"][:]))

    def alloc_l1a(self):
        s = self.sbuf
        need = lambda n: n not in self.sb
        if need("hbuf"):
            s("hbuf", [128, 32, G], BF16)
        if need("vtm"):
            s("vtm", [128, NCH, 1024], BF16)
        if need("xsn"):
            av = self.nc.lookup_mls(self.sb["vtm"]).memorylocations[0].addr
            self.sb["xsn"] = self.nc.alloc_sbuf_tensor_at(
                "xsn", [128, KD, G], BF16, offset=av)
        if need("kft"):
            s("kft", [128, KD, L], BF16)
        if need("otm"):
            s("otm", [128, 1024], BF16)
        if need("am"):
            s("am", [128, L], BF16)
        if need("am2"):
            s("am2", [128, L], BF16)
        if need("qf"):
            s("qf", [128, KD, G], BF16)
        if need("kf"):
            s("kf", [128, KD, G], BF16)
        s("xbb", [128, KD, G], BF16)
        s("xab", [128, KD, G], BF16)
        s("dxb", [128, KD, G], BF16)
        s("mix", [128, KD, G], BF16)
        if need("gtm"):
            s("gtm", [128, NCH, 1024], BF16)
        s("loraf", [32, 5, G], BF16)
        s("t64", [64, G], BF16)
        s("S1", [128, 512], F32)
        s("S1b", [128, 512], BF16)
        s("dvc", [128, 16], BF16)
        s("dsc", [128, KD, NCH], F32)
        s("gnm", [128, 16], F32)
        s("gne", [128, 16], F32)
        s("grs", [128, 16], F32)
        s("xaprev", [128, KD], F32)
        s("xfprev", [128, KD], F32)
        s("scr", [128, L], F32)
        s("scr2", [128, L], F32)
        s("zcol", [128, 1], F32)
        s("tmw1s", [128, KD, 160], BF16)
        s("tmw2s", [32, 1024], BF16)
        s("tdw1s", [128, KD, 64], BF16)
        s("tdw2s", [64, 1024], BF16)
        if need("lnxg_r"):
            s("lnxg_r", [128, 1024], BF16)
            s("lnxb_r", [128, 1024], BF16)
        if need("wbig"):
            s("wbig", [128, KD, 1024], BF16)

    def wbig_v(self, a, b):
        return self.sb["wbig"][:].rearrange("p x y -> p (x y)")[:, 0:a * b].rearrange(
            "p (a b) -> p a b", a=a)

    def emit_l1a_setup(self, B):
        sb = self.sb
        B.do("ve", "dv", [], ["zcol"], lambda e: e.memset(sb["zcol"][:, :], 0.0))
        B.dma("gp", [], ["tmw1s"],
              lambda e: e.dma_start(out=sb["tmw1s"][:],
                                    in_=self.io["tm_w1"].rearrange("(k p) j -> p k j", p=128)))
        B.dma("gp", [], ["tdw1s"],
              lambda e: e.dma_start(out=sb["tdw1s"][:],
                                    in_=self.io["td_w1"].rearrange("(k p) j -> p k j", p=128)))
        B.dma("gp", [], ["tdw2s"],
              lambda e: e.dma_start(out=sb["tdw2s"][:], in_=self.io["td_w2"]))


    def emit_l1a(self, B, zero_state, mask_by_role, src_name=None, dst_name=None,
                 reduced=False):
        sb, ps = self.sb, self.ps
        self._gw_io = (src_name, dst_name)
        self.pdm_gate(B, "sy")
        B.dma("gp", [], ["lnxg_r"], lambda e: e.dma_start(out=sb["lnxg_r"][:], in_=self.io["lnxgrow"]))
        B.dma("gp", [], ["lnxb_r"], lambda e: e.dma_start(out=sb["lnxb_r"][:], in_=self.io["lnxbrow"]))
        EQD, EKG, KRAW, RUK = 0, 8, 16, 24
        if zero_state:
            B.do("ve", "dv", [], ["S1"], lambda e: e.memset(sb["S1"][:, :], 0.0))
            B.do("ve", "dv", [], ["S1b"], lambda e: e.memset(sb["S1b"][:, :], 0.0))
            B.do("ve", "dv", [], ["xaprev"], lambda e: e.memset(sb["xaprev"][:, :], 0.0))
            B.do("ve", "dv", [], ["xfprev"], lambda e: e.memset(sb["xfprev"][:, :], 0.0))
        if mask_by_role:
            B.do("ve", "dv", ["rolec", "S1"], ["S1"],
                 lambda e: e.tensor_scalar(sb["S1"][:, :], sb["S1"][:, :], sb["rolec"][:, 0:1], None, ALU.mult))
            B.do("ve", "dv", ["S1"], ["S1b"], lambda e: e.tensor_copy(sb["S1b"][:, :], sb["S1"][:, :]))
            B.do("ve", "dv", ["rolec", "xaprev"], ["xaprev"],
                 lambda e: e.tensor_scalar(sb["xaprev"][:, :], sb["xaprev"][:, :], sb["rolec"][:, 0:1], None, ALU.mult))
            B.do("ve", "dv", ["rolec", "xfprev"], ["xfprev"],
                 lambda e: e.tensor_scalar(sb["xfprev"][:, :], sb["xfprev"][:, :], sb["rolec"][:, 0:1], None, ALU.mult))
        if self.gw:
            self._l1a_prefetch(B, 0)
        for g in range(NGRP):
            self._l1a_group(B, g * G, EQD, EKG, KRAW, RUK, reduced=reduced)

    def _l1a_prefetch(self, B, gr):
        src_name = self._gw_io[0]
        B.dma("gp", [], ["xsn"],
              lambda e, gr=gr: e.dma_start(
                  out=self.sb["xsn"][:],
                  in_=self.io[src_name].rearrange("(k p) t -> p k t", p=128)[:, :, gr:gr + G]))

    def _l1a_group(self, B, g0, EQD, EKG, KRAW, RUK, reduced=False):
        sb, ps = self.sb, self.ps
        io = self.io
        dst_name = None
        if self.gw:
            src_name, dst_name = self._gw_io
            gr = g0
            g0 = 0
            if not reduced:
                B.dma("sy", [], ["xs"],
                      lambda e: e.dma_start(
                          out=sb["xs"][:],
                          in_=io[src_name].rearrange("(k p) t -> p k t", p=128)[:, :, gr:gr + G]))

        if self.gw:
            self.emit_ln(B, g0, src="xsn")
        else:
            self.emit_ln(B, g0)
        self.emit_ln(B, g0, src="xlnb", dst="xbb", gcol="ln0_g", bcol="ln0_b")
        self.emit_ln(B, g0, src="xbb", dst="xab", gcol="ln1_g", bcol="ln1_b")

        for kd in range(KD):
            B.do("ve", "dv", ["xab", "xaprev"], ["dxb"],
                 lambda e, kd=kd: e.tensor_sub(sb["dxb"][:, kd, 1:G], sb["xab"][:, kd, 0:G - 1],
                                               sb["xab"][:, kd, 1:G]))
            B.do("ve", "dv", ["xab", "xaprev"], ["dxb"],
                 lambda e, kd=kd: e.tensor_sub(sb["dxb"][:, kd, 0:1], sb["xaprev"][:, kd:kd + 1],
                                               sb["xab"][:, kd, 0:1]))
        for kd in range(KD):
            B.do("ve", "dv", ["xab"], ["xaprev"],
                 lambda e, kd=kd: e.tensor_copy(sb["xaprev"][:, kd:kd + 1], sb["xab"][:, kd, G - 1:G]))

        for kd in range(KD):
            B.do("ve", "dv", ["dxb", "pv", "xab"], ["mix"],
                 lambda e, kd=kd: e.scalar_tensor_tensor(
                     sb["mix"][:, kd, :], sb["dxb"][:, kd, :],
                     sb["pv"][:, kd, PV["time_maa_x"]:PV["time_maa_x"] + 1],
                     sb["xab"][:, kd, :], ALU.mult, ALU.add))
        for f in range(5):
            B.do("te", "pe", ["tmw1s", "mix"], ["pP2"],
                 lambda e, f=f: [
                     e.matmul(ps["P2"][0:32, 0:G], sb["tmw1s"][:, kd, f * 32:(f + 1) * 32],
                              sb["mix"][:, kd, :], start=(kd == 0), stop=(kd == KD - 1))
                     for kd in range(KD)][-1])
            B.do("se", "ac", ["pP2"], ["loraf"],
                 lambda e, f=f: e.activation(sb["loraf"][0:32, f, :], ps["P2"][0:32, 0:G], AF.Tanh))

        FNAMES = ["w", "k", "v", "r", "g"]
        for fi, fn in enumerate(FNAMES):
            B.dma("sy", [], ["tmw2s"],
                  lambda e, fi=fi: e.dma_start(out=sb["tmw2s"][:], in_=io["sctm2"][fi]))
            for kd in range(KD):
                pn = ("P0", "P1", "P2", "P3")[kd % 4]
                B.do("te", "pe", ["tmw2s", "loraf"], ["p" + pn],
                     lambda e, pn=pn, kd=kd, fi=fi: e.matmul(
                         ps[pn][:, 0:G], sb["tmw2s"][0:32, kd * 128:(kd + 1) * 128],
                         sb["loraf"][0:32, fi, :], start=True, stop=True))
                cn = "time_maa_" + fn
                B.do("ve", "dv", ["p" + pn, "pv", "dxb"], ["sc0"],
                     lambda e, pn=pn, kd=kd, cn=cn: e.scalar_tensor_tensor(
                         sb["sc0"][:, :], ps[pn][:, 0:G],
                         sb["pv"][:, kd, PV[cn]:PV[cn] + 1],
                         sb["dxb"][:, kd, :], ALU.add, ALU.mult))
                B.do("ve", "dv", ["sc0", "xab"], ["mix"],
                     lambda e, kd=kd: e.tensor_add(sb["mix"][:, kd, :], sb["sc0"][:, :],
                                                   sb["xab"][:, kd, :]))
            if fn == "w":
                B.do("te", "pe", ["tdw1s", "mix"], ["pP2"],
                     lambda e: [
                         e.matmul(ps["P2"][0:64, 0:G], sb["tdw1s"][:, kd, :], sb["mix"][:, kd, :],
                                  start=(kd == 0), stop=(kd == KD - 1))
                         for kd in range(KD)][-1])
                B.do("se", "ac", ["pP2"], ["t64"],
                     lambda e: e.activation(sb["t64"][0:64, :], ps["P2"][0:64, 0:G], AF.Tanh))
                for kd in range(KD):
                    pn = "P0" if kd % 2 == 0 else "P1"
                    B.do("te", "pe", ["tdw2s", "t64"], ["p" + pn],
                         lambda e, pn=pn, kd=kd: e.matmul(
                             ps[pn][:, 0:G], sb["tdw2s"][:, kd * 128:(kd + 1) * 128],
                             sb["t64"][0:64, :], start=True, stop=True))
                    B.do("ve", "dv", ["p" + pn, "pv"], ["sc0"],
                         lambda e, pn=pn, kd=kd: e.tensor_scalar(
                             sb["sc0"][:, :], ps[pn][:, 0:G],
                             sb["pv"][:, kd, PV["time_decay"]:PV["time_decay"] + 1], None, ALU.add))
                    B.do("se", "ac", ["sc0"], ["sc1"],
                         lambda e: e.activation(sb["sc1"][:, :], sb["sc0"][:, :], AF.Exp))
                    B.do("ve", "dv", ["sc1"], ["sc0"],
                         lambda e: e.tensor_tensor_scan(sb["sc0"][:, :], sb["sc1"][:, :],
                                                        sb["sc1"][:, :], 0.0, ALU.add, ALU.bypass))
                    for tc in range(NCH):
                        t0 = tc * L
                        basec = sb["zcol"][:, 0:1] if tc == 0 else sb["sc0"][:, t0 - 1:t0]
                        B.do("ve", "dv", ["sc0", "zcol"], ["scr"],
                             lambda e, t0=t0, basec=basec: e.tensor_scalar(
                                 sb["scr"][:, :], sb["sc0"][:, t0:t0 + L], basec, None, ALU.subtract))
                        B.do("se", "ac", ["scr"], ["hbuf"],
                             lambda e, kd=kd, tc=tc: e.activation(
                                 sb["hbuf"][:, EKG + kd, tc * L:(tc + 1) * L], sb["scr"][:, :], AF.Exp))
                        B.do("ve", "dv", ["scr", "sc1"], ["scr2"],
                             lambda e, t0=t0: e.tensor_sub(sb["scr2"][:, :], sb["scr"][:, :],
                                                           sb["sc1"][:, t0:t0 + L]))
                        B.do("se", "ac", ["scr2", "negone"], ["hbuf"],
                             lambda e, kd=kd, tc=tc: e.activation(
                                 sb["hbuf"][:, EQD + kd, tc * L:(tc + 1) * L], sb["scr2"][:, :],
                                 AF.Exp, scale=sb["negone"][:, 0:1]))
                        B.do("ve", "dv", ["sc0", "zcol"], ["dsc"],
                             lambda e, t0=t0, basec=basec, kd=kd, tc=tc: e.tensor_scalar(
                                 sb["dsc"][:, kd, tc:tc + 1], sb["sc0"][:, t0 + L - 1:t0 + L],
                                 basec, None, ALU.subtract))
                        B.do("se", "ac", ["dsc", "negone"], ["dsc"],
                             lambda e, kd=kd, tc=tc: e.activation(
                                 sb["dsc"][:, kd, tc:tc + 1], sb["dsc"][:, kd, tc:tc + 1],
                                 AF.Exp, scale=sb["negone"][:, 0:1]))
            elif fn == "k":
                for mt in range(KD):
                    wt = "w1a" if mt % 2 == 0 else "w1b"
                    B.dma("sy", [], [wt],
                          lambda e, mt=mt, wt=wt: e.dma_start(
                              out=sb[wt][:], in_=io["scWk"][mt]))
                    pn = ("P0", "P1", "P2", "P3")[mt % 4]
                    B.do("te", "pe", [wt, "mix"], ["p" + pn],
                         lambda e, pn=pn, wt=wt: [
                             e.matmul(ps[pn][:, 0:G], sb[wt][:, kd, :],
                                      sb["mix"][:, kd, :], start=(kd == 0), stop=(kd == KD - 1))
                             for kd in range(KD)][-1])
                    B.do("se", "ac", ["p" + pn], ["hbuf"],
                         lambda e, pn=pn, mt=mt: e.copy(sb["hbuf"][:, KRAW + mt, :], ps[pn][:, 0:G]))
                    B.do("ve", "dv", ["p" + pn, "hbuf"], ["kf"],
                         lambda e, pn=pn, mt=mt: e.tensor_mul(
                             sb["kf"][:, mt, :], ps[pn][:, 0:G], sb["hbuf"][:, EKG + mt, :]))
            elif fn == "v":
                B.dma("sy", [], ["wbig"],
                      lambda e: e.dma_start(out=sb["wbig"][:], in_=io["scWv"]))
                for tc in range(NCH):
                    for nb in range(2):
                        pn = "P0" if (tc * 2 + nb) % 2 == 0 else "P1"
                        B.do("te", "pe", ["wbig", "mix"], ["p" + pn],
                             lambda e, pn=pn, tc=tc, nb=nb: [
                                 e.matmul(ps[pn][:, 0:512], sb["mix"][:, kd, tc * L:(tc + 1) * L],
                                          sb["wbig"][:, kd, nb * 512:(nb + 1) * 512],
                                          start=(kd == 0), stop=(kd == KD - 1))
                                 for kd in range(KD)][-1])
                        B.do("ve", "dv", ["p" + pn], ["vtm"],
                             lambda e, pn=pn, tc=tc, nb=nb: e.tensor_copy(
                                 sb["vtm"][:, tc, nb * 512:(nb + 1) * 512], ps[pn][:, 0:512]))
            elif fn == "r":
                for mt in range(KD):
                    wt = "w1a" if mt % 2 == 0 else "w1b"
                    B.dma("sy", [], [wt],
                          lambda e, mt=mt, wt=wt: e.dma_start(
                              out=sb[wt][:], in_=io["scWr"][mt]))
                    pn = ("P0", "P1", "P2", "P3")[mt % 4]
                    B.do("te", "pe", [wt, "mix"], ["p" + pn],
                         lambda e, pn=pn, wt=wt: [
                             e.matmul(ps[pn][:, 0:G], sb[wt][:, kd, :],
                                      sb["mix"][:, kd, :], start=(kd == 0), stop=(kd == KD - 1))
                             for kd in range(KD)][-1])
                    B.do("ve", "dv", ["p" + pn, "hbuf"], ["qf"],
                         lambda e, pn=pn, mt=mt: e.tensor_mul(
                             sb["qf"][:, mt, :], ps[pn][:, 0:G], sb["hbuf"][:, EQD + mt, :]))
                    B.do("ve", "dv", ["p" + pn, "pv"], ["sc0"],
                         lambda e, pn=pn, mt=mt: e.tensor_scalar(
                             sb["sc0"][:, :], ps[pn][:, 0:G],
                             sb["pv"][:, mt, PV["u_flat"]:PV["u_flat"] + 1], None, ALU.mult))
                    B.do("ve", "dv", ["sc0", "hbuf"], ["hbuf"],
                         lambda e, mt=mt: e.tensor_mul(
                             sb["hbuf"][:, RUK + mt, :], sb["sc0"][:, :], sb["hbuf"][:, KRAW + mt, :]))
            elif fn == "g":
                B.dma("sy", [], ["wbig"],
                      lambda e: e.dma_start(out=sb["wbig"][:], in_=io["scWg"]))
                for tc in range(NCH):
                    for nb in range(2):
                        pn = "P0" if (tc * 2 + nb) % 2 == 0 else "P1"
                        B.do("te", "pe", ["wbig", "mix"], ["p" + pn],
                             lambda e, pn=pn, tc=tc, nb=nb: [
                                 e.matmul(ps[pn][:, 0:512], sb["mix"][:, kd, tc * L:(tc + 1) * L],
                                          sb["wbig"][:, kd, nb * 512:(nb + 1) * 512],
                                          start=(kd == 0), stop=(kd == KD - 1))
                                 for kd in range(KD)][-1])
                        B.do("se", "ac", ["p" + pn], ["gtm"],
                             lambda e, pn=pn, tc=tc, nb=nb: e.activation(
                                 sb["gtm"][:, tc, nb * 512:(nb + 1) * 512], ps[pn][:, 0:512], AF.Silu))

        for tc in range(NCH):
            t0 = tc * L
            out = (not reduced) or tc == NCH - 1
            for kd in range(KD):
                B.do("te", "pe", ["kf", "idb"], ["pPT"],
                     lambda e, kd=kd: e.transpose(ps["PT"][:, kd * 128:(kd + 1) * 128],
                                                  sb["kf"][:, kd, t0:t0 + L], sb["idb"][:, :]))
                B.do("ve", "dv", ["pPT"], ["kft"],
                     lambda e, kd=kd: e.tensor_copy(sb["kft"][:, kd, :],
                                                    ps["PT"][:, kd * 128:(kd + 1) * 128]))
            if out:
                for kd in range(KD):
                    B.do("te", "pe", ["hbuf", "hsel_b"], ["pP2"],
                         lambda e, kd=kd: e.matmul(
                             ps["P2"][:, 2 * kd:2 * kd + 2], sb["hbuf"][:, RUK + kd, t0:t0 + L],
                             sb["hsel_b"][:, :], start=True, stop=True))
                B.do("se", "ac", ["pP2"], ["dvc"],
                     lambda e: e.copy(sb["dvc"][:, :], ps["P2"][:, 0:16]))
            for h in range(NH):
                kd, b0 = h // 2, (h % 2) * 64
                pa = "P4" if h % 2 == 0 else "P5"
                an = "am" if h % 2 == 0 else "am2"
                po = "P0" if h < 8 else "P1"
                oc0 = (h % 8) * 64
                if out:
                    B.do("te", "pe", ["kf", "qf"], ["p" + pa],
                         lambda e, pa=pa, kd=kd, b0=b0: e.matmul(
                             ps[pa][:, 0:L], sb["kf"][b0:b0 + 64, kd, t0:t0 + L],
                             sb["qf"][b0:b0 + 64, kd, t0:t0 + L], start=True, stop=True))
                    B.do("ve", "dv", ["p" + pa, "mS"], [an],
                         lambda e, pa=pa, an=an: e.tensor_mul(sb[an][:, :], ps[pa][:, 0:L],
                                                              sb["mS"][:, :]))
                    B.do("te", "pe", [an, "vtm"], ["p" + po],
                         lambda e, po=po, oc0=oc0, h=h, an=an: e.matmul(
                             ps[po][:, oc0:oc0 + 64], sb[an][:, :],
                             sb["vtm"][:, tc, h * 64:(h + 1) * 64], start=True, stop=False))
                    B.do("te", "pe", ["qf", "S1b"], ["p" + po],
                         lambda e, po=po, oc0=oc0, kd=kd, b0=b0, h=h: e.matmul(
                             ps[po][:, oc0:oc0 + 64], sb["qf"][b0:b0 + 64, kd, t0:t0 + L],
                             sb["S1b"][b0:b0 + 64, (h // 2) * 64:(h // 2 + 1) * 64],
                             start=False, stop=True))
                B.do("te", "pe", ["kft", "vtm"], ["pP6"],
                     lambda e, h=h, kd=kd, b0=b0: e.matmul(
                         ps["P6"][b0:b0 + 64, (h // 2) * 64:(h // 2 + 1) * 64],
                         sb["kft"][:, kd, b0:b0 + 64],
                         sb["vtm"][:, tc, h * 64:(h + 1) * 64], start=True, stop=True))
            if out:
                for h in range(NH):
                    po = "P0" if h < 8 else "P1"
                    oc0 = (h % 8) * 64
                    B.do("ve", "dv", ["p" + po, "dvc", "vtm"], ["otm"],
                         lambda e, po=po, oc0=oc0, h=h: e.scalar_tensor_tensor(
                             sb["otm"][:, h * 64:(h + 1) * 64], sb["vtm"][:, tc, h * 64:(h + 1) * 64],
                             sb["dvc"][:, h:h + 1], ps[po][:, oc0:oc0 + 64], ALU.mult, ALU.add))
            B.do("ve", "dv", ["pP6", "S1"], ["S1"],
                 lambda e: e.tensor_add(sb["S1"][:, :], sb["S1"][:, :], ps["P6"][:, 0:512]))
            for h in range(NH):
                kd, b0 = h // 2, (h % 2) * 64
                B.do("ve", "dv", ["dsc", "S1"], ["S1"],
                     lambda e, kd=kd, b0=b0, h=h: e.tensor_scalar(
                         sb["S1"][b0:b0 + 64, (h // 2) * 64:(h // 2 + 1) * 64],
                         sb["S1"][b0:b0 + 64, (h // 2) * 64:(h // 2 + 1) * 64],
                         sb["dsc"][b0:b0 + 64, kd, tc:tc + 1], None, ALU.mult))
            B.do("ve", "dv", ["S1"], ["S1b"], lambda e: e.tensor_copy(sb["S1b"][:, :], sb["S1"][:, :]))
            if not out:
                continue
            for h in range(NH):
                B.do("se", "ac", ["otm"], ["sq16"],
                     lambda e, h=h: e.activation(sb["sq16"][:, 0:64], sb["otm"][:, h * 64:(h + 1) * 64],
                                                 AF.Square))
                B.do("ve", "dv", ["otm"], ["gnm"],
                     lambda e, h=h: e.reduce_sum(sb["gnm"][:, h:h + 1], sb["otm"][:, h * 64:(h + 1) * 64],
                                                 axis=mybir.AxisListType.X))
                B.do("ve", "dv", ["sq16"], ["gne"],
                     lambda e, h=h: e.reduce_sum(sb["gne"][:, h:h + 1], sb["sq16"][:, 0:64],
                                                 axis=mybir.AxisListType.X))
            B.do("ve", "dv", ["gnm"], ["gnm"],
                 lambda e: e.tensor_scalar(sb["gnm"][:, :], sb["gnm"][:, :], 1.0 / 64, None, ALU.mult))
            B.do("ve", "dv", ["gne"], ["gne"],
                 lambda e: e.tensor_scalar(sb["gne"][:, :], sb["gne"][:, :], 1.0 / 64, None, ALU.mult))
            B.do("se", "ac", ["gnm"], ["grs"],
                 lambda e: e.activation(sb["grs"][:, :], sb["gnm"][:, :], AF.Square))
            B.do("ve", "dv", ["gne", "grs"], ["grs"],
                 lambda e: e.tensor_sub(sb["grs"][:, :], sb["gne"][:, :], sb["grs"][:, :]))
            B.do("se", "ac", ["grs", "epscol"], ["grs"],
                 lambda e: e.activation(sb["grs"][:, :], sb["grs"][:, :], AF.Ln,
                                        bias=sb["epscol"][:, 0:1]))
            B.do("se", "ac", ["grs", "neghalf"], ["grs"],
                 lambda e: e.activation(sb["grs"][:, :], sb["grs"][:, :], AF.Exp,
                                        scale=sb["neghalf"][:, 0:1]))
            for h in range(NH):
                B.do("ve", "dv", ["gnm", "grs", "otm"], ["otm"],
                     lambda e, h=h: e.tensor_scalar(
                         sb["otm"][:, h * 64:(h + 1) * 64], sb["otm"][:, h * 64:(h + 1) * 64],
                         sb["gnm"][:, h:h + 1], sb["grs"][:, h:h + 1], ALU.subtract, ALU.mult))
            B.do("ve", "dv", ["otm", "lnxg_r"], ["sc0"],
                 lambda e: e.tensor_mul(sb["sc0"][:, 0:512], sb["otm"][:, 0:512], sb["lnxg_r"][:, 0:512]))
            B.do("ve", "dv", ["otm", "lnxg_r"], ["sc1"],
                 lambda e: e.tensor_mul(sb["sc1"][:, 0:512], sb["otm"][:, 512:1024], sb["lnxg_r"][:, 512:1024]))
            B.do("ve", "dv", ["sc0", "lnxb_r"], ["sc0"],
                 lambda e: e.tensor_add(sb["sc0"][:, 0:512], sb["sc0"][:, 0:512], sb["lnxb_r"][:, 0:512]))
            B.do("ve", "dv", ["sc1", "lnxb_r"], ["sc1"],
                 lambda e: e.tensor_add(sb["sc1"][:, 0:512], sb["sc1"][:, 0:512], sb["lnxb_r"][:, 512:1024]))
            B.do("ve", "dv", ["sc0", "gtm"], ["otm"],
                 lambda e: e.tensor_mul(sb["otm"][:, 0:512], sb["sc0"][:, 0:512], sb["gtm"][:, tc, 0:512]))
            B.do("ve", "dv", ["sc1", "gtm"], ["otm"],
                 lambda e: e.tensor_mul(sb["otm"][:, 512:1024], sb["sc1"][:, 0:512], sb["gtm"][:, tc, 512:1024]))
            for kd in range(KD):
                B.do("te", "pe", ["otm", "idb"], ["pPT"],
                     lambda e, kd=kd: e.transpose(ps["PT"][:, kd * 128:(kd + 1) * 128],
                                                  sb["otm"][:, kd * 128:(kd + 1) * 128], sb["idb"][:, :]))
                B.do("ve", "dv", ["pPT"], ["dxb"],
                     lambda e, kd=kd: e.tensor_copy(sb["dxb"][:, kd, t0:t0 + L],
                                                    ps["PT"][:, kd * 128:(kd + 1) * 128]))

        if self.gw and gr + G < TL:
            self._l1a_prefetch(B, gr + G)

        c0, cw = ((NCH - 1) * L, L) if reduced else (0, G)
        for mt in range(KD):
            wt = "w1a" if mt % 2 == 0 else "w1b"
            B.dma("sy", [], [wt],
                  lambda e, mt=mt, wt=wt: e.dma_start(out=sb[wt][:], in_=io["scWo"][mt]))
            pn = "P0" if mt % 2 == 0 else "P1"
            B.do("te", "pe", [wt, "dxb"], ["p" + pn],
                 lambda e, pn=pn, wt=wt: [
                     e.matmul(ps[pn][:, 0:cw], sb[wt][:, kd, :],
                              sb["dxb"][:, kd, c0:c0 + cw], start=(kd == 0), stop=(kd == KD - 1))
                     for kd in range(KD)][-1])
            B.do("ve", "dv", ["p" + pn, "xbb"], ["xlnb"],
                 lambda e, pn=pn, mt=mt: e.tensor_add(sb["xlnb"][:, mt, c0:c0 + cw],
                                                      ps[pn][:, 0:cw],
                                                      sb["xbb"][:, mt, c0:c0 + cw]))

        if reduced:
            self.emit_ln(B, g0, src="xlnb", dst="xab", gcol="ln2_g", bcol="ln2_b")
            for kd in range(KD):
                B.do("ve", "dv", ["xab"], ["xfprev"],
                     lambda e, kd=kd: e.tensor_copy(sb["xfprev"][:, kd:kd + 1],
                                                    sb["xab"][:, kd, G - 1:G]))
            return
        self.emit_ln(B, g0, src="xlnb", dst="xab", gcol="ln2_g", bcol="ln2_b")
        for kd in range(KD):
            B.do("ve", "dv", ["xab", "xfprev"], ["mix"],
                 lambda e, kd=kd: e.tensor_sub(sb["mix"][:, kd, 1:G], sb["xab"][:, kd, 0:G - 1],
                                               sb["xab"][:, kd, 1:G]))
            B.do("ve", "dv", ["xab", "xfprev"], ["mix"],
                 lambda e, kd=kd: e.tensor_sub(sb["mix"][:, kd, 0:1], sb["xfprev"][:, kd:kd + 1],
                                               sb["xab"][:, kd, 0:1]))
        for kd in range(KD):
            B.do("ve", "dv", ["xab"], ["xfprev"],
                 lambda e, kd=kd: e.tensor_copy(sb["xfprev"][:, kd:kd + 1], sb["xab"][:, kd, G - 1:G]))
        for kd in range(KD):
            B.do("ve", "dv", ["mix", "pv", "xab"], ["qf"],
                 lambda e, kd=kd: e.scalar_tensor_tensor(
                     sb["qf"][:, kd, :], sb["mix"][:, kd, :],
                     sb["pv"][:, kd, PV["ffn_maa_r"]:PV["ffn_maa_r"] + 1],
                     sb["xab"][:, kd, :], ALU.mult, ALU.add))
        for mt in range(KD):
            wt = "w1a" if mt % 2 == 0 else "w1b"
            B.dma("sy", [], [wt],
                  lambda e, mt=mt, wt=wt: e.dma_start(out=sb[wt][:], in_=io["scWr_ffn"][mt]))
            pn = "P0" if mt % 2 == 0 else "P1"
            B.do("te", "pe", [wt, "qf"], ["p" + pn],
                 lambda e, pn=pn, wt=wt: [
                     e.matmul(ps[pn][:, 0:G], sb[wt][:, kd, :],
                              sb["qf"][:, kd, :], start=(kd == 0), stop=(kd == KD - 1))
                     for kd in range(KD)][-1])
            B.do("se", "ac", ["p" + pn], ["kf"],
                 lambda e, pn=pn, mt=mt: e.activation(sb["kf"][:, mt, :], ps[pn][:, 0:G], AF.Sigmoid))
            if mt == 0:
                for kd in range(KD):
                    B.do("ve", "dv", ["mix", "pv", "xab"], ["xbb"],
                         lambda e, kd=kd: e.scalar_tensor_tensor(
                             sb["xbb"][:, kd, :], sb["mix"][:, kd, :],
                             sb["pv"][:, kd, PV["ffn_maa_k"]:PV["ffn_maa_k"] + 1],
                             sb["xab"][:, kd, :], ALU.mult, ALU.add))
        for j in range(FK // 128):
            wt = "w1a" if j % 2 == 0 else "w1b"
            B.dma("sy", [], [wt],
                  lambda e, j=j, wt=wt: e.dma_start(out=sb[wt][:], in_=io["scWk_ffn"][j]))
            pn = ("P0", "P1", "P2", "P3")[j % 4]
            B.do("te", "pe", [wt, "xbb"], ["p" + pn],
                 lambda e, pn=pn, wt=wt: [
                     e.matmul(ps[pn][:, 0:G], sb[wt][:, kd, :], sb["xbb"][:, kd, :],
                              start=(kd == 0), stop=(kd == KD - 1))
                     for kd in range(KD)][-1])
            B.do("se", "ac", ["p" + pn], ["xb16"],
                 lambda e, pn=pn: e.activation(sb["xb16"][:, :], ps[pn][:, 0:G], AF.Relu))
            B.do("se", "ac", ["xb16"], ["hbuf"],
                 lambda e, j=j: e.activation(sb["hbuf"][:, j, :], sb["xb16"][:, :], AF.Square))
        wv2 = self.wbig_v(28, 128)
        for mt in range(KD):
            use_big = mt % 2 == 1
            wk = "wbig" if use_big else "w2a"
            wap = wv2 if use_big else sb["w2a"][:, 0:28, :]
            B.dma("sy", [], [wk],
                  lambda e, mt=mt, wap=wap: e.dma_start(out=wap, in_=io["scWv_ffn"][mt]))
            pn = ("P0", "P1", "P2", "P3")[mt % 4]
            B.do("te", "pe", [wk, "hbuf"], ["p" + pn],
                 lambda e, pn=pn, wap=wap: [
                     e.matmul(ps[pn][:, 0:G], wap[:, k, :], sb["hbuf"][:, k, :],
                              start=(k == 0), stop=(k == 27))
                     for k in range(28)][-1])
            B.do("ve", "dv", ["p" + pn, "kf"], ["sc0"],
                 lambda e, pn=pn, mt=mt: e.tensor_mul(sb["sc0"][:, :], ps[pn][:, 0:G],
                                                      sb["kf"][:, mt, :]))
            B.do("ve", "dv", ["sc0", "xs"], ["xs"],
                 lambda e, mt=mt: e.tensor_add(sb["xs"][:, mt, g0:g0 + G], sb["xs"][:, mt, g0:g0 + G],
                                               sb["sc0"][:, :]))
            B.do("ve", "dv", ["xlnb", "xs"], ["xs"],
                 lambda e, mt=mt: e.tensor_add(sb["xs"][:, mt, g0:g0 + G], sb["xs"][:, mt, g0:g0 + G],
                                               sb["xlnb"][:, mt, :]))
        if self.gw and dst_name is not None:
            B.dma("sy", ["xs"], [],
                  lambda e: e.dma_start(
                      out=io[dst_name].rearrange("(k p) t -> p k t", p=128)[:, :, gr:gr + G],
                      in_=sb["xs"][:]))


    def alloc_l1f(self):
        s = self.sbuf
        need = lambda n: n not in self.sb
        if need("hbuf"):
            s("hbuf", [128, 32, G], BF16)
        if need("w1a"):
            s("w1a", [128, KD, 128], BF16)
            s("w1b", [128, KD, 128], BF16)
        if need("w2a"):
            s("w2a", [128, 32, 128], BF16)
        self.ensure_w1_ring()
        if need("idf"):
            s("idf", [128, 128], F32)
        need = lambda n: n not in self.sb
        if need("accf"):
            s("accf", [128, KD, G], F32)
        s("wgt32", [128, NCH, 8], F32)
        s("lgt", [128, 8], F32)
        s("lgc", [128, 8], F32)
        s("mk1", [128, 8], F32)
        s("mk2", [128, 8], F32)
        s("mx8", [128, 8], F32)
        s("lgt2", [128, 8], F32)
        s("mxb", [128, 8], F32)
        s("mk1b", [128, 8], F32)
        s("mk2b", [128, 8], F32)
        if need("gater"):
            s("gater", [1, G], F32)
        if need("gb"):
            s("gb", [128, G], BF16)
        s("rws", [128, KD, 8], BF16)
        if need("b1s"):
            s("b1s", [128, 64, 8], F32)
        s("rbL_s", [128, 8], F32)

    def emit_l1f_setup(self, B):
        sb = self.sb
        B.dma("gp", [], ["rws"],
              lambda e: e.dma_start(out=sb["rws"][:],
                                    in_=self.io["rwT"].rearrange("(k p) j -> p k j", p=128)))
        B.dma("sy", [], ["rbL_s"], lambda e: e.dma_start(out=sb["rbL_s"][:], in_=self.io["rbL"]))
        if self.has_l0a is False:
            B.dma("sy", [], ["idf"], lambda e: e.dma_start(out=sb["idf"][:], in_=self.consts["ident_f"].ap()))

    def emit_l1f(self, B, src_name=None, dst_name=None):
        sb, ps = self.sb, self.ps
        io = self.io
        self.pdm_gate(B, "sy", stage="moe")
        B.dma("sy", [], ["b1s"],
              lambda e: e.dma_start(out=sb["b1s"][:],
                                    in_=io["m_b1L"].rearrange("p (j e) -> p j e", e=8)))
        for g in range(NGRP):
            gr = g * G
            g0 = 0 if self.gw else gr
            if self.gw:
                B.dma("sy", [], ["xs"],
                      lambda e, gr=gr: e.dma_start(
                          out=sb["xs"][:],
                          in_=io[src_name].rearrange("(k p) t -> p k t", p=128)[:, :, gr:gr + G]))
            self.emit_ln(B, g0)
            for tc in range(NCH):
                B.do("te", "pe", ["xlnb", "rws"], ["pP2"],
                     lambda e, tc=tc: [
                         e.matmul(ps["P2"][:, 0:8], sb["xlnb"][:, kd, tc * L:(tc + 1) * L],
                                  sb["rws"][:, kd, :], start=(kd == 0), stop=(kd == KD - 1))
                         for kd in range(KD)][-1])
                B.do("ve", "dv", ["pP2", "rbL_s"], ["lgt"],
                     lambda e: e.tensor_add(sb["lgt"][:, :], ps["P2"][:, 0:8], sb["rbL_s"][:, :]))
                B.do("se", "ac", ["lgt"], ["lgt2"],
                     lambda e: e.copy(sb["lgt2"][:, :], sb["lgt"][:, :]))
                B.do("ve", "dv", ["lgt2"], ["mx8"],
                     lambda e: e.tensor_max(sb["mx8"][:, 0:4], sb["lgt2"][:, 0:4], sb["lgt2"][:, 4:8]))
                B.do("se", "ac", ["mx8"], ["mxb"],
                     lambda e: e.copy(sb["mxb"][:, 0:4], sb["mx8"][:, 0:4]))
                B.do("ve", "dv", ["mxb"], ["mx8"],
                     lambda e: e.tensor_max(sb["mx8"][:, 4:6], sb["mxb"][:, 0:2], sb["mxb"][:, 2:4]))
                B.do("se", "ac", ["mx8"], ["mxb"],
                     lambda e: e.copy(sb["mxb"][:, 4:6], sb["mx8"][:, 4:6]))
                B.do("ve", "dv", ["mxb"], ["mx8"],
                     lambda e: e.tensor_max(sb["mx8"][:, 6:7], sb["mxb"][:, 4:5], sb["mxb"][:, 5:6]))
                B.do("se", "ac", ["mx8"], ["mxb"],
                     lambda e: e.copy(sb["mxb"][:, 6:7], sb["mx8"][:, 6:7]))
                B.do("ve", "dv", ["lgt2", "mxb"], ["mk1"],
                     lambda e: e.tensor_scalar(sb["mk1"][:, :], sb["lgt2"][:, :], sb["mxb"][:, 6:7],
                                               None, ALU.is_equal))
                B.do("se", "ac", ["mk1"], ["mk1b"],
                     lambda e: e.copy(sb["mk1b"][:, :], sb["mk1"][:, :]))
                B.do("ve", "dv", ["mk1b", "lgt2"], ["mk2"],
                     lambda e: e.scalar_tensor_tensor(sb["mk2"][:, :], sb["mk1b"][:, :], -1e30,
                                                      sb["lgt2"][:, :], ALU.mult, ALU.add))
                B.do("se", "ac", ["mk2"], ["mk2b"],
                     lambda e: e.copy(sb["mk2b"][:, :], sb["mk2"][:, :]))
                B.do("ve", "dv", ["mk2b"], ["mx8"],
                     lambda e: e.tensor_max(sb["mx8"][:, 0:4], sb["mk2b"][:, 0:4], sb["mk2b"][:, 4:8]))
                B.do("se", "ac", ["mx8"], ["mxb"],
                     lambda e: e.copy(sb["mxb"][:, 0:4], sb["mx8"][:, 0:4]))
                B.do("ve", "dv", ["mxb"], ["mx8"],
                     lambda e: e.tensor_max(sb["mx8"][:, 4:6], sb["mxb"][:, 0:2], sb["mxb"][:, 2:4]))
                B.do("se", "ac", ["mx8"], ["mxb"],
                     lambda e: e.copy(sb["mxb"][:, 4:6], sb["mx8"][:, 4:6]))
                B.do("ve", "dv", ["mxb"], ["mx8"],
                     lambda e: e.tensor_max(sb["mx8"][:, 7:8], sb["mxb"][:, 4:5], sb["mxb"][:, 5:6]))
                B.do("se", "ac", ["mx8"], ["mxb"],
                     lambda e: e.copy(sb["mxb"][:, 7:8], sb["mx8"][:, 7:8]))
                B.do("ve", "dv", ["mxb"], ["lgc"],
                     lambda e: e.tensor_scalar(sb["lgc"][:, 0:1], sb["mxb"][:, 7:8],
                                               sb["mxb"][:, 6:7], None, ALU.subtract))
                B.do("se", "ac", ["lgc"], ["lgc"],
                     lambda e: e.activation(sb["lgc"][:, 4:5], sb["lgc"][:, 0:1], AF.Exp))
                B.do("se", "ac", ["lgc", "onecol_f"], ["lgc"],
                     lambda e: e.activation(sb["lgc"][:, 5:6], sb["lgc"][:, 4:5], AF.Ln,
                                            bias=sb["onecol_f"][:, 0:1]))
                B.do("se", "ac", ["lgc", "negone"], ["lgc"],
                     lambda e: e.activation(sb["lgc"][:, 5:6], sb["lgc"][:, 5:6], AF.Exp,
                                            scale=sb["negone"][:, 0:1]))
                B.do("ve", "dv", ["lgc"], ["lgc"],
                     lambda e: e.tensor_mul(sb["lgc"][:, 6:7], sb["lgc"][:, 4:5], sb["lgc"][:, 5:6]))
                B.do("ve", "dv", ["mk2b", "mxb"], ["mk2"],
                     lambda e: e.tensor_scalar(sb["mk2"][:, :], sb["mk2b"][:, :], sb["mxb"][:, 7:8],
                                               None, ALU.is_equal))
                B.do("se", "ac", ["mk2", "lgc"], ["mk2b"],
                     lambda e: e.mul(sb["mk2b"][:, :], sb["mk2"][:, :], sb["lgc"][:, 6:7]))
                B.do("ve", "dv", ["mk1b", "lgc", "mk2b"], ["wgt32"],
                     lambda e, tc=tc: e.scalar_tensor_tensor(
                         sb["wgt32"][:, tc, :], sb["mk1b"][:, :], sb["lgc"][:, 5:6],
                         sb["mk2b"][:, :], ALU.mult, ALU.add))
                if tc == 0 and g == 0:
                    self.tap(B, "lgt_0", "lgt", sb["lgt"][:], [128, 8])
                    self.tap(B, "mk1_0", "mk1", sb["mk1"][:], [128, 8])
                    self.tap(B, "mk2_0", "mk2", sb["mk2"][:], [128, 8])
                    self.tap(B, "lgc_0", "lgc", sb["lgc"][:], [128, 8])
            if g == 0:
                self.tap(B, "wgt0", "wgt32", sb["wgt32"][:], [128, NCH, 8])
                self.tap(B, "xln3", "xlnb", sb["xlnb"][:], [128, KD, G])
            for ex in range(NE):
                for tc in range(NCH):
                    B.do("te", "pe", ["wgt32", "idf"], ["pP3"],
                         lambda e, tc=tc, ex=ex: e.transpose(
                             ps["P3"][0:1, 0:128], sb["wgt32"][:, tc, ex:ex + 1], sb["idf"][:, :]))
                    B.do("ve", "dv", ["pP3"], ["gater"],
                         lambda e, tc=tc: e.tensor_copy(sb["gater"][0:1, tc * L:(tc + 1) * L],
                                                        ps["P3"][0:1, 0:128]))
                B.do("te", "pe", ["gater", "onerow_f"], ["pP3"],
                     lambda e: e.matmul(ps["P3"][:, 0:G], sb["onerow_f"][0:1, :], sb["gater"][0:1, :],
                                        start=True, stop=True))
                B.do("ve", "dv", ["pP3"], ["gb"],
                     lambda e: e.tensor_copy(sb["gb"][:, :], ps["P3"][:, 0:G]))
                if g == 0 and ex == 0:
                    self.tap(B, "gb0", "gb", sb["gb"][:], [128, G])
                for j in range(32):
                    wt = MK.W1RING[j % 6]
                    B.dma("sy", [], [wt],
                          lambda e, j=j, ex=ex, wt=wt: e.dma_start(
                              out=sb[wt][:], in_=io["scw1"][ex, j]))
                    pn = ("P0", "P1", "P2")[j % 3]
                    B.do("te", "pe", [wt, "xlnb"], ["p" + pn],
                         lambda e, pn=pn, wt=wt: [
                             e.matmul(ps[pn][:, 0:G], sb[wt][:, kd, :], sb["xlnb"][:, kd, :],
                                      start=(kd == 0), stop=(kd == KD - 1))
                             for kd in range(KD)][-1])
                    B.do("se", "ac", ["p" + pn, "b1s"], ["hbuf"],
                         lambda e, pn=pn, j=j, ex=ex: e.activation(
                             sb["hbuf"][:, j, :], ps[pn][:, 0:G], AF.Silu,
                             bias=sb["b1s"][:, j, ex:ex + 1]))
                for j in range(32):
                    wt = MK.W1RING[(j + 32) % 6]
                    B.dma("sy", [], [wt],
                          lambda e, j=j, ex=ex, wt=wt: e.dma_start(
                              out=sb[wt][:], in_=io["scw1"][ex, j + 32]))
                    pn = ("P0", "P1", "P2")[j % 3]
                    B.do("te", "pe", [wt, "xlnb"], ["p" + pn],
                         lambda e, pn=pn, wt=wt: [
                             e.matmul(ps[pn][:, 0:G], sb[wt][:, kd, :], sb["xlnb"][:, kd, :],
                                      start=(kd == 0), stop=(kd == KD - 1))
                             for kd in range(KD)][-1])
                    B.do("ve", "dv", ["p" + pn, "b1s", "hbuf"], ["hbuf"],
                         lambda e, pn=pn, j=j, ex=ex: e.scalar_tensor_tensor(
                             sb["hbuf"][:, j, :], ps[pn][:, 0:G],
                             sb["b1s"][:, j + 32, ex:ex + 1],
                             sb["hbuf"][:, j, :], ALU.add, ALU.mult))
                wv2f = self.wbig_v(32, 128)
                for mt in range(KD):
                    wk = "w2a" if mt % 2 == 0 else "wbig"
                    wap = sb["w2a"][:] if mt % 2 == 0 else wv2f
                    B.dma("sy", [], [wk],
                          lambda e, mt=mt, ex=ex, wap=wap: e.dma_start(
                              out=wap, in_=io["scw2"][ex, mt]))
                    pn = ("P0", "P1", "P2")[mt % 3]
                    B.do("te", "pe", [wk, "hbuf"], ["p" + pn],
                         lambda e, pn=pn, wap=wap: [
                             e.matmul(ps[pn][:, 0:G], wap[:, k, :], sb["hbuf"][:, k, :],
                                      start=(k == 0), stop=(k == 31))
                             for k in range(31 + 1)][-1])
                    bn = "m_b2_%d" % ex
                    if ex == 0:
                        B.do("ve", "dv", ["p" + pn, "pv", "gb"], ["accf"],
                             lambda e, pn=pn, mt=mt, bn=bn: e.scalar_tensor_tensor(
                                 sb["accf"][:, mt, :], ps[pn][:, 0:G],
                                 sb["pv"][:, mt, PV[bn]:PV[bn] + 1], sb["gb"][:, :],
                                 ALU.add, ALU.mult))
                    else:
                        B.do("ve", "dv", ["p" + pn, "pv", "gb"], ["sc1"],
                             lambda e, pn=pn, mt=mt, bn=bn: e.scalar_tensor_tensor(
                                 sb["sc1"][:, :], ps[pn][:, 0:G],
                                 sb["pv"][:, mt, PV[bn]:PV[bn] + 1], sb["gb"][:, :],
                                 ALU.add, ALU.mult))
                        B.do("ve", "dv", ["sc1", "accf"], ["accf"],
                             lambda e, mt=mt: e.tensor_add(sb["accf"][:, mt, :],
                                                           sb["accf"][:, mt, :],
                                                           sb["sc1"][:, :]))
            if g == 0:
                self.tap(B, "acc_fin", "accf", sb["accf"][:], [128, KD, G])
            for mt in range(KD):
                B.do("ve", "dv", ["accf", "xs"], ["xs"],
                     lambda e, mt=mt, g0=g0: e.tensor_add(sb["xs"][:, mt, g0:g0 + G],
                                                          sb["xs"][:, mt, g0:g0 + G],
                                                          sb["accf"][:, mt, :]))
            if self.gw:
                B.dma("sy", ["xs"], [],
                      lambda e, gr=gr: e.dma_start(
                          out=io[dst_name].rearrange("(k p) t -> p k t", p=128)[:, :, gr:gr + G],
                          in_=sb["xs"][:]))



    def alloc_l0f_fused(self):
        s = self.sbuf
        if "w1a" not in self.sb:
            s("w1a", [128, KD, 128], BF16)
            s("w1b", [128, KD, 128], BF16)
        if "w2a" not in self.sb:
            s("w2a", [128, 32, 128], BF16)

    def emit_reload_wqkv(self, B):
        self.emit_build_omegas(B)
        sb = self.sb
        B.dma("sy", [], ["wqkv"],
              lambda e: e.dma_start(out=sb["wqkv"][:], in_=self.io["scqkv"]))
        B.dma("sy", [], ["wbig"],
              lambda e: e.dma_start(out=sb["wbig"][:], in_=self.io["scwo"]))


def build_l1a_launch(spine=False):
    mk = MK("l1a", gw=True)
    nc = mk.nc
    mk.dram_in("xmn", [D, TL])
    if spine:
        mk.dram_in("xsp", [D, TL])
    mk.dram_in("role", [1, 1])
    mk.dram_in("rolecol", [128, 1])
    mk.dram_in("scolh", [128, 1])
    mk.dram_in("quads", [2, 2])
    mk.dram_in("pvecT", [D, NV])
    for k in ("tm_w1", "td_w1", "td_w2", "Wr", "Wk", "Wv", "Wg", "Wo",
              "Wk_ffn", "Wv_ffn", "Wr_ffn"):
        mk.dram_in(k, list(L1A_W_SHAPES[k]))
    mk.dram_in("tm_w2", [5, 32, D])
    mk.dram_in("lnxgrow", [128, D])
    mk.dram_in("lnxbrow", [128, D])
    mk.alloc_predump(moe=False, l0f=False, l1a=True)
    mk.dram_out("x3mn", [D, TL])
    mk.alloc_common(l0a=False)
    mk.alloc_l1a()
    if "w1a" not in mk.sb:
        mk.sbuf("w1a", [128, KD, 128], BF16)
        mk.sbuf("w1b", [128, KD, 128], BF16)
    if "w2a" not in mk.sb:
        mk.sbuf("w2a", [128, 32, 128], BF16)
    with nc.body("prog", valid_engines=[ET.Pool, ET.Activation, ET.PE, ET.DVE, ET.SP]) as _:
        B = BB(nc)
        B.conflict("xsn", "vtm")
        mk.emit_setup(B, l0a=False)
        mk.emit_l1a_setup(B)
        mk.emit_predump(moe=False, l0f=False, l1a=True)
        if spine:
            mk.emit_l1a(B, zero_state=True, mask_by_role=False, src_name="xsp", dst_name=None)
            mk.emit_l1a(B, zero_state=False, mask_by_role=True, src_name="xmn", dst_name="x3mn")
        else:
            mk.emit_l1a(B, zero_state=True, mask_by_role=False, src_name="xmn", dst_name="x3mn")
        B.fence_all("sy")
    return mk


def run_l1a_test(p):
    h = prep_host(p)
    mk = build_l1a_launch()
    in_maps = []
    for c in range(8):
        b, half = c // 2, c % 2
        m = {"xmn": h["x_fm"][(b, half)], "role": np.array([[float(half)]], np.float32),
             "rolecol": np.full((128, 1), float(half), np.float32), "scolh": h["scolh"],
             "quads": h["quads"], "pvecT": h["pvecT"], "tm_w2": h["tm_w2"],
             "lnxgrow": h["lnxgrow"], "lnxbrow": h["lnxbrow"]}
        for k in ("tm_w1", "td_w1", "td_w2", "Wr", "Wk", "Wv", "Wg", "Wo",
                  "Wk_ffn", "Wv_ffn", "Wr_ffn"):
            m[k] = h[k]
        in_maps.append(m)
    return run_bass_kernel_spmd(mk.nc, in_maps, core_ids=list(range(8)))


def build_l1f_launch(dbg=False):
    mk = MK("l1f", gw=True, dbg=dbg)
    nc = mk.nc
    mk.dram_in("xmn", [D, TL])
    mk.dram_in("role", [1, 1])
    mk.dram_in("rolecol", [128, 1])
    mk.dram_in("scolh", [128, 1])
    mk.dram_in("quads", [2, 2])
    mk.dram_in("pvecT", [D, NV])
    mk.dram_in("rwT", [D, 8])
    mk.dram_in("rbL", [128, 8])
    mk.dram_in("m_b1L", [128, 512])
    mk.io["m_w1T"] = [mk.nc.dram_tensor("m_w1T_%d" % ex, [64, 128, KD, 128], F32,
                                        kind="ExternalInput").ap() for ex in range(NE)]
    mk.io["m_w2T"] = [mk.nc.dram_tensor("m_w2T_%d" % ex, [KD, 128, 32, 128], F32,
                                        kind="ExternalInput").ap() for ex in range(NE)]
    mk.alloc_predump(moe=True, l0f=False)
    mk.dram_out("x4mn", [D, TL])
    mk.alloc_common(l0a=False)
    if "wbig" not in mk.sb:
        mk.sbuf("wbig", [128, 32, 128], BF16)
    mk.alloc_l1f()
    with nc.body("prog", valid_engines=[ET.Pool, ET.Activation, ET.PE, ET.DVE, ET.SP]) as _:
        B = BB(nc)
        mk.emit_setup(B, l0a=False)
        mk.emit_l1f_setup(B)
        mk.emit_predump(moe=True, l0f=False)
        mk.emit_l1f(B, src_name="xmn", dst_name="x4mn")
        B.fence_all("sy")
    return mk


def run_l1f_test(p, dbg=False):
    h = prep_host(p)
    mk = build_l1f_launch(dbg)
    in_maps = []
    for c in range(8):
        b, half = c // 2, c % 2
        m = {"xmn": h["x_fm"][(b, half)], "role": np.array([[float(half)]], np.float32),
             "rolecol": np.full((128, 1), float(half), np.float32), "scolh": h["scolh"],
             "quads": h["quads"], "pvecT": h["pvecT"], "rwT": h["rwT"],
             "rbL": h["rbL"], "m_b1L": h["m_b1L"]}
        for ex in range(NE):
            m["m_w1T_%d" % ex] = h["m_w1T"][ex]
            m["m_w2T_%d" % ex] = h["m_w2T"][ex]
        in_maps.append(m)
    return run_bass_kernel_spmd(mk.nc, in_maps, core_ids=list(range(8)))



def build_l0f_launch(spine=False):
    mk = MK("l0f")
    nc = mk.nc
    mk.dram_in("xmn", [D, TL])
    mk.dram_in("role", [1, 1])
    mk.dram_in("rolecol", [128, 1])
    mk.dram_in("scolh", [128, 1])
    mk.dram_in("quads", [2, 2])
    mk.dram_in("pvecT", [D, NV])
    mk.dram_in("w1f0", [32, 128, KD, 128])
    mk.dram_in("w2f0", [KD, 128, 32, 128])
    mk.alloc_predump(moe=False, l0f=True)
    mk.dram_out("x2mn", [D, TL])
    if spine:
        mk.dram_in("xsp", [D, TL])
        mk.dram_out("x2sp", [D, TL])
    mk.alloc_common(l0a=False)
    mk.alloc_l0f()
    with nc.body("prog", valid_engines=[ET.Pool, ET.Activation, ET.PE, ET.DVE, ET.SP]) as _:
        B = BB(nc)
        mk.emit_setup(B, l0a=False)
        mk.emit_predump(moe=False, l0f=True)
        if spine:
            mk.emit_load_stream(B, "xsp")
            mk.emit_l0f(B)
            mk.emit_store_stream(B, "x2sp")
        mk.emit_load_stream(B, "xmn")
        mk.emit_l0f(B)
        mk.emit_store_stream(B, "x2mn")
        B.fence_all("sy")
    return mk


def run_l0f_test(p):
    h = prep_host(p)
    mk = build_l0f_launch()
    in_maps = []
    for c in range(8):
        b, half = c // 2, c % 2
        m = {"xmn": h["x_fm"][(b, half)], "role": np.array([[float(half)]], np.float32),
             "rolecol": np.full((128, 1), float(half), np.float32), "scolh": h["scolh"],
             "quads": h["quads"], "pvecT": h["pvecT"],
             "w1f0": h["w1f0"], "w2f0": h["w2f0"]}
        in_maps.append(m)
    return run_bass_kernel_spmd(mk.nc, in_maps, core_ids=list(range(8)))




def build_l0a_launch(spine_and_main, dbg=False):
    mk = MK("l0a", dbg=dbg)
    nc = mk.nc
    mk.dram_in("xmn", [D, TL])
    if spine_and_main:
        mk.dram_in("xsp", [D, TL])
    mk.dram_in("role", [1, 1])
    mk.dram_in("rolecol", [128, 1])
    mk.dram_in("scolh", [128, 1])
    mk.dram_in("wqkvT", [128, KD, 3 * D])
    mk.dram_in("woT", [128, KD, D])
    mk.dram_in("omegaT", [64, 1024])
    mk.dram_in("quads", [2, 2])
    mk.dram_in("bqkv", [3, 1024])
    mk.dram_in("pvecT", [D, NV])
    mk.alloc_predump_l0a()
    mk.dram_out("x1mn", [D, TL])
    if spine_and_main:
        mk.dram_out("x1sp", [D, TL])
    mk.alloc_common()
    with nc.body("prog", valid_engines=[ET.Pool, ET.Activation, ET.PE, ET.DVE, ET.SP]) as _:
        B = BB(nc)
        mk.emit_predump_l0a()
        mk.emit_setup(B)
        if spine_and_main:
            mk.emit_load_stream(B, "xsp")
            mk.emit_l0a(B, zero_state=True, mask_state_by_role=False)
            mk.emit_store_stream(B, "x1sp")
            mk.emit_load_stream(B, "xmn")
            mk.emit_l0a(B, zero_state=False, mask_state_by_role=True)
            mk.emit_store_stream(B, "x1mn")
        else:
            mk.emit_load_stream(B, "xmn")
            mk.emit_l0a(B, zero_state=True, mask_state_by_role=False)
            mk.emit_store_stream(B, "x1mn")
        B.fence_all("sy")
    return mk


def run_l0a_test(p, spine_and_main=False, dbg=False):
    h = prep_host(p)
    mk = build_l0a_launch(spine_and_main, dbg=dbg)
    in_maps = []
    for c in range(8):
        b, half = c // 2, c % 2
        m = {"xmn": h["x_fm"][(b, half)], "role": np.array([[float(half)]], np.float32),
             "rolecol": np.full((128, 1), float(half), np.float32), "scolh": h["scolh"],
             "wqkvT": h["wqkvT"], "woT": h["woT"], "omegaT": h["omegaT"],
             "quads": h["quads"], "bqkv": h["bqkv"], "pvecT": h["pvecT"]}
        if spine_and_main:
            m["xsp"] = h["x_fm"][(b, 0)]
        in_maps.append(m)
    res = run_bass_kernel_spmd(mk.nc, in_maps, core_ids=list(range(8)))
    return res




def build_fused(dbg_streams=False):
    mk = MK("fused", gw=True)
    nc = mk.nc
    mk.dram_in("xmn", [D, TL])
    mk.dram_in("xsp", [D, TL])
    mk.dram_in("role", [1, 1])
    mk.dram_in("rolecol", [128, 1])
    mk.dram_in("scolh", [128, 1])
    mk.dram_in("quads", [2, 2])
    mk.dram_in("pvecT", [D, NV])
    mk.dram_in("wqkvT", [128, KD, 3 * D])
    mk.dram_in("woT", [128, KD, D])
    mk.dram_in("omegaT", [64, 1024])
    mk.dram_in("bqkv", [3, 1024])
    mk.dram_in("w1f0", [32, 128, KD, 128])
    mk.dram_in("w2f0", [KD, 128, 32, 128])
    for k in ("tm_w1", "td_w1", "td_w2", "Wr", "Wk", "Wv", "Wg", "Wo",
              "Wk_ffn", "Wv_ffn", "Wr_ffn"):
        mk.dram_in(k, list(L1A_W_SHAPES[k]))
    mk.dram_in("tm_w2", [5, 32, D])
    mk.dram_in("lnxgrow", [128, D])
    mk.dram_in("lnxbrow", [128, D])
    mk.dram_in("rwT", [D, 8])
    mk.dram_in("rbL", [128, 8])
    mk.dram_in("m_b1L", [128, 512])
    mk.io["m_w1T"] = [nc.dram_tensor("m_w1T_%d" % ex, [64, 128, KD, 128], F32,
                                     kind="ExternalInput").ap() for ex in range(NE)]
    mk.io["m_w2T"] = [nc.dram_tensor("m_w2T_%d" % ex, [KD, 128, 32, 128], F32,
                                     kind="ExternalInput").ap() for ex in range(NE)]
    mk.alloc_predump(moe=True, l0f=True, l1a=True)
    mk.alloc_predump_l0a()
    mk.dram_out("x4mn", [D, TL])
    kind = "ExternalOutput" if dbg_streams else "Internal"
    mk.io["s1"] = nc.dram_tensor("s1", [D, TL], F32, kind=kind).ap()
    mk.io["s2"] = nc.dram_tensor("s2", [D, TL], F32, kind=kind).ap()
    mk.io["s3"] = nc.dram_tensor("s3", [D, TL], F32, kind=kind).ap()

    mk.alloc_common(l0a=True)
    addr = nc.lookup_mls(mk.sb["wqkv"]).memorylocations[0].addr
    mk.sb["hbuf"] = nc.alloc_sbuf_tensor_at("hbuf", [128, 32, G], BF16, offset=addr)
    mk.sb["accf"] = nc.alloc_sbuf_tensor_at("accf", [128, KD, G], F32, offset=addr + 32768)
    mk.sbuf("w1a", [128, KD, 128], BF16)
    mk.sbuf("w1b", [128, KD, 128], BF16)
    mk.sbuf("w2a", [128, 32, 128], BF16)
    addr2 = nc.lookup_mls(mk.sb["w2a"]).memorylocations[0].addr
    mk.sb["gtm"] = nc.alloc_sbuf_tensor_at("gtm", [128, NCH, 1024], BF16, offset=addr2)
    aA = nc.lookup_mls(mk.sb["omA"]).memorylocations[0].addr
    aB = nc.lookup_mls(mk.sb["omB"]).memorylocations[0].addr
    mk.sb["lnxg_r"] = nc.alloc_sbuf_tensor_at("lnxg_r", [128, 1024], BF16, offset=aA)
    mk.sb["lnxb_r"] = nc.alloc_sbuf_tensor_at("lnxb_r", [128, 1024], BF16, offset=aB)
    mk.alloc_l1a()
    mk.alloc_l0f_fused()
    aK = nc.lookup_mls(mk.sb["kft"]).memorylocations[0].addr
    aO = nc.lookup_mls(mk.sb["otm"]).memorylocations[0].addr
    aL = nc.lookup_mls(mk.sb["loraf"]).memorylocations[0].addr
    mk.sb["gater"] = nc.alloc_sbuf_tensor_at("gater", [1, G], F32, offset=aK)
    mk.sb["gb"] = nc.alloc_sbuf_tensor_at("gb", [128, G], BF16, offset=aO)
    mk.sb["b1s"] = nc.alloc_sbuf_tensor_at("b1s", [128, 64, 8], F32, offset=aL)
    aV = nc.lookup_mls(mk.sb["vtm"]).memorylocations[0].addr
    for _i, _nm in enumerate(("w1c", "w1d", "w1e", "w1f")):
        mk.sb[_nm] = nc.alloc_sbuf_tensor_at(_nm, [128, KD, 128], BF16,
                                             offset=aV + _i * 2048)
    mk.alloc_l1f()
    with nc.body("prog", valid_engines=[ET.Pool, ET.Activation, ET.PE, ET.DVE, ET.SP]) as _:
        B = BB(nc)
        B.conflict("hbuf", "wqkv")
        B.conflict("accf", "wqkv")
        B.conflict("gtm", "w2a")
        B.conflict("lnxg_r", "omA")
        B.conflict("lnxb_r", "omB")
        B.conflict("gater", "kft")
        B.conflict("gb", "otm")
        B.conflict("b1s", "loraf")
        for _nm in ("w1c", "w1d", "w1e", "w1f"):
            B.conflict(_nm, "vtm")
            B.conflict(_nm, "xsn")
        B.conflict("xsn", "vtm")
        with nc.named_scope("setup"):
            mk.emit_predump_l0a()
            mk.emit_setup(B, l0a=True)
            mk.emit_l1a_setup(B)
            mk.emit_l1f_setup(B)
            mk.emit_predump(moe=False, l0f=True, l1a=True)
        with nc.named_scope("sL0A"):
            mk.emit_l0a(B, zero_state=True, mask_state_by_role=False,
                        src_name="xsp", dst_name="s1")
        with nc.named_scope("sL0F"):
            mk.emit_l0f(B, src_name="s1", dst_name="s2")
        with nc.named_scope("sL1A"):
            mk.emit_l1a(B, zero_state=True, mask_by_role=False, src_name="s2", dst_name=None,
                        reduced=True)
        with nc.named_scope("reload"):
            mk.emit_reload_wqkv(B)
            mk.emit_predump(moe=True, l0f=False, l1a=False)
        with nc.named_scope("mL0A"):
            mk.emit_l0a(B, zero_state=False, mask_state_by_role=True,
                        src_name="xmn", dst_name="s1")
        with nc.named_scope("mL0F"):
            mk.emit_l0f(B, src_name="s1", dst_name="s2")
        with nc.named_scope("mL1A"):
            mk.emit_l1a(B, zero_state=False, mask_by_role=True, src_name="s2", dst_name="s3")
        with nc.named_scope("mL1F"):
            mk.emit_l1f(B, src_name="s3", dst_name="x4mn")
        B.fence_all("sy")
    return mk


_PROGS = {}
FUSED = True


def _get_progs():
    if "p" not in _PROGS:
        _PROGS["p"] = (build_l0a_launch(True), build_l0f_launch(True),
                       build_l1a_launch(True), build_l1f_launch())
    return _PROGS["p"]


def _get_fused():
    if "f" not in _PROGS:
        _PROGS["f"] = build_fused()
    return _PROGS["f"]


def kernel(**inputs):
    h = prep_host(inputs)
    if FUSED:
        mk = _get_fused()
        im = []
        for c in range(8):
            b, half = c // 2, c % 2
            m = {"xmn": h["x_fm"][(b, half)], "xsp": h["x_fm"][(b, 0)],
                 "role": np.array([[float(half)]], np.float32),
                 "rolecol": np.full((128, 1), float(half), np.float32),
                 "scolh": h["scolh"], "quads": h["quads"], "pvecT": h["pvecT"],
                 "wqkvT": h["wqkvT"], "woT": h["woT"], "omegaT": h["omegaT"],
                 "bqkv": h["bqkv"], "w1f0": h["w1f0"], "w2f0": h["w2f0"],
                 "tm_w2": h["tm_w2"], "lnxgrow": h["lnxgrow"], "lnxbrow": h["lnxbrow"],
                 "rwT": h["rwT"], "rbL": h["rbL"], "m_b1L": h["m_b1L"]}
            for k in ("tm_w1", "td_w1", "td_w2", "Wr", "Wk", "Wv", "Wg", "Wo",
                      "Wk_ffn", "Wv_ffn", "Wr_ffn"):
                m[k] = h[k]
            for ex in range(NE):
                m["m_w1T_%d" % ex] = h["m_w1T"][ex]
                m["m_w2T_%d" % ex] = h["m_w2T"][ex]
            im.append(m)
        rr = run_bass_kernel_spmd(mk.nc, im, core_ids=list(range(8))).results
        out = np.zeros((4, 4096, 1024), np.float32)
        for c in range(8):
            b, half = c // 2, c % 2
            out[b, half * 2048:(half + 1) * 2048] = rr[c]["x4mn"].T
        return out
    mka, mkf0, mk1a, mk1f = _get_progs()
    base = []
    for c in range(8):
        b, half = c // 2, c % 2
        base.append({"role": np.array([[float(half)]], np.float32),
                     "rolecol": np.full((128, 1), float(half), np.float32),
                     "scolh": h["scolh"], "quads": h["quads"], "pvecT": h["pvecT"]})
    im = []
    for c in range(8):
        b, half = c // 2, c % 2
        m = dict(base[c])
        m.update({"xmn": h["x_fm"][(b, half)], "xsp": h["x_fm"][(b, 0)],
                  "wqkvT": h["wqkvT"], "woT": h["woT"], "omegaT": h["omegaT"],
                  "bqkv": h["bqkv"]})
        im.append(m)
    r1 = run_bass_kernel_spmd(mka.nc, im, core_ids=list(range(8))).results
    im = []
    for c in range(8):
        m = dict(base[c])
        m.update({"xmn": r1[c]["x1mn"], "xsp": r1[c]["x1sp"],
                  "w1f0": h["w1f0"], "w2f0": h["w2f0"]})
        im.append(m)
    r2 = run_bass_kernel_spmd(mkf0.nc, im, core_ids=list(range(8))).results
    im = []
    for c in range(8):
        m = dict(base[c])
        m.update({"xmn": r2[c]["x2mn"], "xsp": r2[c]["x2sp"], "tm_w2": h["tm_w2"],
                  "lnxgrow": h["lnxgrow"], "lnxbrow": h["lnxbrow"]})
        for k in ("tm_w1", "td_w1", "td_w2", "Wr", "Wk", "Wv", "Wg", "Wo",
                  "Wk_ffn", "Wv_ffn", "Wr_ffn"):
            m[k] = h[k]
        im.append(m)
    r3 = run_bass_kernel_spmd(mk1a.nc, im, core_ids=list(range(8))).results
    im = []
    for c in range(8):
        m = dict(base[c])
        m.update({"xmn": r3[c]["x3mn"], "rwT": h["rwT"], "rbL": h["rbL"],
                  "m_b1L": h["m_b1L"]})
        for ex in range(NE):
            m["m_w1T_%d" % ex] = h["m_w1T"][ex]
            m["m_w2T_%d" % ex] = h["m_w2T"][ex]
        im.append(m)
    r4 = run_bass_kernel_spmd(mk1f.nc, im, core_ids=list(range(8))).results
    out = np.zeros((4, 4096, 1024), np.float32)
    for c in range(8):
        b, half = c // 2, c % 2
        out[b, half * 2048:(half + 1) * 2048] = r4[c]["x4mn"].T
    return out





# version 15
# speedup vs baseline: 1.0366x; 1.0252x over previous
import numpy as np
import concourse.bass as bass
import concourse.mybir as mybir
from concourse.bass_utils import run_bass_kernel_spmd

F32 = mybir.dt.float32
BF16 = mybir.dt.bfloat16
AF = mybir.ActivationFunctionType
ALU = mybir.AluOpType
ET = mybir.EngineType

D = 1024
KD = 8
TL = 2048
G = 512
NGRP = TL // G
L = 128
NCH = G // L
NH = 16
HD = 64
DFF0 = 4096
FK = 3584
DFF1 = 4096
NE = 8

INV_SQRT32 = float(1.0 / np.sqrt(32.0))


ENG_OF = {"te": "tensor", "ve": "vector", "se": "scalar", "sy": "sync", "gp": "gpsimd"}
NSY, NGP = 8, 6
CN_ENG = {"pe": "te", "dv": "ve", "ac": "se", "po": "gp"}
for _i in range(NSY):
    CN_ENG[f"da{_i}"] = "__dma__"
for _i in range(NGP):
    CN_ENG[f"db{_i}"] = "__dma__"


class Ctr:
    def __init__(self, nc, name):
        self.h = nc.alloc_semaphore(name)
        self.n = 0
        self.name = name


FULL_SYNC = False
FS_CTRS = None
SAME_ENGINE_SKIP = True


class BB:

    def __init__(self, nc):
        self.nc = nc
        self.c = {k: Ctr(nc, "sem_" + k) for k in ("pe", "dv", "ac", "po")}
        for i in range(NSY):
            self.c[f"da{i}"] = Ctr(nc, f"sem_da{i}")
        for i in range(NGP):
            self.c[f"db{i}"] = Ctr(nc, f"sem_db{i}")
        self._rr = {"sy": 0, "gp": 0}
        self.conflicts = {}
        self.marks = {}
        self.wm = {e: {} for e in ENG_OF}

    def conflict(self, a, b):
        self.conflicts.setdefault(a, set()).add(b)
        self.conflicts.setdefault(b, set()).add(a)

    def eng(self, e):
        return getattr(self.nc, ENG_OF[e])

    def _wait(self, e, ctr_name, val):
        if val <= 0:
            return
        if self.wm[e].get(ctr_name, 0) >= val:
            return
        self.eng(e).wait_ge(self.c[ctr_name].h, val)
        self.wm[e][ctr_name] = val

    def do(self, e, ctr, reads, writes, f, inc=1):
        need = {}
        if FULL_SYNC and (FULL_SYNC is True or e in FULL_SYNC):
            for _cn, _ctr in self.c.items():
                if FS_CTRS is None or _cn in FS_CTRS:
                    need[_cn] = _ctr.n
        for k in reads:
            for kk in (k, *self.conflicts.get(k, ())):
                m = self.marks.get(kk)
                if m and m["w"]:
                    cn, v = m["w"]
                    need[cn] = max(need.get(cn, 0), v)
        for k in writes:
            for kk in (k, *self.conflicts.get(k, ())):
                m = self.marks.get(kk)
                if m:
                    if m["w"]:
                        cn, v = m["w"]
                        need[cn] = max(need.get(cn, 0), v)
                    for cn, v in m["r"].items():
                        need[cn] = max(need.get(cn, 0), v)
        if SAME_ENGINE_SKIP:
            for cn in list(need):
                if CN_ENG[cn] == e:
                    need.pop(cn)
        for cn, v in sorted(need.items()):
            self._wait(e, cn, v)
        ins = f(self.eng(e))
        assert ins is not None, "emitter must return the last instruction"
        ins.then_inc(self.c[ctr].h, inc)
        self.c[ctr].n += inc
        cnt = self.c[ctr].n
        for k in writes:
            self.marks[k] = {"w": (ctr, cnt), "r": {}}
        for k in reads:
            m = self.marks.setdefault(k, {"w": None, "r": {}})
            m["r"][ctr] = max(m["r"].get(ctr, 0), cnt)
        return ins

    def dma(self, e, reads, writes, f):
        n = NSY if e == "sy" else NGP
        i = self._rr[e]
        self._rr[e] = (i + 1) % n
        cn = (f"da{i}" if e == "sy" else f"db{i}")
        self._wait(e, cn, self.c[cn].n)
        return self.do(e, cn, reads, writes, f, inc=16)

    def fence_all(self, e):
        for cn, ctr in self.c.items():
            self._wait(e, cn, ctr.n)



def _c(a):
    return np.ascontiguousarray(a)


def prep_host(p):
    h = {}
    _ = np.asarray(p["L0_ffn_Wg"])
    x = np.asarray(p["L0_attn_x"])
    h["x_fm"] = {}
    for b in range(4):
        h["x_fm"][(b, 0)] = _c(x[b, :2048].T)
        h["x_fm"][(b, 1)] = _c(x[b, 2048:].T)
    h["wqkvT"] = _c(np.asarray(p["L0_attn_w_qkv"]).T.reshape(8, 128, 3072)
                    .transpose(1, 0, 2))
    h["woT"] = _c(np.asarray(p["L0_attn_w_out"]).T.reshape(8, 128, 1024)
                  .transpose(1, 0, 2))
    om = np.asarray(p["L0_attn_omega"])
    h["omegaT"] = _c(om.transpose(2, 1, 0, 3).reshape(64, 1024))
    h["quads"] = _c(np.stack([np.asarray(p["L0_attn_quad_nodes"]),
                              np.asarray(p["L0_attn_quad_weights"])], 1))
    h["bqkv"] = _c(np.asarray(p["L0_attn_b_qkv"]).reshape(3, 1024))
    h["w1f0"] = _c(np.asarray(p["L0_ffn_w1"]).reshape(8, 128, 32, 128)
                   .transpose(2, 1, 0, 3))
    h["w2f0"] = _c(np.asarray(p["L0_ffn_w2"]).reshape(32, 128, 8, 128)
                   .transpose(2, 1, 0, 3))
    for k in ("tm_w1", "td_w1", "td_w2"):
        h[k] = _c(np.asarray(p["L1_attn_" + k]))
    for k in ("Wr", "Wk", "Wo", "Wr_ffn"):
        h[k] = _c(np.asarray(p["L1_attn_" + k]).reshape(8, 128, 8, 128)
                  .transpose(2, 1, 0, 3))
    for k in ("Wv", "Wg"):
        h[k] = _c(np.asarray(p["L1_attn_" + k]).reshape(8, 128, 1024)
                  .transpose(1, 0, 2))
    h["Wk_ffn"] = _c(np.asarray(p["L1_attn_Wk_ffn"]).reshape(8, 128, 28, 128)
                     .transpose(2, 1, 0, 3))
    h["Wv_ffn"] = _c(np.asarray(p["L1_attn_Wv_ffn"]).reshape(28, 128, 8, 128)
                     .transpose(2, 1, 0, 3))
    h["tm_w2"] = _c(np.asarray(p["L1_attn_tm_w2"]))
    h["rwT"] = _c(np.asarray(p["L1_ffn_router_w"]).T)
    h["m_b1"] = _c(np.asarray(p["L1_ffn_b1"]))
    vecs = []
    for name in PV_ORDER:
        if name == "u_flat":
            vecs.append(np.asarray(p["L1_attn_time_faaaa"]).reshape(1024))
        elif name == "b_out":
            vecs.append(np.asarray(p["L0_attn_b_out"]))
        elif name == "bq":
            vecs.append(np.asarray(p["L0_attn_b_qkv"])[0:1024])
        elif name == "bk":
            vecs.append(np.asarray(p["L0_attn_b_qkv"])[1024:2048])
        elif name.startswith("m_b2_"):
            e = int(name.split("_")[-1])
            vecs.append(np.asarray(p["L1_ffn_b2"])[e])
        elif name == "router_b8":
            rb = np.zeros(1024, np.float32)
            rb[:8] = np.asarray(p["L1_ffn_router_b"])
            vecs.append(rb)
        else:
            vecs.append(np.asarray(p["L1_attn_" + name]))
    h["pvecT"] = _c(np.stack(vecs, axis=1).astype(np.float32))
    h["rbL"] = _c(np.tile(np.asarray(p["L1_ffn_router_b"])[None, :], (128, 1)).astype(np.float32))
    h["m_b1L"] = _c(np.asarray(p["L1_ffn_b1"]).reshape(8, 64, 128).transpose(2, 1, 0).reshape(128, 512))
    h["m_w1T"] = [_c(np.asarray(p["L1_ffn_w1"])[ex].T.reshape(8, 128, 64, 128)
                     .transpose(2, 1, 0, 3)) for ex in range(8)]
    h["m_w2T"] = [_c(np.asarray(p["L1_ffn_w2"])[ex].T.reshape(32, 128, 8, 128)
                     .transpose(2, 1, 0, 3)) for ex in range(8)]
    h["lnxgrow"] = _c(np.tile(np.asarray(p["L1_attn_lnx_g"])[None, :], (128, 1)))
    h["lnxbrow"] = _c(np.tile(np.asarray(p["L1_attn_lnx_b"])[None, :], (128, 1)))
    qn = np.asarray(p["L0_attn_quad_nodes"])
    h["scolh"] = _c(qn[(np.arange(128) // 32) % 2].reshape(128, 1).astype(np.float32))
    return h


L1A_W_SHAPES = {
    "tm_w1": (D, 160), "td_w1": (D, 64), "td_w2": (64, D),
    "Wr": (KD, 128, KD, 128), "Wk": (KD, 128, KD, 128),
    "Wo": (KD, 128, KD, 128), "Wr_ffn": (KD, 128, KD, 128),
    "Wv": (128, KD, 1024), "Wg": (128, KD, 1024),
    "Wk_ffn": (28, 128, KD, 128), "Wv_ffn": (KD, 128, 28, 128),
}

PV_ORDER = [
    "b_out", "bq", "bk",
    "ln0_g", "ln0_b", "ln1_g", "ln1_b", "ln2_g", "ln2_b", "lnx_g", "lnx_b",
    "time_maa_x", "time_maa_w", "time_maa_k", "time_maa_v", "time_maa_r", "time_maa_g",
    "time_decay", "u_flat", "ffn_maa_k", "ffn_maa_r",
    "m_b2_0", "m_b2_1", "m_b2_2", "m_b2_3", "m_b2_4", "m_b2_5", "m_b2_6", "m_b2_7",
    "router_b8",
]
PV = {n: i for i, n in enumerate(PV_ORDER)}
NV = len(PV_ORDER)



def make_consts(nc):
    c = {}
    tri_incl = np.triu(np.ones((L, L), np.float32))
    tri_strict = np.triu(np.ones((L, L), np.float32), 1)
    c["m_incl"] = nc.inline_tensor(tri_incl, "m_incl")
    c["m_strict"] = nc.inline_tensor(tri_strict, "m_strict")
    c["ident_b"] = nc.inline_tensor(np.eye(128, dtype=np.float32), "ident_b")
    c["ident_f"] = nc.inline_tensor(np.eye(128, dtype=np.float32), "ident_f")
    hsel = np.zeros((128, 2), np.float32)
    hsel[0:64, 0] = 1.0
    hsel[64:128, 1] = 1.0
    c["hsel"] = nc.inline_tensor(hsel, "hsel")
    c["hselT"] = nc.inline_tensor(_c(hsel.T), "hselT")
    rsel = np.zeros((2, 128), np.float32)
    for pp in range(128):
        rsel[(pp // 32) % 2, pp] = 1.0
    c["rsel"] = nc.inline_tensor(rsel, "rsel")
    rselw = np.zeros((2, 1024), np.float32)
    for j in range(1024):
        rselw[(j // 32) % 2, j] = 1.0
    c["rselw"] = nc.inline_tensor(rselw, "rselw")
    c["ones_col"] = nc.inline_tensor(np.ones((128, 1), np.float32), "ones_col")
    c["ones_row"] = nc.inline_tensor(np.ones((1, 128), np.float32), "ones_row")
    return c



class MK:

    def __init__(self, launch, dbg=False, gw=False):
        self.launch = launch
        self.dbg = dbg
        self.gw = gw
        nc = bass.Bass("TRN2", target_bir_lowering=False, detect_race_conditions=False)
        self.nc = nc
        self.io = {}
        self.sb = {}
        self.ps = {}
        self.consts = make_consts(nc)

    def tap(self, B, name, key, ap, shape):
        if not self.dbg:
            return
        if isinstance(self.dbg, (set, frozenset)) and name not in self.dbg:
            return
        out = self.nc.dram_tensor("dbg_" + name, list(shape), F32, kind="ExternalOutput").ap()
        B.dma("gp", [key], [], lambda e: e.dma_start(out=out, in_=ap))

    def dram_in(self, name, shape):
        self.io[name] = self.nc.dram_tensor(name, list(shape), F32, kind="ExternalInput").ap()
        return self.io[name]

    def dram_out(self, name, shape):
        self.io[name] = self.nc.dram_tensor(name, list(shape), F32, kind="ExternalOutput").ap()
        return self.io[name]

    def sbuf(self, name, shape, dtype):
        t = self.nc.alloc_sbuf_tensor(name, list(shape), dtype)
        self.sb[name] = t
        return t

    def psum(self, name, shape, dtype=F32):
        t = self.nc.alloc_psum_tensor(name, list(shape), dtype)
        self.ps[name] = t
        return t

    def alloc_common(self, l0a=True):
        s = self.sbuf
        self.has_l0a = l0a
        s("xs", [128, KD, G if self.gw else TL], F32)
        s("xlnb", [128, KD, G], BF16)
        s("sq16", [128, G], BF16)
        s("xb16", [128, G], BF16)
        s("sc0", [128, G], F32)
        s("sc1", [128, G], F32)
        s("row0", [2, 1024], F32)
        if l0a:
            s("qf", [128, KD, G], BF16)
            s("kf", [128, KD, G], BF16)
            s("vtm", [128, NCH, 1024], BF16)
            s("kft", [128, KD, L], BF16)
            s("otm", [128, 1024], BF16)
        if l0a:
            s("S0", [128, 512], F32)
            s("S0b", [128, 512], BF16)
            s("ks", [128, 16], F32)
            s("ksb", [128, 16], BF16)
            s("rden", [128, 16], F32)
            s("denlog", [128, 256], F32)
            s("am", [128, L], BF16)
            s("am2", [128, L], BF16)
        s("row1", [1, 1024], F32)
        s("pv", [128, KD, NV], F32)
        s("quad", [2, 2], F32)
        s("qtmp", [2, 2], F32)
        s("scol", [128, 1], F32)
        s("rolec", [128, 1], F32)
        if l0a:
            s("omA", [128, 1024], BF16)
            s("omB", [128, 1024], BF16)
            s("bvrow", [128, 1024], BF16)
        if l0a:
            s("mI", [128, L], F32)
            s("idf", [128, 128], F32)
            s("hselT_f", [2, 128], F32)
        s("mS", [128, L], F32)
        s("idb", [128, 128], BF16)
        s("hsel_b", [128, 2], BF16)
        s("onecol_b", [128, 1], BF16)
        s("onerow_f", [1, 128], F32)
        s("onerow_b", [1, 128], BF16)
        s("epscol", [128, 1], F32)
        s("negone", [128, 1], F32)
        s("neghalf", [128, 1], F32)
        s("onecol_f", [128, 1], F32)
        if l0a:
            s("wqkv", [128, KD, 3072], BF16)
            if "wbig" not in self.sb:
                s("wbig", [128, KD, 1024], BF16)
        self.psum("P0", [128, 512])
        self.psum("P1", [128, 512])
        self.psum("P2", [128, 512])
        self.psum("P3", [128, 512])
        self.psum("P4", [128, 512])
        self.psum("P5", [128, 512])
        self.psum("P6", [128, 512])
        self.psum("PT", [128, 1024], BF16)

    def emit_setup(self, B, l0a=True):
        nc, sb, c = self.nc, self.sb, self.consts
        if l0a:
            B.dma("sy", [], ["mI"], lambda e: e.dma_start(out=sb["mI"][:], in_=c["m_incl"].ap()))
            B.dma("sy", [], ["idf"], lambda e: e.dma_start(out=sb["idf"][:], in_=c["ident_f"].ap()))
            B.dma("sy", [], ["hselT_f"], lambda e: e.dma_start(out=sb["hselT_f"][:], in_=c["hselT"].ap()))
        B.dma("sy", [], ["mS"], lambda e: e.dma_start(out=sb["mS"][:], in_=c["m_strict"].ap()))
        B.dma("gp", [], ["idb"], lambda e: e.dma_start(out=sb["idb"][:], in_=c["ident_b"].ap()))
        B.dma("gp", [], ["hsel_b"], lambda e: e.dma_start(out=sb["hsel_b"][:], in_=c["hsel"].ap()))
        B.dma("gp", [], ["onecol_b"], lambda e: e.dma_start(out=sb["onecol_b"][:], in_=c["ones_col"].ap()))
        B.dma("sy", [], ["onerow_f"], lambda e: e.dma_start(out=sb["onerow_f"][:], in_=c["ones_row"].ap()))
        B.dma("gp", [], ["onerow_b"], lambda e: e.dma_start(out=sb["onerow_b"][:], in_=c["ones_row"].ap()))
        B.do("ve", "dv", [], ["epscol"], lambda e: e.memset(sb["epscol"][:, :], 1e-5))
        B.do("ve", "dv", [], ["negone"], lambda e: e.memset(sb["negone"][:, :], -1.0))
        B.do("ve", "dv", [], ["neghalf"], lambda e: e.memset(sb["neghalf"][:, :], -0.5))
        B.do("ve", "dv", [], ["onecol_f"], lambda e: e.memset(sb["onecol_f"][:, :], 1.0))
        B.dma("sy", [], ["pv"],
             lambda e: e.dma_start(out=sb["pv"][:],
                                   in_=self.io["pvecT"].rearrange("(k p) v -> p k v", p=128)))
        B.dma("sy", [], ["quad"], lambda e: e.dma_start(out=sb["quad"][:], in_=self.io["quads"]))
        B.dma("sy", [], ["rolec"], lambda e: e.dma_start(out=sb["rolec"][:], in_=self.io["rolecol"]))
        B.dma("sy", [], ["scol"], lambda e: e.dma_start(out=sb["scol"][:], in_=self.io["scolh"]))
        if not l0a:
            return
        self.emit_build_omegas(B)
        self.pdm2_gate(B, "sy")
        B.dma("sy", [], ["wqkv"],
             lambda e: e.dma_start(out=sb["wqkv"][:], in_=self.io["scqkv"]))
        B.dma("sy", [], ["wbig"],
             lambda e: e.dma_start(out=sb["wbig"][:], in_=self.io["scwo"]))
        B.dma("sy", [], ["row1"],
             lambda e: e.dma_start(out=sb["row1"][0:1, :], in_=self.io["bqkv"][2:3, :]))
        for half, pn in ((0, "P4"), (1, "P5")):
            B.do("te", "pe", ["row1", "onerow_f"], ["p" + pn],
                 lambda e, half=half, pn=pn: e.matmul(
                     self.ps[pn][:, 0:512], sb["onerow_f"][0:1, :],
                     sb["row1"][0:1, half * 512:(half + 1) * 512], start=True, stop=True))
            B.do("ve", "dv", ["p" + pn], ["bvrow"],
                 lambda e, half=half, pn=pn: e.tensor_copy(
                     sb["bvrow"][:, half * 512:(half + 1) * 512], self.ps[pn][:, 0:512]))

    def emit_build_omegas(self, B):
        nc, sb, c = self.nc, self.sb, self.consts
        B.do("ve", "dv", ["quad"], ["qtmp"],
             lambda e: e.tensor_scalar(sb["qtmp"][:, 0:1], sb["quad"][:, 0:1], 0.0, 2.0, ALU.max, ALU.mult))
        B.do("se", "ac", ["qtmp"], ["qtmp"],
             lambda e: e.activation(sb["qtmp"][:, 0:1], sb["qtmp"][:, 0:1], AF.Sqrt))
        B.do("ve", "dv", ["quad"], ["qtmp"],
             lambda e: e.tensor_scalar(sb["qtmp"][:, 1:2], sb["quad"][:, 1:2], 0.0, None, ALU.max))
        B.do("se", "ac", ["qtmp"], ["qtmp"],
             lambda e: e.activation(sb["qtmp"][:, 1:2], sb["qtmp"][:, 1:2], AF.Sqrt))
        B.do("se", "ac", ["qtmp"], ["qtmp"],
             lambda e: e.activation(sb["qtmp"][:, 1:2], sb["qtmp"][:, 1:2], AF.Sqrt))
        B.do("ve", "dv", ["qtmp"], ["qtmp"],
             lambda e: e.tensor_scalar(sb["qtmp"][:, 1:2], sb["qtmp"][:, 1:2], INV_SQRT32, None, ALU.mult))
        B.dma("sy", [], ["sc0"],
             lambda e: e.dma_start(out=sb["sc0"][0:2, 0:512], in_=c["rselw"].ap()[:, 0:512]))
        B.dma("sy", [], ["sc1"],
             lambda e: e.dma_start(out=sb["sc1"][0:2, 0:512], in_=c["rselw"].ap()[:, 512:1024]))
        for nm, col in (("row0", 0), ("row1", 1)):
            for half in range(2):
                B.do("te", "pe", ["sc0", "sc1", "qtmp"], ["pP3"],
                     lambda e, col=col, half=half: e.matmul(
                         self.ps["P3"][0:1, 0:512],
                         sb["qtmp"][:, col:col + 1],
                         sb["sc0" if half == 0 else "sc1"][0:2, 0:512],
                         start=True, stop=True))
                B.do("ve", "dv", ["pP3"], [nm],
                     lambda e, nm=nm, half=half: e.tensor_copy(
                         sb[nm][0:1, half * 512:(half + 1) * 512], self.ps["P3"][0:1, 0:512]))
        for base in (0, 64):
            B.dma("sy", [], ["sc0"],
                 lambda e, base=base: e.dma_start(out=sb["sc0"][base:base + 64, :],
                                                  in_=self.io["omegaT"][:, 0:512]))
            B.dma("sy", [], ["sc1"],
                 lambda e, base=base: e.dma_start(out=sb["sc1"][base:base + 64, :],
                                                  in_=self.io["omegaT"][:, 512:1024]))
        for nm, rownm in (("omA", "row0"), ("omB", "row1")):
            for half, pn in ((0, "P4"), (1, "P5")):
                B.do("te", "pe", [rownm, "onerow_f"], ["p" + pn],
                     lambda e, half=half, pn=pn, rownm=rownm: e.matmul(
                         self.ps[pn][:, 0:512], sb["onerow_f"][0:1, :],
                         sb[rownm][0:1, half * 512:(half + 1) * 512], start=True, stop=True))
                B.do("ve", "dv", ["p" + pn, "sc0", "sc1"], [nm],
                     lambda e, half=half, pn=pn, nm=nm: e.tensor_mul(
                         sb[nm][:, half * 512:(half + 1) * 512],
                         sb["sc0" if half == 0 else "sc1"][:, 0:512],
                         self.ps[pn][:, 0:512]))
        self.tap(B, "scol", "scol", self.sb["scol"][:], [128, 1])
        self.tap(B, "omAv", "omA", self.sb["omA"][:], [128, 1024])
        self.tap(B, "omBv", "omB", self.sb["omB"][:], [128, 1024])
        self.tap(B, "carow", "row0", self.sb["row0"][0:1, :], [1, 1024])
        self.tap(B, "cbrow", "row1", self.sb["row1"][0:1, :], [1, 1024])

    def emit_load_stream(self, B, src_name):
        sb = self.sb
        B.dma("sy", [], ["xs"],
             lambda e: e.dma_start(out=sb["xs"][:],
                                   in_=self.io[src_name].rearrange("(k p) t -> p k t", p=128)))

    def emit_store_stream(self, B, dst_name):
        sb = self.sb
        B.dma("sy", ["xs"], [],
             lambda e: e.dma_start(out=self.io[dst_name].rearrange("(k p) t -> p k t", p=128),
                                   in_=sb["xs"][:]))

    def emit_ln(self, B, g0, gname="", dst="xlnb", eps_key="epscol", src=None,
                gcol=None, bcol=None):
        sb, ps = self.sb, self.ps
        xs = sb["xs"]
        if src is None:
            src_ap = lambda kd: xs[:, kd, g0:g0 + G]
            src_key = "xs"
        else:
            src_ap = lambda kd: sb[src][:, kd, :]
            src_key = src
        for kd in range(KD):
            if src is None:
                B.do("se", "ac", [src_key], ["xb16"],
                     lambda e, kd=kd: e.activation(sb["xb16"][:, :], src_ap(kd), AF.Copy))
                sum_src, sum_key = sb["xb16"], "xb16"
            else:
                sum_src, sum_key = None, src_key
            B.do("se", "ac", [src_key], ["sq16"],
                 lambda e, kd=kd: e.activation(sb["sq16"][:, :], src_ap(kd), AF.Square))
            B.do("te", "pe", [sum_key, "onecol_b"], ["pP2"],
                 lambda e, kd=kd: e.matmul(ps["P2"][0:1, 0:G], sb["onecol_b"][:, :],
                                           sum_src[:, :] if sum_src is not None else src_ap(kd),
                                           start=(kd == 0), stop=(kd == KD - 1)))
            B.do("te", "pe", ["sq16", "onecol_b"], ["pP3"],
                 lambda e, kd=kd: e.matmul(ps["P3"][0:1, 0:G], sb["onecol_b"][:, :], sb["sq16"][:, :],
                                           start=(kd == 0), stop=(kd == KD - 1)))
        r, r1 = sb["row0"], sb["row1"]
        B.do("ve", "dv", ["pP2"], ["row0"],
             lambda e: e.tensor_scalar(r[0:1, 0:G], ps["P2"][0:1, 0:G], 1.0 / D, None, ALU.mult))
        B.do("ve", "dv", ["pP3"], ["row0"],
             lambda e: e.tensor_scalar(r[0:1, G:2 * G], ps["P3"][0:1, 0:G], 1.0 / D, None, ALU.mult))
        B.do("se", "ac", ["row0"], ["row1"],
             lambda e: e.activation(r1[0:1, 0:G], r[0:1, 0:G], AF.Square))
        B.do("ve", "dv", ["row0", "row1"], ["row1"],
             lambda e: e.tensor_sub(r1[0:1, G:2 * G], r[0:1, G:2 * G], r1[0:1, 0:G]))
        B.do("se", "ac", ["row1", "epscol"], ["row1"],
             lambda e: e.activation(r1[0:1, G:2 * G], r1[0:1, G:2 * G], AF.Ln,
                                    bias=sb[eps_key][0:1, 0:1]))
        B.do("se", "ac", ["row1", "neghalf"], ["row1"],
             lambda e: e.activation(r1[0:1, G:2 * G], r1[0:1, G:2 * G], AF.Exp,
                                    scale=sb["neghalf"][0:1, 0:1]))
        B.do("ve", "dv", ["row1", "row0"], ["row0"],
             lambda e: e.tensor_mul(r[0:1, 0:G], r[0:1, 0:G], r1[0:1, G:2 * G]))
        B.do("te", "pe", ["row1", "onerow_f"], ["pP4"],
             lambda e: e.matmul(ps["P4"][:, 0:G], sb["onerow_f"][0:1, :], r1[0:1, G:2 * G],
                                start=True, stop=True))
        B.do("te", "pe", ["row0", "onerow_f"], ["pP5"],
             lambda e: e.matmul(ps["P5"][:, 0:G], sb["onerow_f"][0:1, :], r[0:1, 0:G],
                                start=True, stop=True))
        for kd in range(KD):
            B.do("ve", "dv", [src_key, "pP4"], ["sc0"],
                 lambda e, kd=kd: e.tensor_mul(sb["sc0"][:, :], src_ap(kd), ps["P4"][:, 0:G]))
            if gcol is None:
                B.do("ve", "dv", ["sc0", "pP5"], [dst],
                     lambda e, kd=kd: e.tensor_sub(sb[dst][:, kd, :], sb["sc0"][:, :], ps["P5"][:, 0:G]))
            else:
                B.do("ve", "dv", ["sc0", "pP5"], ["sc1"],
                     lambda e, kd=kd: e.tensor_sub(sb["sc1"][:, :], sb["sc0"][:, :], ps["P5"][:, 0:G]))
                B.do("ve", "dv", ["sc1", "pv"], [dst],
                     lambda e, kd=kd: e.tensor_scalar(
                         sb[dst][:, kd, :], sb["sc1"][:, :],
                         sb["pv"][:, kd, PV[gcol]:PV[gcol] + 1],
                         sb["pv"][:, kd, PV[bcol]:PV[bcol] + 1], ALU.mult, ALU.add))

    def emit_l0a(self, B, zero_state, mask_state_by_role, add_bias_cols=True,
                 src_name=None, dst_name=None):
        self._gw_io0 = (src_name, dst_name)
        """Full L0 attention block: xs <- xs + attn(LN(xs)). State in S0/ks.
        zero_state: memset states first. mask_state_by_role: multiply states by role col."""
        sb, ps = self.sb, self.ps
        if zero_state:
            B.do("ve", "dv", [], ["S0"], lambda e: e.memset(sb["S0"][:, :], 0.0))
            B.do("ve", "dv", [], ["S0b"], lambda e: e.memset(sb["S0b"][:, :], 0.0))
            B.do("ve", "dv", [], ["ks"], lambda e: e.memset(sb["ks"][:, :], 0.0))
            B.do("ve", "dv", [], ["ksb"], lambda e: e.memset(sb["ksb"][:, :], 0.0))
        if mask_state_by_role:
            B.do("ve", "dv", ["rolec", "S0"], ["S0"],
                 lambda e: e.tensor_scalar(sb["S0"][:, :], sb["S0"][:, :], sb["rolec"][:, 0:1], None, ALU.mult))
            B.do("ve", "dv", ["S0"], ["S0b"], lambda e: e.tensor_copy(sb["S0b"][:, :], sb["S0"][:, :]))
            B.do("ve", "dv", ["rolec", "ks"], ["ks"],
                 lambda e: e.tensor_scalar(sb["ks"][:, :], sb["ks"][:, :], sb["rolec"][:, 0:1], None, ALU.mult))
            B.do("se", "ac", ["ks"], ["ksb"], lambda e: e.copy(sb["ksb"][:, :], sb["ks"][:, :]))

        for g in range(NGRP):
            g0 = 0 if self.gw else g * G
            if self.gw:
                sn, dn = self._gw_io0
                B.dma("sy", [], ["xs"],
                      lambda e, g=g, sn=sn: e.dma_start(
                          out=sb["xs"][:],
                          in_=self.io[sn].rearrange("(k p) t -> p k t", p=128)[:, :, g * G:(g + 1) * G]))
            self.emit_ln(B, g0)
            self._l0a_group(B, g0, add_bias_cols)
            if self.gw and dn is not None:
                B.dma("sy", ["xs"], [],
                      lambda e, g=g, dn=dn: e.dma_start(
                          out=self.io[dn].rearrange("(k p) t -> p k t", p=128)[:, :, g * G:(g + 1) * G],
                          in_=sb["xs"][:]))

    def _l0a_group(self, B, g0, add_bias_cols):
        sb, ps = self.sb, self.ps

        for qk in range(2):
            dstf = "qf" if qk == 0 else "kf"
            for mt in range(KD):
                pn = "P0" if mt % 2 == 0 else "P1"
                wcol0 = qk * 1024 + mt * 128
                B.do("te", "pe", ["wqkv", "xlnb"], ["p" + pn],
                     lambda e, pn=pn, wcol0=wcol0: [
                         e.matmul(ps[pn][:, 0:G], sb["wqkv"][:, kd, wcol0:wcol0 + 128],
                                  sb["xlnb"][:, kd, :], start=(kd == 0), stop=(kd == KD - 1))
                         for kd in range(KD)][-1])
                bcol = PV["bq"] if qk == 0 else PV["bk"]
                B.do("ve", "dv", ["p" + pn, "pv"], ["sc0"],
                     lambda e, pn=pn, mt=mt, bcol=bcol: e.tensor_scalar(
                         sb["sc0"][:, :], ps[pn][:, 0:G],
                         sb["pv"][:, mt, bcol:bcol + 1], None, ALU.add))
                B.do("se", "ac", ["sc0"], ["sq16"],
                     lambda e: e.activation(sb["sq16"][:, :], sb["sc0"][:, :], AF.Square))
                B.do("te", "pe", ["sq16", "hsel_b"], ["pP2"],
                     lambda e: e.matmul(ps["P2"][0:2, 0:G], sb["hsel_b"][:, :], sb["sq16"][:, :],
                                        start=True, stop=True))
                B.do("ve", "dv", ["pP2"], ["row0"],
                     lambda e: e.tensor_scalar(sb["row0"][0:2, 0:G], ps["P2"][0:2, 0:G],
                                               1e-24, None, ALU.max))
                B.do("se", "ac", ["row0"], ["row0"],
                     lambda e: e.activation(sb["row0"][0:2, 0:G], sb["row0"][0:2, 0:G], AF.Ln))
                B.do("se", "ac", ["row0", "neghalf"], ["row0"],
                     lambda e: e.activation(sb["row0"][0:2, 0:G], sb["row0"][0:2, 0:G], AF.Exp,
                                            scale=sb["neghalf"][0:2, 0:1]))
                B.do("te", "pe", ["row0", "hselT_f"], ["pP3"],
                     lambda e: e.matmul(ps["P3"][:, 0:G], sb["hselT_f"][:, :], sb["row0"][0:2, 0:G],
                                        start=True, stop=True))
                B.do("ve", "dv", ["sc0", "pP3"], ["xb16"],
                     lambda e: e.tensor_mul(sb["xb16"][:, :], sb["sc0"][:, :], ps["P3"][:, 0:G]))
                for hh in range(2):
                    h = 2 * mt + hh
                    b0 = hh * 64
                    B.do("te", "pe", ["xb16", "omA"], ["pP4"],
                         lambda e, b0=b0, h=h: e.matmul(
                             ps["P4"][b0:b0 + 64, 0:G], sb["omA"][b0:b0 + 64, h * 64:(h + 1) * 64],
                             sb["xb16"][b0:b0 + 64, :], start=True, stop=True))
                    B.do("te", "pe", ["xb16", "omB"], ["pP5"],
                         lambda e, b0=b0, h=h: e.matmul(
                             ps["P5"][b0:b0 + 64, 0:G], sb["omB"][b0:b0 + 64, h * 64:(h + 1) * 64],
                             sb["xb16"][b0:b0 + 64, :], start=True, stop=True))
                B.do("ve", "dv", ["pP4", "scol"], ["sc1"],
                     lambda e: e.tensor_scalar(sb["sc1"][:, :], ps["P4"][:, 0:G],
                                               sb["scol"][:, 0:1], None, ALU.subtract))
                B.do("ve", "dv", ["sc1"], ["sc1"],
                     lambda e: e.tensor_scalar(sb["sc1"][:, :], sb["sc1"][:, :], -20.0, 20.0,
                                               ALU.max, ALU.min))
                B.do("se", "ac", ["sc1"], ["sc1"],
                     lambda e: e.activation(sb["sc1"][:, :], sb["sc1"][:, :], AF.Exp))
                B.do("se", "ac", ["pP5"], ["sq16"],
                     lambda e: e.activation(sb["sq16"][:, :], ps["P5"][:, 0:G], AF.Square))
                B.do("ve", "dv", ["sc1", "sq16"], [dstf],
                     lambda e, dstf=dstf, mt=mt: e.tensor_mul(sb[dstf][:, mt, :], sb["sc1"][:, :],
                                                              sb["sq16"][:, :]))
                if g0 == 0 and qk == 0 and mt == 0:
                    self.tap(B, "q0", "sc0", sb["sc0"][:], [128, G])
                    self.tap(B, "rn0", "row0", sb["row0"][0:2, 0:G], [2, G])
                    self.tap(B, "zn0", "xb16", sb["xb16"][:], [128, G])
                    self.tap(B, "e0", "sc1", sb["sc1"][:], [128, G])
                    self.tap(B, "sqB0", "sq16", sb["sq16"][:], [128, G])

        if g0 == 0:
            self.tap(B, "xlnb", "xlnb", sb["xlnb"][:], [128, KD, G])
            self.tap(B, "qf", "qf", sb["qf"][:], [128, KD, G])
            self.tap(B, "kf", "kf", sb["kf"][:], [128, KD, G])

        for tc in range(NCH):
            for nb in range(2):
                pn = "P0" if (tc * 2 + nb) % 2 == 0 else "P1"
                B.do("te", "pe", ["xlnb", "wqkv"], ["p" + pn],
                     lambda e, pn=pn, tc=tc, nb=nb: [
                         e.matmul(ps[pn][:, 0:512],
                                  sb["xlnb"][:, kd, tc * L:(tc + 1) * L],
                                  sb["wqkv"][:, kd, 2048 + nb * 512:2048 + (nb + 1) * 512],
                                  start=(kd == 0), stop=(kd == KD - 1))
                         for kd in range(KD)][-1])
                B.do("ve", "dv", ["p" + pn, "bvrow"], ["vtm"],
                     lambda e, pn=pn, tc=tc, nb=nb: e.tensor_add(
                         sb["vtm"][:, tc, nb * 512:(nb + 1) * 512],
                         ps[pn][:, 0:512], sb["bvrow"][:, nb * 512:(nb + 1) * 512]))

        if g0 == 0:
            self.tap(B, "vtm", "vtm", sb["vtm"][:], [128, NCH, 1024])

        for tc in range(NCH):
            t0 = tc * L
            for kd in range(KD):
                B.do("te", "pe", ["kf", "idb"], ["pPT"],
                     lambda e, kd=kd: e.transpose(ps["PT"][:, kd * 128:(kd + 1) * 128],
                                                  sb["kf"][:, kd, t0:t0 + L], sb["idb"][:, :]))
                B.do("ve", "dv", ["pPT"], ["kft"],
                     lambda e, kd=kd: e.tensor_copy(sb["kft"][:, kd, :],
                                                    ps["PT"][:, kd * 128:(kd + 1) * 128]))
            for h in range(NH):
                kd, b0 = h // 2, (h % 2) * 64
                pa = "P4" if h % 2 == 0 else "P5"
                an = "am" if h % 2 == 0 else "am2"
                B.do("te", "pe", ["kf", "qf"], ["p" + pa],
                     lambda e, pa=pa, kd=kd, b0=b0: e.matmul(
                         ps[pa][:, 0:L], sb["kf"][b0:b0 + 64, kd, t0:t0 + L],
                         sb["qf"][b0:b0 + 64, kd, t0:t0 + L], start=True, stop=True))
                B.do("ve", "dv", ["p" + pa, "mI"], [an],
                     lambda e, pa=pa, an=an: e.tensor_mul(sb[an][:, :], ps[pa][:, 0:L],
                                                          sb["mI"][:, :]))
                if g0 == 0 and tc == 0 and h == 0:
                    self.tap(B, "am0", "am", sb["am"][:], [128, L])
                po = "P0" if h < 8 else "P1"
                oc0 = (h % 8) * 64
                B.do("te", "pe", [an, "vtm"], ["p" + po],
                     lambda e, po=po, oc0=oc0, kd=kd, h=h, an=an: e.matmul(
                         ps[po][:, oc0:oc0 + 64], sb[an][:, :],
                         sb["vtm"][:, tc, h * 64:(h + 1) * 64], start=True, stop=False))
                B.do("te", "pe", ["qf", "S0b"], ["p" + po],
                     lambda e, po=po, oc0=oc0, kd=kd, b0=b0, h=h: e.matmul(
                         ps[po][:, oc0:oc0 + 64], sb["qf"][b0:b0 + 64, kd, t0:t0 + L],
                         sb["S0b"][b0:b0 + 64, (h // 2) * 64:(h // 2 + 1) * 64],
                         start=False, stop=True))
                B.do("te", "pe", [an, "onecol_b"], ["pP2"],
                     lambda e, h=h, an=an: e.matmul(ps["P2"][:, h:h + 1], sb[an][:, :],
                                                    sb["onecol_b"][:, :], start=True, stop=False))
                B.do("te", "pe", ["qf", "ksb"], ["pP2"],
                     lambda e, h=h, kd=kd, b0=b0: e.matmul(
                         ps["P2"][:, h:h + 1], sb["qf"][b0:b0 + 64, kd, t0:t0 + L],
                         sb["ksb"][b0:b0 + 64, h // 2:h // 2 + 1], start=False, stop=True))
                B.do("te", "pe", ["kft", "vtm"], ["pP6"],
                     lambda e, h=h, kd=kd, b0=b0: e.matmul(
                         ps["P6"][b0:b0 + 64, (h // 2) * 64:(h // 2 + 1) * 64],
                         sb["kft"][:, kd, b0:b0 + 64],
                         sb["vtm"][:, tc, h * 64:(h + 1) * 64], start=True, stop=True))
                B.do("te", "pe", ["kft", "onecol_b"], ["pP3"],
                     lambda e, h=h, kd=kd, b0=b0: e.matmul(
                         ps["P3"][b0:b0 + 64, h // 2:h // 2 + 1],
                         sb["kft"][:, kd, b0:b0 + 64],
                         sb["onecol_b"][:, :], start=True, stop=True))
            if g0 == 0 and tc == 0:
                B.do("ve", "dv", ["pP2"], ["sc0"],
                     lambda e: e.tensor_copy(sb["sc0"][:, 0:16], ps["P2"][:, 0:16]))
                self.tap(B, "den0", "sc0", sb["sc0"][:, 0:16], [128, 16])
            B.do("ve", "dv", ["pP2"], ["sc1"],
                 lambda e: e.tensor_scalar(sb["sc1"][:, 0:16], ps["P2"][:, 0:16], 1e-6, None, ALU.max))
            if g0 == 0 and tc == 0:
                self.tap(B, "denm0", "sc1", sb["sc1"][:, 0:16], [128, 16])
            B.do("se", "ac", ["sc1"], ["rden"],
                 lambda e: e.activation(sb["rden"][:, :], sb["sc1"][:, 0:16], AF.Ln))
            B.do("se", "ac", ["rden", "negone"], ["rden"],
                 lambda e: e.activation(sb["rden"][:, :], sb["rden"][:, :], AF.Exp,
                                        scale=sb["negone"][:, 0:1]))
            for h in range(NH):
                po = "P0" if h < 8 else "P1"
                oc0 = (h % 8) * 64
                B.do("ve", "dv", ["p" + po, "rden"], ["otm"],
                     lambda e, po=po, oc0=oc0, h=h: e.tensor_scalar(
                         sb["otm"][:, h * 64:(h + 1) * 64], ps[po][:, oc0:oc0 + 64],
                         sb["rden"][:, h:h + 1], None, ALU.mult))
            if g0 == 0 and tc == 0:
                B.do("ve", "dv", ["pP3"], ["sc0"],
                     lambda e: e.tensor_copy(sb["sc0"][:, 0:16], ps["P3"][:, 0:16]))
                self.tap(B, "p3c0", "sc0", sb["sc0"][:, 0:16], [128, 16])
            B.do("ve", "dv", ["pP6", "S0"], ["S0"],
                 lambda e: e.tensor_add(sb["S0"][:, :], sb["S0"][:, :], ps["P6"][:, 0:512]))
            B.do("ve", "dv", ["S0"], ["S0b"], lambda e: e.tensor_copy(sb["S0b"][:, :], sb["S0"][:, :]))
            B.do("ve", "dv", ["pP3", "ks"], ["ks"],
                 lambda e: e.tensor_add(sb["ks"][:, :], sb["ks"][:, :], ps["P3"][:, 0:16]))
            B.do("se", "ac", ["ks"], ["ksb"], lambda e: e.copy(sb["ksb"][:, :], sb["ks"][:, :]))
            if g0 == 0 and tc == 0:
                self.tap(B, "ks0post", "ks", sb["ks"][:], [128, 16])
                self.tap(B, "ksb0post", "ksb", sb["ksb"][:], [128, 16])
            if g0 == 0 and tc == 0:
                self.tap(B, "otm0", "otm", sb["otm"][:], [128, 1024])
                self.tap(B, "rden0", "rden", sb["rden"][:], [128, 16])
                self.tap(B, "S0c0", "S0", sb["S0"][:], [128, 512])
            if g0 == 0 and tc == 3:
                self.tap(B, "otm3", "otm", sb["otm"][:], [128, 1024])
                self.tap(B, "rden3", "rden", sb["rden"][:], [128, 16])
            if g0 == 0 and tc == 1:
                self.tap(B, "den1", "sc1", sb["sc1"][:, 0:16], [128, 16])
                self.tap(B, "otm1", "otm", sb["otm"][:], [128, 1024])
                self.tap(B, "ks1", "ks", sb["ks"][:], [128, 16])
                self.tap(B, "S0c1", "S0", sb["S0"][:], [128, 512])
            for kd in range(KD):
                B.do("te", "pe", ["otm", "idb"], ["pPT"],
                     lambda e, kd=kd: e.transpose(ps["PT"][:, kd * 128:(kd + 1) * 128],
                                                  sb["otm"][:, kd * 128:(kd + 1) * 128], sb["idb"][:, :]))
                B.do("ve", "dv", ["pPT"], ["xlnb"],
                     lambda e, kd=kd: e.tensor_copy(sb["xlnb"][:, kd, t0:t0 + L],
                                                    ps["PT"][:, kd * 128:(kd + 1) * 128]))

        if g0 == 0:
            self.tap(B, "afm0", "xlnb", sb["xlnb"][:], [128, KD, G])
        if g0 == (NGRP - 1) * G:
            self.tap(B, "denlog", "denlog", sb["denlog"][:], [128, 256])
        for mt in range(KD):
            pn = "P0" if mt % 2 == 0 else "P1"
            B.do("te", "pe", ["wbig", "xlnb"], ["p" + pn],
                 lambda e, pn=pn, mt=mt: [
                     e.matmul(ps[pn][:, 0:G], sb["wbig"][:, kd, mt * 128:(mt + 1) * 128],
                              sb["xlnb"][:, kd, :], start=(kd == 0), stop=(kd == KD - 1))
                     for kd in range(KD)][-1])
            B.do("ve", "dv", ["p" + pn, "pv", "xs"], ["xs"],
                 lambda e, pn=pn, mt=mt: e.scalar_tensor_tensor(
                     sb["xs"][:, mt, g0:g0 + G], ps[pn][:, 0:G],
                     sb["pv"][:, mt, PV["b_out"]:PV["b_out"] + 1],
                     sb["xs"][:, mt, g0:g0 + G], ALU.add, ALU.add))


    def alloc_predump(self, moe=True, l0f=True, l1a=False):
        nc = self.nc
        if l0f:
            self.io["scf1"] = nc.dram_tensor("scf1", [32, 128, KD, 128], BF16,
                                             kind="Internal").ap()
            self.io["scf2"] = nc.dram_tensor("scf2", [KD, 128, 32, 128], BF16,
                                             kind="Internal").ap()
        if moe:
            self.io["scw1"] = nc.dram_tensor("scw1", [NE, 64, 128, KD, 128], BF16,
                                             kind="Internal").ap()
            self.io["scw2"] = nc.dram_tensor("scw2", [NE, KD, 128, 32, 128], BF16,
                                             kind="Internal").ap()
        if l1a:
            for k in ("Wr", "Wk", "Wo", "Wr_ffn"):
                self.io["sc" + k] = nc.dram_tensor("sc" + k, [KD, 128, KD, 128],
                                                   BF16, kind="Internal").ap()
            for k in ("Wv", "Wg"):
                self.io["sc" + k] = nc.dram_tensor("sc" + k, [128, KD, 1024],
                                                   BF16, kind="Internal").ap()
            self.io["scWk_ffn"] = nc.dram_tensor("scWk_ffn", [28, 128, KD, 128],
                                                 BF16, kind="Internal").ap()
            self.io["scWv_ffn"] = nc.dram_tensor("scWv_ffn", [KD, 128, 28, 128],
                                                 BF16, kind="Internal").ap()
            self.io["sctm2"] = nc.dram_tensor("sctm2", [5, 32, 1024], BF16,
                                              kind="Internal").ap()
        self.pdm_sem = nc.alloc_semaphore("pdm")
        self.pdm_total = 0
        self.pdm_stage = {}
        self._pdm_waited = {}

    def alloc_predump_l0a(self):
        nc = self.nc
        self.io["scqkv"] = nc.dram_tensor("scqkv", [128, KD, 3 * D], BF16,
                                          kind="Internal").ap()
        self.io["scwo"] = nc.dram_tensor("scwo", [128, KD, D], BF16,
                                         kind="Internal").ap()
        self.pdm2_sem = nc.alloc_semaphore("pdm2")
        self.pdm2_total = 0
        self._pdm2_waited = False

    def emit_predump_l0a(self):
        nc, io = self.nc, self.io
        nc.gpsimd.dma_start(out=io["scqkv"], in_=io["wqkvT"]).then_inc(self.pdm2_sem, 16)
        nc.gpsimd.dma_start(out=io["scwo"], in_=io["woT"]).then_inc(self.pdm2_sem, 16)
        self.pdm2_total = 32

    def pdm2_gate(self, B, e):
        if getattr(self, "pdm2_total", 0) and not self._pdm2_waited:
            self._pdm2_waited = True
            B.eng(e).wait_ge(self.pdm2_sem, self.pdm2_total)

    def emit_predump(self, moe=True, l0f=True, l1a=False):
        nc, io = self.nc, self.io
        n = self.pdm_total // 16
        if l0f:
            for s in range(32):
                nc.gpsimd.dma_start(out=io["scf1"][s], in_=io["w1f0"][s]) \
                    .then_inc(self.pdm_sem, 16)
                n += 1
            for mt in range(KD):
                nc.gpsimd.dma_start(out=io["scf2"][mt], in_=io["w2f0"][mt]) \
                    .then_inc(self.pdm_sem, 16)
                n += 1
        if l1a:
            for k in ("Wr", "Wk", "Wo", "Wr_ffn"):
                for mt in range(KD):
                    nc.gpsimd.dma_start(out=io["sc" + k][mt], in_=io[k][mt]) \
                        .then_inc(self.pdm_sem, 16)
                    n += 1
            for k in ("Wv", "Wg"):
                nc.gpsimd.dma_start(out=io["sc" + k], in_=io[k]) \
                    .then_inc(self.pdm_sem, 16)
                n += 1
            for s in range(28):
                nc.gpsimd.dma_start(out=io["scWk_ffn"][s], in_=io["Wk_ffn"][s]) \
                    .then_inc(self.pdm_sem, 16)
                n += 1
            for mt in range(KD):
                nc.gpsimd.dma_start(out=io["scWv_ffn"][mt], in_=io["Wv_ffn"][mt]) \
                    .then_inc(self.pdm_sem, 16)
                n += 1
            for f in range(5):
                nc.gpsimd.dma_start(out=io["sctm2"][f], in_=io["tm_w2"][f]) \
                    .then_inc(self.pdm_sem, 16)
                n += 1
        if moe:
            for ex in range(NE):
                for s in range(64):
                    nc.gpsimd.dma_start(out=io["scw1"][ex, s], in_=io["m_w1T"][ex][s]) \
                        .then_inc(self.pdm_sem, 16)
                    n += 1
                for mt in range(KD):
                    nc.gpsimd.dma_start(out=io["scw2"][ex, mt], in_=io["m_w2T"][ex][mt]) \
                        .then_inc(self.pdm_sem, 16)
                    n += 1
        self.pdm_total = n * 16
        self.pdm_stage["moe" if moe else "base"] = self.pdm_total
        if not moe:
            self.pdm_stage.setdefault("base", self.pdm_total)

    def pdm_gate(self, B, e, stage="base"):
        tot = self.pdm_stage.get(stage, getattr(self, "pdm_total", 0))
        if tot and self._pdm_waited.get(e, -1) < tot:
            self._pdm_waited[e] = tot
            B.eng(e).wait_ge(self.pdm_sem, tot)

    W1RING = ("w1a", "w1b", "w1c", "w1d", "w1e", "w1f")

    def ensure_w1_ring(self):
        for nm in MK.W1RING[2:]:
            if nm not in self.sb:
                self.sbuf(nm, [128, KD, 128], BF16)

    def alloc_l0f(self):
        s = self.sbuf
        if "hbuf" not in self.sb:
            s("hbuf", [128, 32, G], BF16)
        s("w1a", [128, KD, 128], BF16)
        s("w1b", [128, KD, 128], BF16)
        s("w2a", [128, 32, 128], BF16)
        self.ensure_w1_ring()

    def emit_l0f(self, B, src_name=None, dst_name=None):
        sb, ps = self.sb, self.ps
        self.pdm_gate(B, "sy")
        pf = self.gw and "qf" in self.sb

        def _pfetch(g):
            B.dma("gp", ["xs"], ["qf"],
                  lambda e, g=g: e.dma_start(
                      out=sb["qf"][:],
                      in_=self.io[src_name].rearrange("(k p) t -> p k t", p=128)[:, :, g * G:(g + 1) * G]))
        if pf:
            _pfetch(0)
        for g in range(NGRP):
            g0 = 0 if self.gw else g * G
            if self.gw:
                B.dma("sy", [], ["xs"],
                      lambda e, g=g: e.dma_start(
                          out=sb["xs"][:],
                          in_=self.io[src_name].rearrange("(k p) t -> p k t", p=128)[:, :, g * G:(g + 1) * G]))
            if pf:
                self.emit_ln(B, g0, src="qf")
                if g + 1 < NGRP:
                    _pfetch(g + 1)
            else:
                self.emit_ln(B, g0)
            for j in range(32):
                wt = MK.W1RING[j % 6]
                B.dma("sy", [], [wt],
                      lambda e, wt=wt, j=j: e.dma_start(
                          out=sb[wt][:], in_=self.io["scf1"][j]))
                pn = ("P0", "P1", "P2", "P3")[j % 4]
                B.do("te", "pe", [wt, "xlnb"], ["p" + pn],
                     lambda e, pn=pn, wt=wt: [
                         e.matmul(ps[pn][:, 0:G], sb[wt][:, kd, :], sb["xlnb"][:, kd, :],
                                  start=(kd == 0), stop=(kd == KD - 1))
                         for kd in range(KD)][-1])
                B.do("se", "ac", ["p" + pn], ["hbuf"],
                     lambda e, pn=pn, j=j: e.activation(sb["hbuf"][:, j, :], ps[pn][:, 0:G], AF.Relu))
            wv2f = self.wbig_v(32, 128) if "wbig" in self.sb else None
            for mt in range(KD):
                use_big = (mt % 2 == 1) and wv2f is not None
                wk = "wbig" if use_big else "w2a"
                wap = wv2f if use_big else sb["w2a"][:]
                B.dma("sy", [], [wk],
                      lambda e, mt=mt, wap=wap: e.dma_start(
                          out=wap, in_=self.io["scf2"][mt]))
                pn = ("P0", "P1", "P2", "P3")[mt % 4]
                B.do("te", "pe", [wk, "hbuf"], ["p" + pn],
                     lambda e, pn=pn, wap=wap: [
                         e.matmul(ps[pn][:, 0:G], wap[:, k, :], sb["hbuf"][:, k, :],
                                  start=(k == 0), stop=(k == 31))
                         for k in range(32)][-1])
                B.do("ve", "dv", ["p" + pn, "xs"], ["xs"],
                     lambda e, pn=pn, mt=mt: e.tensor_add(
                         sb["xs"][:, mt, g0:g0 + G], sb["xs"][:, mt, g0:g0 + G], ps[pn][:, 0:G]))
            if self.gw:
                B.dma("sy", ["xs"], [],
                      lambda e, g=g: e.dma_start(
                          out=self.io[dst_name].rearrange("(k p) t -> p k t", p=128)[:, :, g * G:(g + 1) * G],
                          in_=sb["xs"][:]))

    def alloc_l1a(self):
        s = self.sbuf
        need = lambda n: n not in self.sb
        if need("hbuf"):
            s("hbuf", [128, 32, G], BF16)
        if need("vtm"):
            s("vtm", [128, NCH, 1024], BF16)
        if need("xsn"):
            av = self.nc.lookup_mls(self.sb["vtm"]).memorylocations[0].addr
            self.sb["xsn"] = self.nc.alloc_sbuf_tensor_at(
                "xsn", [128, KD, G], BF16, offset=av)
        if need("kft"):
            s("kft", [128, KD, L], BF16)
        if need("otm"):
            s("otm", [128, 1024], BF16)
        if need("am"):
            s("am", [128, L], BF16)
        if need("am2"):
            s("am2", [128, L], BF16)
        if need("qf"):
            s("qf", [128, KD, G], BF16)
        if need("kf"):
            s("kf", [128, KD, G], BF16)
        s("xbb", [128, KD, G], BF16)
        s("xab", [128, KD, G], BF16)
        s("dxb", [128, KD, G], BF16)
        s("mix", [128, KD, G], BF16)
        if need("gtm"):
            s("gtm", [128, NCH, 1024], BF16)
        s("loraf", [32, 5, G], BF16)
        s("t64", [64, G], BF16)
        s("S1", [128, 512], F32)
        s("S1b", [128, 512], BF16)
        s("dvc", [128, 16], BF16)
        s("dsc", [128, KD, NCH], F32)
        s("gnm", [128, 16], F32)
        s("gne", [128, 16], F32)
        s("grs", [128, 16], F32)
        s("xaprev", [128, KD], F32)
        s("xfprev", [128, KD], F32)
        s("scr", [128, L], F32)
        s("scr2", [128, L], F32)
        s("zcol", [128, 1], F32)
        s("tmw1s", [128, KD, 160], BF16)
        s("tmw2s", [32, 1024], BF16)
        s("tdw1s", [128, KD, 64], BF16)
        s("tdw2s", [64, 1024], BF16)
        if need("lnxg_r"):
            s("lnxg_r", [128, 1024], BF16)
            s("lnxb_r", [128, 1024], BF16)
        if need("wbig"):
            s("wbig", [128, KD, 1024], BF16)

    def wbig_v(self, a, b):
        return self.sb["wbig"][:].rearrange("p x y -> p (x y)")[:, 0:a * b].rearrange(
            "p (a b) -> p a b", a=a)

    def emit_l1a_setup(self, B):
        sb = self.sb
        B.do("ve", "dv", [], ["zcol"], lambda e: e.memset(sb["zcol"][:, :], 0.0))
        B.dma("gp", [], ["tmw1s"],
              lambda e: e.dma_start(out=sb["tmw1s"][:],
                                    in_=self.io["tm_w1"].rearrange("(k p) j -> p k j", p=128)))
        B.dma("gp", [], ["tdw1s"],
              lambda e: e.dma_start(out=sb["tdw1s"][:],
                                    in_=self.io["td_w1"].rearrange("(k p) j -> p k j", p=128)))
        B.dma("gp", [], ["tdw2s"],
              lambda e: e.dma_start(out=sb["tdw2s"][:], in_=self.io["td_w2"]))


    def emit_l1a(self, B, zero_state, mask_by_role, src_name=None, dst_name=None,
                 reduced=False):
        sb, ps = self.sb, self.ps
        self._gw_io = (src_name, dst_name)
        self.pdm_gate(B, "sy")
        B.dma("gp", [], ["lnxg_r"], lambda e: e.dma_start(out=sb["lnxg_r"][:], in_=self.io["lnxgrow"]))
        B.dma("gp", [], ["lnxb_r"], lambda e: e.dma_start(out=sb["lnxb_r"][:], in_=self.io["lnxbrow"]))
        EQD, EKG, KRAW, RUK = 0, 8, 16, 24
        if zero_state:
            B.do("ve", "dv", [], ["S1"], lambda e: e.memset(sb["S1"][:, :], 0.0))
            B.do("ve", "dv", [], ["S1b"], lambda e: e.memset(sb["S1b"][:, :], 0.0))
            B.do("ve", "dv", [], ["xaprev"], lambda e: e.memset(sb["xaprev"][:, :], 0.0))
            B.do("ve", "dv", [], ["xfprev"], lambda e: e.memset(sb["xfprev"][:, :], 0.0))
        if mask_by_role:
            B.do("ve", "dv", ["rolec", "S1"], ["S1"],
                 lambda e: e.tensor_scalar(sb["S1"][:, :], sb["S1"][:, :], sb["rolec"][:, 0:1], None, ALU.mult))
            B.do("ve", "dv", ["S1"], ["S1b"], lambda e: e.tensor_copy(sb["S1b"][:, :], sb["S1"][:, :]))
            B.do("ve", "dv", ["rolec", "xaprev"], ["xaprev"],
                 lambda e: e.tensor_scalar(sb["xaprev"][:, :], sb["xaprev"][:, :], sb["rolec"][:, 0:1], None, ALU.mult))
            B.do("ve", "dv", ["rolec", "xfprev"], ["xfprev"],
                 lambda e: e.tensor_scalar(sb["xfprev"][:, :], sb["xfprev"][:, :], sb["rolec"][:, 0:1], None, ALU.mult))
        if self.gw:
            self._l1a_prefetch(B, 0)
        for g in range(NGRP):
            self._l1a_group(B, g * G, EQD, EKG, KRAW, RUK, reduced=reduced)

    def _l1a_prefetch(self, B, gr):
        src_name = self._gw_io[0]
        B.dma("gp", [], ["xsn"],
              lambda e, gr=gr: e.dma_start(
                  out=self.sb["xsn"][:],
                  in_=self.io[src_name].rearrange("(k p) t -> p k t", p=128)[:, :, gr:gr + G]))

    def _l1a_group(self, B, g0, EQD, EKG, KRAW, RUK, reduced=False):
        sb, ps = self.sb, self.ps
        io = self.io
        dst_name = None
        if self.gw:
            src_name, dst_name = self._gw_io
            gr = g0
            g0 = 0
            if not reduced:
                B.dma("sy", [], ["xs"],
                      lambda e: e.dma_start(
                          out=sb["xs"][:],
                          in_=io[src_name].rearrange("(k p) t -> p k t", p=128)[:, :, gr:gr + G]))

        if self.gw:
            self.emit_ln(B, g0, src="xsn")
        else:
            self.emit_ln(B, g0)
        self.emit_ln(B, g0, src="xlnb", dst="xbb", gcol="ln0_g", bcol="ln0_b")
        self.emit_ln(B, g0, src="xbb", dst="xab", gcol="ln1_g", bcol="ln1_b")

        for kd in range(KD):
            B.do("ve", "dv", ["xab", "xaprev"], ["dxb"],
                 lambda e, kd=kd: e.tensor_sub(sb["dxb"][:, kd, 1:G], sb["xab"][:, kd, 0:G - 1],
                                               sb["xab"][:, kd, 1:G]))
            B.do("ve", "dv", ["xab", "xaprev"], ["dxb"],
                 lambda e, kd=kd: e.tensor_sub(sb["dxb"][:, kd, 0:1], sb["xaprev"][:, kd:kd + 1],
                                               sb["xab"][:, kd, 0:1]))
        for kd in range(KD):
            B.do("ve", "dv", ["xab"], ["xaprev"],
                 lambda e, kd=kd: e.tensor_copy(sb["xaprev"][:, kd:kd + 1], sb["xab"][:, kd, G - 1:G]))

        for kd in range(KD):
            B.do("ve", "dv", ["dxb", "pv", "xab"], ["mix"],
                 lambda e, kd=kd: e.scalar_tensor_tensor(
                     sb["mix"][:, kd, :], sb["dxb"][:, kd, :],
                     sb["pv"][:, kd, PV["time_maa_x"]:PV["time_maa_x"] + 1],
                     sb["xab"][:, kd, :], ALU.mult, ALU.add))
        for f in range(5):
            B.do("te", "pe", ["tmw1s", "mix"], ["pP2"],
                 lambda e, f=f: [
                     e.matmul(ps["P2"][0:32, 0:G], sb["tmw1s"][:, kd, f * 32:(f + 1) * 32],
                              sb["mix"][:, kd, :], start=(kd == 0), stop=(kd == KD - 1))
                     for kd in range(KD)][-1])
            B.do("se", "ac", ["pP2"], ["loraf"],
                 lambda e, f=f: e.activation(sb["loraf"][0:32, f, :], ps["P2"][0:32, 0:G], AF.Tanh))

        FNAMES = ["w", "k", "v", "r", "g"]
        for fi, fn in enumerate(FNAMES):
            B.dma("sy", [], ["tmw2s"],
                  lambda e, fi=fi: e.dma_start(out=sb["tmw2s"][:], in_=io["sctm2"][fi]))
            for kd in range(KD):
                pn = ("P0", "P1", "P2", "P3")[kd % 4]
                B.do("te", "pe", ["tmw2s", "loraf"], ["p" + pn],
                     lambda e, pn=pn, kd=kd, fi=fi: e.matmul(
                         ps[pn][:, 0:G], sb["tmw2s"][0:32, kd * 128:(kd + 1) * 128],
                         sb["loraf"][0:32, fi, :], start=True, stop=True))
                cn = "time_maa_" + fn
                B.do("ve", "dv", ["p" + pn, "pv", "dxb"], ["sc0"],
                     lambda e, pn=pn, kd=kd, cn=cn: e.scalar_tensor_tensor(
                         sb["sc0"][:, :], ps[pn][:, 0:G],
                         sb["pv"][:, kd, PV[cn]:PV[cn] + 1],
                         sb["dxb"][:, kd, :], ALU.add, ALU.mult))
                B.do("ve", "dv", ["sc0", "xab"], ["mix"],
                     lambda e, kd=kd: e.tensor_add(sb["mix"][:, kd, :], sb["sc0"][:, :],
                                                   sb["xab"][:, kd, :]))
            if fn == "w":
                B.do("te", "pe", ["tdw1s", "mix"], ["pP2"],
                     lambda e: [
                         e.matmul(ps["P2"][0:64, 0:G], sb["tdw1s"][:, kd, :], sb["mix"][:, kd, :],
                                  start=(kd == 0), stop=(kd == KD - 1))
                         for kd in range(KD)][-1])
                B.do("se", "ac", ["pP2"], ["t64"],
                     lambda e: e.activation(sb["t64"][0:64, :], ps["P2"][0:64, 0:G], AF.Tanh))
                for kd in range(KD):
                    pn = "P0" if kd % 2 == 0 else "P1"
                    B.do("te", "pe", ["tdw2s", "t64"], ["p" + pn],
                         lambda e, pn=pn, kd=kd: e.matmul(
                             ps[pn][:, 0:G], sb["tdw2s"][:, kd * 128:(kd + 1) * 128],
                             sb["t64"][0:64, :], start=True, stop=True))
                    B.do("ve", "dv", ["p" + pn, "pv"], ["sc0"],
                         lambda e, pn=pn, kd=kd: e.tensor_scalar(
                             sb["sc0"][:, :], ps[pn][:, 0:G],
                             sb["pv"][:, kd, PV["time_decay"]:PV["time_decay"] + 1], None, ALU.add))
                    B.do("se", "ac", ["sc0"], ["sc1"],
                         lambda e: e.activation(sb["sc1"][:, :], sb["sc0"][:, :], AF.Exp))
                    B.do("ve", "dv", ["sc1"], ["sc0"],
                         lambda e: e.tensor_tensor_scan(sb["sc0"][:, :], sb["sc1"][:, :],
                                                        sb["sc1"][:, :], 0.0, ALU.add, ALU.bypass))
                    for tc in range(NCH):
                        t0 = tc * L
                        basec = sb["zcol"][:, 0:1] if tc == 0 else sb["sc0"][:, t0 - 1:t0]
                        B.do("ve", "dv", ["sc0", "zcol"], ["scr"],
                             lambda e, t0=t0, basec=basec: e.tensor_scalar(
                                 sb["scr"][:, :], sb["sc0"][:, t0:t0 + L], basec, None, ALU.subtract))
                        B.do("se", "ac", ["scr"], ["hbuf"],
                             lambda e, kd=kd, tc=tc: e.activation(
                                 sb["hbuf"][:, EKG + kd, tc * L:(tc + 1) * L], sb["scr"][:, :], AF.Exp))
                        B.do("ve", "dv", ["scr", "sc1"], ["scr2"],
                             lambda e, t0=t0: e.tensor_sub(sb["scr2"][:, :], sb["scr"][:, :],
                                                           sb["sc1"][:, t0:t0 + L]))
                        B.do("se", "ac", ["scr2", "negone"], ["hbuf"],
                             lambda e, kd=kd, tc=tc: e.activation(
                                 sb["hbuf"][:, EQD + kd, tc * L:(tc + 1) * L], sb["scr2"][:, :],
                                 AF.Exp, scale=sb["negone"][:, 0:1]))
                        B.do("ve", "dv", ["sc0", "zcol"], ["dsc"],
                             lambda e, t0=t0, basec=basec, kd=kd, tc=tc: e.tensor_scalar(
                                 sb["dsc"][:, kd, tc:tc + 1], sb["sc0"][:, t0 + L - 1:t0 + L],
                                 basec, None, ALU.subtract))
                        B.do("se", "ac", ["dsc", "negone"], ["dsc"],
                             lambda e, kd=kd, tc=tc: e.activation(
                                 sb["dsc"][:, kd, tc:tc + 1], sb["dsc"][:, kd, tc:tc + 1],
                                 AF.Exp, scale=sb["negone"][:, 0:1]))
            elif fn == "k":
                for mt in range(KD):
                    wt = "w1a" if mt % 2 == 0 else "w1b"
                    B.dma("sy", [], [wt],
                          lambda e, mt=mt, wt=wt: e.dma_start(
                              out=sb[wt][:], in_=io["scWk"][mt]))
                    pn = ("P0", "P1", "P2", "P3")[mt % 4]
                    B.do("te", "pe", [wt, "mix"], ["p" + pn],
                         lambda e, pn=pn, wt=wt: [
                             e.matmul(ps[pn][:, 0:G], sb[wt][:, kd, :],
                                      sb["mix"][:, kd, :], start=(kd == 0), stop=(kd == KD - 1))
                             for kd in range(KD)][-1])
                    B.do("se", "ac", ["p" + pn], ["hbuf"],
                         lambda e, pn=pn, mt=mt: e.copy(sb["hbuf"][:, KRAW + mt, :], ps[pn][:, 0:G]))
                    B.do("ve", "dv", ["p" + pn, "hbuf"], ["kf"],
                         lambda e, pn=pn, mt=mt: e.tensor_mul(
                             sb["kf"][:, mt, :], ps[pn][:, 0:G], sb["hbuf"][:, EKG + mt, :]))
            elif fn == "v":
                B.dma("sy", [], ["wbig"],
                      lambda e: e.dma_start(out=sb["wbig"][:], in_=io["scWv"]))
                for tc in range(NCH):
                    for nb in range(2):
                        pn = "P0" if (tc * 2 + nb) % 2 == 0 else "P1"
                        B.do("te", "pe", ["wbig", "mix"], ["p" + pn],
                             lambda e, pn=pn, tc=tc, nb=nb: [
                                 e.matmul(ps[pn][:, 0:512], sb["mix"][:, kd, tc * L:(tc + 1) * L],
                                          sb["wbig"][:, kd, nb * 512:(nb + 1) * 512],
                                          start=(kd == 0), stop=(kd == KD - 1))
                                 for kd in range(KD)][-1])
                        B.do("ve", "dv", ["p" + pn], ["vtm"],
                             lambda e, pn=pn, tc=tc, nb=nb: e.tensor_copy(
                                 sb["vtm"][:, tc, nb * 512:(nb + 1) * 512], ps[pn][:, 0:512]))
            elif fn == "r":
                for mt in range(KD):
                    wt = "w1a" if mt % 2 == 0 else "w1b"
                    B.dma("sy", [], [wt],
                          lambda e, mt=mt, wt=wt: e.dma_start(
                              out=sb[wt][:], in_=io["scWr"][mt]))
                    pn = ("P0", "P1", "P2", "P3")[mt % 4]
                    B.do("te", "pe", [wt, "mix"], ["p" + pn],
                         lambda e, pn=pn, wt=wt: [
                             e.matmul(ps[pn][:, 0:G], sb[wt][:, kd, :],
                                      sb["mix"][:, kd, :], start=(kd == 0), stop=(kd == KD - 1))
                             for kd in range(KD)][-1])
                    B.do("ve", "dv", ["p" + pn, "hbuf"], ["qf"],
                         lambda e, pn=pn, mt=mt: e.tensor_mul(
                             sb["qf"][:, mt, :], ps[pn][:, 0:G], sb["hbuf"][:, EQD + mt, :]))
                    B.do("ve", "dv", ["p" + pn, "pv"], ["sc0"],
                         lambda e, pn=pn, mt=mt: e.tensor_scalar(
                             sb["sc0"][:, :], ps[pn][:, 0:G],
                             sb["pv"][:, mt, PV["u_flat"]:PV["u_flat"] + 1], None, ALU.mult))
                    B.do("ve", "dv", ["sc0", "hbuf"], ["hbuf"],
                         lambda e, mt=mt: e.tensor_mul(
                             sb["hbuf"][:, RUK + mt, :], sb["sc0"][:, :], sb["hbuf"][:, KRAW + mt, :]))
            elif fn == "g":
                B.dma("sy", [], ["wbig"],
                      lambda e: e.dma_start(out=sb["wbig"][:], in_=io["scWg"]))
                for tc in range(NCH):
                    for nb in range(2):
                        pn = "P0" if (tc * 2 + nb) % 2 == 0 else "P1"
                        B.do("te", "pe", ["wbig", "mix"], ["p" + pn],
                             lambda e, pn=pn, tc=tc, nb=nb: [
                                 e.matmul(ps[pn][:, 0:512], sb["mix"][:, kd, tc * L:(tc + 1) * L],
                                          sb["wbig"][:, kd, nb * 512:(nb + 1) * 512],
                                          start=(kd == 0), stop=(kd == KD - 1))
                                 for kd in range(KD)][-1])
                        B.do("se", "ac", ["p" + pn], ["gtm"],
                             lambda e, pn=pn, tc=tc, nb=nb: e.activation(
                                 sb["gtm"][:, tc, nb * 512:(nb + 1) * 512], ps[pn][:, 0:512], AF.Silu))

        for tc in range(NCH):
            t0 = tc * L
            out = (not reduced) or tc == NCH - 1
            for kd in range(KD):
                B.do("te", "pe", ["kf", "idb"], ["pPT"],
                     lambda e, kd=kd: e.transpose(ps["PT"][:, kd * 128:(kd + 1) * 128],
                                                  sb["kf"][:, kd, t0:t0 + L], sb["idb"][:, :]))
                B.do("ve", "dv", ["pPT"], ["kft"],
                     lambda e, kd=kd: e.tensor_copy(sb["kft"][:, kd, :],
                                                    ps["PT"][:, kd * 128:(kd + 1) * 128]))
            if out:
                for kd in range(KD):
                    B.do("te", "pe", ["hbuf", "hsel_b"], ["pP2"],
                         lambda e, kd=kd: e.matmul(
                             ps["P2"][:, 2 * kd:2 * kd + 2], sb["hbuf"][:, RUK + kd, t0:t0 + L],
                             sb["hsel_b"][:, :], start=True, stop=True))
                B.do("se", "ac", ["pP2"], ["dvc"],
                     lambda e: e.copy(sb["dvc"][:, :], ps["P2"][:, 0:16]))
            for h in range(NH):
                kd, b0 = h // 2, (h % 2) * 64
                pa = "P4" if h % 2 == 0 else "P5"
                an = "am" if h % 2 == 0 else "am2"
                po = "P0" if h < 8 else "P1"
                oc0 = (h % 8) * 64
                if out:
                    B.do("te", "pe", ["kf", "qf"], ["p" + pa],
                         lambda e, pa=pa, kd=kd, b0=b0: e.matmul(
                             ps[pa][:, 0:L], sb["kf"][b0:b0 + 64, kd, t0:t0 + L],
                             sb["qf"][b0:b0 + 64, kd, t0:t0 + L], start=True, stop=True))
                    B.do("ve", "dv", ["p" + pa, "mS"], [an],
                         lambda e, pa=pa, an=an: e.tensor_mul(sb[an][:, :], ps[pa][:, 0:L],
                                                              sb["mS"][:, :]))
                    B.do("te", "pe", [an, "vtm"], ["p" + po],
                         lambda e, po=po, oc0=oc0, h=h, an=an: e.matmul(
                             ps[po][:, oc0:oc0 + 64], sb[an][:, :],
                             sb["vtm"][:, tc, h * 64:(h + 1) * 64], start=True, stop=False))
                    B.do("te", "pe", ["qf", "S1b"], ["p" + po],
                         lambda e, po=po, oc0=oc0, kd=kd, b0=b0, h=h: e.matmul(
                             ps[po][:, oc0:oc0 + 64], sb["qf"][b0:b0 + 64, kd, t0:t0 + L],
                             sb["S1b"][b0:b0 + 64, (h // 2) * 64:(h // 2 + 1) * 64],
                             start=False, stop=True))
                B.do("te", "pe", ["kft", "vtm"], ["pP6"],
                     lambda e, h=h, kd=kd, b0=b0: e.matmul(
                         ps["P6"][b0:b0 + 64, (h // 2) * 64:(h // 2 + 1) * 64],
                         sb["kft"][:, kd, b0:b0 + 64],
                         sb["vtm"][:, tc, h * 64:(h + 1) * 64], start=True, stop=True))
            if out:
                for h in range(NH):
                    po = "P0" if h < 8 else "P1"
                    oc0 = (h % 8) * 64
                    B.do("ve", "dv", ["p" + po, "dvc", "vtm"], ["otm"],
                         lambda e, po=po, oc0=oc0, h=h: e.scalar_tensor_tensor(
                             sb["otm"][:, h * 64:(h + 1) * 64], sb["vtm"][:, tc, h * 64:(h + 1) * 64],
                             sb["dvc"][:, h:h + 1], ps[po][:, oc0:oc0 + 64], ALU.mult, ALU.add))
            B.do("ve", "dv", ["pP6", "S1"], ["S1"],
                 lambda e: e.tensor_add(sb["S1"][:, :], sb["S1"][:, :], ps["P6"][:, 0:512]))
            for h in range(NH):
                kd, b0 = h // 2, (h % 2) * 64
                B.do("ve", "dv", ["dsc", "S1"], ["S1"],
                     lambda e, kd=kd, b0=b0, h=h: e.tensor_scalar(
                         sb["S1"][b0:b0 + 64, (h // 2) * 64:(h // 2 + 1) * 64],
                         sb["S1"][b0:b0 + 64, (h // 2) * 64:(h // 2 + 1) * 64],
                         sb["dsc"][b0:b0 + 64, kd, tc:tc + 1], None, ALU.mult))
            B.do("ve", "dv", ["S1"], ["S1b"], lambda e: e.tensor_copy(sb["S1b"][:, :], sb["S1"][:, :]))
            if not out:
                continue
            for h in range(NH):
                B.do("se", "ac", ["otm"], ["sq16"],
                     lambda e, h=h: e.activation(sb["sq16"][:, 0:64], sb["otm"][:, h * 64:(h + 1) * 64],
                                                 AF.Square))
                B.do("ve", "dv", ["otm"], ["gnm"],
                     lambda e, h=h: e.reduce_sum(sb["gnm"][:, h:h + 1], sb["otm"][:, h * 64:(h + 1) * 64],
                                                 axis=mybir.AxisListType.X))
                B.do("ve", "dv", ["sq16"], ["gne"],
                     lambda e, h=h: e.reduce_sum(sb["gne"][:, h:h + 1], sb["sq16"][:, 0:64],
                                                 axis=mybir.AxisListType.X))
            B.do("ve", "dv", ["gnm"], ["gnm"],
                 lambda e: e.tensor_scalar(sb["gnm"][:, :], sb["gnm"][:, :], 1.0 / 64, None, ALU.mult))
            B.do("ve", "dv", ["gne"], ["gne"],
                 lambda e: e.tensor_scalar(sb["gne"][:, :], sb["gne"][:, :], 1.0 / 64, None, ALU.mult))
            B.do("se", "ac", ["gnm"], ["grs"],
                 lambda e: e.activation(sb["grs"][:, :], sb["gnm"][:, :], AF.Square))
            B.do("ve", "dv", ["gne", "grs"], ["grs"],
                 lambda e: e.tensor_sub(sb["grs"][:, :], sb["gne"][:, :], sb["grs"][:, :]))
            B.do("se", "ac", ["grs", "epscol"], ["grs"],
                 lambda e: e.activation(sb["grs"][:, :], sb["grs"][:, :], AF.Ln,
                                        bias=sb["epscol"][:, 0:1]))
            B.do("se", "ac", ["grs", "neghalf"], ["grs"],
                 lambda e: e.activation(sb["grs"][:, :], sb["grs"][:, :], AF.Exp,
                                        scale=sb["neghalf"][:, 0:1]))
            for h in range(NH):
                B.do("ve", "dv", ["gnm", "grs", "otm"], ["otm"],
                     lambda e, h=h: e.tensor_scalar(
                         sb["otm"][:, h * 64:(h + 1) * 64], sb["otm"][:, h * 64:(h + 1) * 64],
                         sb["gnm"][:, h:h + 1], sb["grs"][:, h:h + 1], ALU.subtract, ALU.mult))
            B.do("ve", "dv", ["otm", "lnxg_r"], ["sc0"],
                 lambda e: e.tensor_mul(sb["sc0"][:, 0:512], sb["otm"][:, 0:512], sb["lnxg_r"][:, 0:512]))
            B.do("ve", "dv", ["otm", "lnxg_r"], ["sc1"],
                 lambda e: e.tensor_mul(sb["sc1"][:, 0:512], sb["otm"][:, 512:1024], sb["lnxg_r"][:, 512:1024]))
            B.do("ve", "dv", ["sc0", "lnxb_r"], ["sc0"],
                 lambda e: e.tensor_add(sb["sc0"][:, 0:512], sb["sc0"][:, 0:512], sb["lnxb_r"][:, 0:512]))
            B.do("ve", "dv", ["sc1", "lnxb_r"], ["sc1"],
                 lambda e: e.tensor_add(sb["sc1"][:, 0:512], sb["sc1"][:, 0:512], sb["lnxb_r"][:, 512:1024]))
            B.do("ve", "dv", ["sc0", "gtm"], ["otm"],
                 lambda e: e.tensor_mul(sb["otm"][:, 0:512], sb["sc0"][:, 0:512], sb["gtm"][:, tc, 0:512]))
            B.do("ve", "dv", ["sc1", "gtm"], ["otm"],
                 lambda e: e.tensor_mul(sb["otm"][:, 512:1024], sb["sc1"][:, 0:512], sb["gtm"][:, tc, 512:1024]))
            for kd in range(KD):
                B.do("te", "pe", ["otm", "idb"], ["pPT"],
                     lambda e, kd=kd: e.transpose(ps["PT"][:, kd * 128:(kd + 1) * 128],
                                                  sb["otm"][:, kd * 128:(kd + 1) * 128], sb["idb"][:, :]))
                B.do("ve", "dv", ["pPT"], ["dxb"],
                     lambda e, kd=kd: e.tensor_copy(sb["dxb"][:, kd, t0:t0 + L],
                                                    ps["PT"][:, kd * 128:(kd + 1) * 128]))

        if self.gw and gr + G < TL:
            self._l1a_prefetch(B, gr + G)

        c0, cw = ((NCH - 1) * L, L) if reduced else (0, G)
        for mt in range(KD):
            wt = "w1a" if mt % 2 == 0 else "w1b"
            B.dma("sy", [], [wt],
                  lambda e, mt=mt, wt=wt: e.dma_start(out=sb[wt][:], in_=io["scWo"][mt]))
            pn = "P0" if mt % 2 == 0 else "P1"
            B.do("te", "pe", [wt, "dxb"], ["p" + pn],
                 lambda e, pn=pn, wt=wt: [
                     e.matmul(ps[pn][:, 0:cw], sb[wt][:, kd, :],
                              sb["dxb"][:, kd, c0:c0 + cw], start=(kd == 0), stop=(kd == KD - 1))
                     for kd in range(KD)][-1])
            B.do("ve", "dv", ["p" + pn, "xbb"], ["xlnb"],
                 lambda e, pn=pn, mt=mt: e.tensor_add(sb["xlnb"][:, mt, c0:c0 + cw],
                                                      ps[pn][:, 0:cw],
                                                      sb["xbb"][:, mt, c0:c0 + cw]))

        if reduced:
            self.emit_ln(B, g0, src="xlnb", dst="xab", gcol="ln2_g", bcol="ln2_b")
            for kd in range(KD):
                B.do("ve", "dv", ["xab"], ["xfprev"],
                     lambda e, kd=kd: e.tensor_copy(sb["xfprev"][:, kd:kd + 1],
                                                    sb["xab"][:, kd, G - 1:G]))
            return
        self.emit_ln(B, g0, src="xlnb", dst="xab", gcol="ln2_g", bcol="ln2_b")
        for kd in range(KD):
            B.do("ve", "dv", ["xab", "xfprev"], ["mix"],
                 lambda e, kd=kd: e.tensor_sub(sb["mix"][:, kd, 1:G], sb["xab"][:, kd, 0:G - 1],
                                               sb["xab"][:, kd, 1:G]))
            B.do("ve", "dv", ["xab", "xfprev"], ["mix"],
                 lambda e, kd=kd: e.tensor_sub(sb["mix"][:, kd, 0:1], sb["xfprev"][:, kd:kd + 1],
                                               sb["xab"][:, kd, 0:1]))
        for kd in range(KD):
            B.do("ve", "dv", ["xab"], ["xfprev"],
                 lambda e, kd=kd: e.tensor_copy(sb["xfprev"][:, kd:kd + 1], sb["xab"][:, kd, G - 1:G]))
        for kd in range(KD):
            B.do("ve", "dv", ["mix", "pv", "xab"], ["qf"],
                 lambda e, kd=kd: e.scalar_tensor_tensor(
                     sb["qf"][:, kd, :], sb["mix"][:, kd, :],
                     sb["pv"][:, kd, PV["ffn_maa_r"]:PV["ffn_maa_r"] + 1],
                     sb["xab"][:, kd, :], ALU.mult, ALU.add))
        for mt in range(KD):
            wt = "w1a" if mt % 2 == 0 else "w1b"
            B.dma("sy", [], [wt],
                  lambda e, mt=mt, wt=wt: e.dma_start(out=sb[wt][:], in_=io["scWr_ffn"][mt]))
            pn = "P0" if mt % 2 == 0 else "P1"
            B.do("te", "pe", [wt, "qf"], ["p" + pn],
                 lambda e, pn=pn, wt=wt: [
                     e.matmul(ps[pn][:, 0:G], sb[wt][:, kd, :],
                              sb["qf"][:, kd, :], start=(kd == 0), stop=(kd == KD - 1))
                     for kd in range(KD)][-1])
            B.do("se", "ac", ["p" + pn], ["kf"],
                 lambda e, pn=pn, mt=mt: e.activation(sb["kf"][:, mt, :], ps[pn][:, 0:G], AF.Sigmoid))
            if mt == 0:
                for kd in range(KD):
                    B.do("ve", "dv", ["mix", "pv", "xab"], ["xbb"],
                         lambda e, kd=kd: e.scalar_tensor_tensor(
                             sb["xbb"][:, kd, :], sb["mix"][:, kd, :],
                             sb["pv"][:, kd, PV["ffn_maa_k"]:PV["ffn_maa_k"] + 1],
                             sb["xab"][:, kd, :], ALU.mult, ALU.add))
        for j in range(FK // 128):
            wt = "w1a" if j % 2 == 0 else "w1b"
            B.dma("sy", [], [wt],
                  lambda e, j=j, wt=wt: e.dma_start(out=sb[wt][:], in_=io["scWk_ffn"][j]))
            pn = ("P0", "P1", "P2", "P3")[j % 4]
            B.do("te", "pe", [wt, "xbb"], ["p" + pn],
                 lambda e, pn=pn, wt=wt: [
                     e.matmul(ps[pn][:, 0:G], sb[wt][:, kd, :], sb["xbb"][:, kd, :],
                              start=(kd == 0), stop=(kd == KD - 1))
                     for kd in range(KD)][-1])
            B.do("se", "ac", ["p" + pn], ["xb16"],
                 lambda e, pn=pn: e.activation(sb["xb16"][:, :], ps[pn][:, 0:G], AF.Relu))
            B.do("se", "ac", ["xb16"], ["hbuf"],
                 lambda e, j=j: e.activation(sb["hbuf"][:, j, :], sb["xb16"][:, :], AF.Square))
        wv2 = self.wbig_v(28, 128)
        for mt in range(KD):
            use_big = mt % 2 == 1
            wk = "wbig" if use_big else "w2a"
            wap = wv2 if use_big else sb["w2a"][:, 0:28, :]
            B.dma("sy", [], [wk],
                  lambda e, mt=mt, wap=wap: e.dma_start(out=wap, in_=io["scWv_ffn"][mt]))
            pn = ("P0", "P1", "P2", "P3")[mt % 4]
            B.do("te", "pe", [wk, "hbuf"], ["p" + pn],
                 lambda e, pn=pn, wap=wap: [
                     e.matmul(ps[pn][:, 0:G], wap[:, k, :], sb["hbuf"][:, k, :],
                              start=(k == 0), stop=(k == 27))
                     for k in range(28)][-1])
            B.do("ve", "dv", ["p" + pn, "kf"], ["sc0"],
                 lambda e, pn=pn, mt=mt: e.tensor_mul(sb["sc0"][:, :], ps[pn][:, 0:G],
                                                      sb["kf"][:, mt, :]))
            B.do("ve", "dv", ["sc0", "xs"], ["xs"],
                 lambda e, mt=mt: e.tensor_add(sb["xs"][:, mt, g0:g0 + G], sb["xs"][:, mt, g0:g0 + G],
                                               sb["sc0"][:, :]))
            B.do("ve", "dv", ["xlnb", "xs"], ["xs"],
                 lambda e, mt=mt: e.tensor_add(sb["xs"][:, mt, g0:g0 + G], sb["xs"][:, mt, g0:g0 + G],
                                               sb["xlnb"][:, mt, :]))
        if self.gw and dst_name is not None:
            B.dma("sy", ["xs"], [],
                  lambda e: e.dma_start(
                      out=io[dst_name].rearrange("(k p) t -> p k t", p=128)[:, :, gr:gr + G],
                      in_=sb["xs"][:]))


    def alloc_l1f(self):
        s = self.sbuf
        need = lambda n: n not in self.sb
        if need("hbuf"):
            s("hbuf", [128, 32, G], BF16)
        if need("w1a"):
            s("w1a", [128, KD, 128], BF16)
            s("w1b", [128, KD, 128], BF16)
        if need("w2a"):
            s("w2a", [128, 32, 128], BF16)
        self.ensure_w1_ring()
        if need("idf"):
            s("idf", [128, 128], F32)
        need = lambda n: n not in self.sb
        if need("accf"):
            s("accf", [128, KD, G], F32)
        s("wgt32", [128, NCH, 8], F32)
        s("lgt", [128, 8], F32)
        s("lgc", [128, 8], F32)
        s("mk1", [128, 8], F32)
        s("mk2", [128, 8], F32)
        s("mx8", [128, 8], F32)
        s("lgt2", [128, 8], F32)
        s("mxb", [128, 8], F32)
        s("mk1b", [128, 8], F32)
        s("mk2b", [128, 8], F32)
        if need("gater"):
            s("gater", [1, G], F32)
        if need("gb"):
            s("gb", [128, G], BF16)
        s("rws", [128, KD, 8], BF16)
        if need("b1s"):
            s("b1s", [128, 64, 8], F32)
        s("rbL_s", [128, 8], F32)

    def emit_l1f_setup(self, B):
        sb = self.sb
        B.dma("gp", [], ["rws"],
              lambda e: e.dma_start(out=sb["rws"][:],
                                    in_=self.io["rwT"].rearrange("(k p) j -> p k j", p=128)))
        B.dma("sy", [], ["rbL_s"], lambda e: e.dma_start(out=sb["rbL_s"][:], in_=self.io["rbL"]))
        if self.has_l0a is False:
            B.dma("sy", [], ["idf"], lambda e: e.dma_start(out=sb["idf"][:], in_=self.consts["ident_f"].ap()))

    def emit_l1f(self, B, src_name=None, dst_name=None):
        sb, ps = self.sb, self.ps
        io = self.io
        self.pdm_gate(B, "sy", stage="moe")
        B.dma("sy", [], ["b1s"],
              lambda e: e.dma_start(out=sb["b1s"][:],
                                    in_=io["m_b1L"].rearrange("p (j e) -> p j e", e=8)))
        for g in range(NGRP):
            gr = g * G
            g0 = 0 if self.gw else gr
            if self.gw:
                B.dma("sy", [], ["xs"],
                      lambda e, gr=gr: e.dma_start(
                          out=sb["xs"][:],
                          in_=io[src_name].rearrange("(k p) t -> p k t", p=128)[:, :, gr:gr + G]))
            self.emit_ln(B, g0)
            for tc in range(NCH):
                B.do("te", "pe", ["xlnb", "rws"], ["pP2"],
                     lambda e, tc=tc: [
                         e.matmul(ps["P2"][:, 0:8], sb["xlnb"][:, kd, tc * L:(tc + 1) * L],
                                  sb["rws"][:, kd, :], start=(kd == 0), stop=(kd == KD - 1))
                         for kd in range(KD)][-1])
                B.do("ve", "dv", ["pP2", "rbL_s"], ["lgt"],
                     lambda e: e.tensor_add(sb["lgt"][:, :], ps["P2"][:, 0:8], sb["rbL_s"][:, :]))
                B.do("se", "ac", ["lgt"], ["lgt2"],
                     lambda e: e.copy(sb["lgt2"][:, :], sb["lgt"][:, :]))
                B.do("ve", "dv", ["lgt2"], ["mx8"],
                     lambda e: e.tensor_max(sb["mx8"][:, 0:4], sb["lgt2"][:, 0:4], sb["lgt2"][:, 4:8]))
                B.do("se", "ac", ["mx8"], ["mxb"],
                     lambda e: e.copy(sb["mxb"][:, 0:4], sb["mx8"][:, 0:4]))
                B.do("ve", "dv", ["mxb"], ["mx8"],
                     lambda e: e.tensor_max(sb["mx8"][:, 4:6], sb["mxb"][:, 0:2], sb["mxb"][:, 2:4]))
                B.do("se", "ac", ["mx8"], ["mxb"],
                     lambda e: e.copy(sb["mxb"][:, 4:6], sb["mx8"][:, 4:6]))
                B.do("ve", "dv", ["mxb"], ["mx8"],
                     lambda e: e.tensor_max(sb["mx8"][:, 6:7], sb["mxb"][:, 4:5], sb["mxb"][:, 5:6]))
                B.do("se", "ac", ["mx8"], ["mxb"],
                     lambda e: e.copy(sb["mxb"][:, 6:7], sb["mx8"][:, 6:7]))
                B.do("ve", "dv", ["lgt2", "mxb"], ["mk1"],
                     lambda e: e.tensor_scalar(sb["mk1"][:, :], sb["lgt2"][:, :], sb["mxb"][:, 6:7],
                                               None, ALU.is_equal))
                B.do("se", "ac", ["mk1"], ["mk1b"],
                     lambda e: e.copy(sb["mk1b"][:, :], sb["mk1"][:, :]))
                B.do("ve", "dv", ["mk1b", "lgt2"], ["mk2"],
                     lambda e: e.scalar_tensor_tensor(sb["mk2"][:, :], sb["mk1b"][:, :], -1e30,
                                                      sb["lgt2"][:, :], ALU.mult, ALU.add))
                B.do("se", "ac", ["mk2"], ["mk2b"],
                     lambda e: e.copy(sb["mk2b"][:, :], sb["mk2"][:, :]))
                B.do("ve", "dv", ["mk2b"], ["mx8"],
                     lambda e: e.tensor_max(sb["mx8"][:, 0:4], sb["mk2b"][:, 0:4], sb["mk2b"][:, 4:8]))
                B.do("se", "ac", ["mx8"], ["mxb"],
                     lambda e: e.copy(sb["mxb"][:, 0:4], sb["mx8"][:, 0:4]))
                B.do("ve", "dv", ["mxb"], ["mx8"],
                     lambda e: e.tensor_max(sb["mx8"][:, 4:6], sb["mxb"][:, 0:2], sb["mxb"][:, 2:4]))
                B.do("se", "ac", ["mx8"], ["mxb"],
                     lambda e: e.copy(sb["mxb"][:, 4:6], sb["mx8"][:, 4:6]))
                B.do("ve", "dv", ["mxb"], ["mx8"],
                     lambda e: e.tensor_max(sb["mx8"][:, 7:8], sb["mxb"][:, 4:5], sb["mxb"][:, 5:6]))
                B.do("se", "ac", ["mx8"], ["mxb"],
                     lambda e: e.copy(sb["mxb"][:, 7:8], sb["mx8"][:, 7:8]))
                B.do("ve", "dv", ["mxb"], ["lgc"],
                     lambda e: e.tensor_scalar(sb["lgc"][:, 0:1], sb["mxb"][:, 7:8],
                                               sb["mxb"][:, 6:7], None, ALU.subtract))
                B.do("se", "ac", ["lgc"], ["lgc"],
                     lambda e: e.activation(sb["lgc"][:, 4:5], sb["lgc"][:, 0:1], AF.Exp))
                B.do("se", "ac", ["lgc", "onecol_f"], ["lgc"],
                     lambda e: e.activation(sb["lgc"][:, 5:6], sb["lgc"][:, 4:5], AF.Ln,
                                            bias=sb["onecol_f"][:, 0:1]))
                B.do("se", "ac", ["lgc", "negone"], ["lgc"],
                     lambda e: e.activation(sb["lgc"][:, 5:6], sb["lgc"][:, 5:6], AF.Exp,
                                            scale=sb["negone"][:, 0:1]))
                B.do("ve", "dv", ["lgc"], ["lgc"],
                     lambda e: e.tensor_mul(sb["lgc"][:, 6:7], sb["lgc"][:, 4:5], sb["lgc"][:, 5:6]))
                B.do("ve", "dv", ["mk2b", "mxb"], ["mk2"],
                     lambda e: e.tensor_scalar(sb["mk2"][:, :], sb["mk2b"][:, :], sb["mxb"][:, 7:8],
                                               None, ALU.is_equal))
                B.do("se", "ac", ["mk2", "lgc"], ["mk2b"],
                     lambda e: e.mul(sb["mk2b"][:, :], sb["mk2"][:, :], sb["lgc"][:, 6:7]))
                B.do("ve", "dv", ["mk1b", "lgc", "mk2b"], ["wgt32"],
                     lambda e, tc=tc: e.scalar_tensor_tensor(
                         sb["wgt32"][:, tc, :], sb["mk1b"][:, :], sb["lgc"][:, 5:6],
                         sb["mk2b"][:, :], ALU.mult, ALU.add))
                if tc == 0 and g == 0:
                    self.tap(B, "lgt_0", "lgt", sb["lgt"][:], [128, 8])
                    self.tap(B, "mk1_0", "mk1", sb["mk1"][:], [128, 8])
                    self.tap(B, "mk2_0", "mk2", sb["mk2"][:], [128, 8])
                    self.tap(B, "lgc_0", "lgc", sb["lgc"][:], [128, 8])
            if g == 0:
                self.tap(B, "wgt0", "wgt32", sb["wgt32"][:], [128, NCH, 8])
                self.tap(B, "xln3", "xlnb", sb["xlnb"][:], [128, KD, G])
            for ex in range(NE):
                for tc in range(NCH):
                    B.do("te", "pe", ["wgt32", "idf"], ["pP3"],
                         lambda e, tc=tc, ex=ex: e.transpose(
                             ps["P3"][0:1, 0:128], sb["wgt32"][:, tc, ex:ex + 1], sb["idf"][:, :]))
                    B.do("ve", "dv", ["pP3"], ["gater"],
                         lambda e, tc=tc: e.tensor_copy(sb["gater"][0:1, tc * L:(tc + 1) * L],
                                                        ps["P3"][0:1, 0:128]))
                B.do("te", "pe", ["gater", "onerow_f"], ["pP3"],
                     lambda e: e.matmul(ps["P3"][:, 0:G], sb["onerow_f"][0:1, :], sb["gater"][0:1, :],
                                        start=True, stop=True))
                B.do("ve", "dv", ["pP3"], ["gb"],
                     lambda e: e.tensor_copy(sb["gb"][:, :], ps["P3"][:, 0:G]))
                if g == 0 and ex == 0:
                    self.tap(B, "gb0", "gb", sb["gb"][:], [128, G])
                for j in range(32):
                    wt = MK.W1RING[j % 6]
                    B.dma("sy", [], [wt],
                          lambda e, j=j, ex=ex, wt=wt: e.dma_start(
                              out=sb[wt][:], in_=io["scw1"][ex, j]))
                    pn = ("P0", "P1", "P2")[j % 3]
                    B.do("te", "pe", [wt, "xlnb"], ["p" + pn],
                         lambda e, pn=pn, wt=wt: [
                             e.matmul(ps[pn][:, 0:G], sb[wt][:, kd, :], sb["xlnb"][:, kd, :],
                                      start=(kd == 0), stop=(kd == KD - 1))
                             for kd in range(KD)][-1])
                    B.do("se", "ac", ["p" + pn, "b1s"], ["hbuf"],
                         lambda e, pn=pn, j=j, ex=ex: e.activation(
                             sb["hbuf"][:, j, :], ps[pn][:, 0:G], AF.Silu,
                             bias=sb["b1s"][:, j, ex:ex + 1]))
                for j in range(32):
                    wt = MK.W1RING[(j + 32) % 6]
                    B.dma("sy", [], [wt],
                          lambda e, j=j, ex=ex, wt=wt: e.dma_start(
                              out=sb[wt][:], in_=io["scw1"][ex, j + 32]))
                    pn = ("P0", "P1", "P2")[j % 3]
                    B.do("te", "pe", [wt, "xlnb"], ["p" + pn],
                         lambda e, pn=pn, wt=wt: [
                             e.matmul(ps[pn][:, 0:G], sb[wt][:, kd, :], sb["xlnb"][:, kd, :],
                                      start=(kd == 0), stop=(kd == KD - 1))
                             for kd in range(KD)][-1])
                    B.do("ve", "dv", ["p" + pn, "b1s", "hbuf"], ["hbuf"],
                         lambda e, pn=pn, j=j, ex=ex: e.scalar_tensor_tensor(
                             sb["hbuf"][:, j, :], ps[pn][:, 0:G],
                             sb["b1s"][:, j + 32, ex:ex + 1],
                             sb["hbuf"][:, j, :], ALU.add, ALU.mult))
                wv2f = self.wbig_v(32, 128)
                for mt in range(KD):
                    wk = "w2a" if mt % 2 == 0 else "wbig"
                    wap = sb["w2a"][:] if mt % 2 == 0 else wv2f
                    B.dma("sy", [], [wk],
                          lambda e, mt=mt, ex=ex, wap=wap: e.dma_start(
                              out=wap, in_=io["scw2"][ex, mt]))
                    pn = ("P0", "P1", "P2")[mt % 3]
                    B.do("te", "pe", [wk, "hbuf"], ["p" + pn],
                         lambda e, pn=pn, wap=wap: [
                             e.matmul(ps[pn][:, 0:G], wap[:, k, :], sb["hbuf"][:, k, :],
                                      start=(k == 0), stop=(k == 31))
                             for k in range(31 + 1)][-1])
                    bn = "m_b2_%d" % ex
                    if ex == 0:
                        B.do("ve", "dv", ["p" + pn, "pv", "gb"], ["accf"],
                             lambda e, pn=pn, mt=mt, bn=bn: e.scalar_tensor_tensor(
                                 sb["accf"][:, mt, :], ps[pn][:, 0:G],
                                 sb["pv"][:, mt, PV[bn]:PV[bn] + 1], sb["gb"][:, :],
                                 ALU.add, ALU.mult))
                    else:
                        B.do("ve", "dv", ["p" + pn, "pv", "gb"], ["sc1"],
                             lambda e, pn=pn, mt=mt, bn=bn: e.scalar_tensor_tensor(
                                 sb["sc1"][:, :], ps[pn][:, 0:G],
                                 sb["pv"][:, mt, PV[bn]:PV[bn] + 1], sb["gb"][:, :],
                                 ALU.add, ALU.mult))
                        B.do("ve", "dv", ["sc1", "accf"], ["accf"],
                             lambda e, mt=mt: e.tensor_add(sb["accf"][:, mt, :],
                                                           sb["accf"][:, mt, :],
                                                           sb["sc1"][:, :]))
            if g == 0:
                self.tap(B, "acc_fin", "accf", sb["accf"][:], [128, KD, G])
            for mt in range(KD):
                B.do("ve", "dv", ["accf", "xs"], ["xs"],
                     lambda e, mt=mt, g0=g0: e.tensor_add(sb["xs"][:, mt, g0:g0 + G],
                                                          sb["xs"][:, mt, g0:g0 + G],
                                                          sb["accf"][:, mt, :]))
            if self.gw:
                B.dma("sy", ["xs"], [],
                      lambda e, gr=gr: e.dma_start(
                          out=io[dst_name].rearrange("(k p) t -> p k t", p=128)[:, :, gr:gr + G],
                          in_=sb["xs"][:]))



    def alloc_l0f_fused(self):
        s = self.sbuf
        if "w1a" not in self.sb:
            s("w1a", [128, KD, 128], BF16)
            s("w1b", [128, KD, 128], BF16)
        if "w2a" not in self.sb:
            s("w2a", [128, 32, 128], BF16)

    def emit_reload_wqkv(self, B):
        self.emit_build_omegas(B)
        sb = self.sb
        B.dma("sy", [], ["wqkv"],
              lambda e: e.dma_start(out=sb["wqkv"][:], in_=self.io["scqkv"]))
        B.dma("sy", [], ["wbig"],
              lambda e: e.dma_start(out=sb["wbig"][:], in_=self.io["scwo"]))


def build_l1a_launch(spine=False):
    mk = MK("l1a", gw=True)
    nc = mk.nc
    mk.dram_in("xmn", [D, TL])
    if spine:
        mk.dram_in("xsp", [D, TL])
    mk.dram_in("role", [1, 1])
    mk.dram_in("rolecol", [128, 1])
    mk.dram_in("scolh", [128, 1])
    mk.dram_in("quads", [2, 2])
    mk.dram_in("pvecT", [D, NV])
    for k in ("tm_w1", "td_w1", "td_w2", "Wr", "Wk", "Wv", "Wg", "Wo",
              "Wk_ffn", "Wv_ffn", "Wr_ffn"):
        mk.dram_in(k, list(L1A_W_SHAPES[k]))
    mk.dram_in("tm_w2", [5, 32, D])
    mk.dram_in("lnxgrow", [128, D])
    mk.dram_in("lnxbrow", [128, D])
    mk.alloc_predump(moe=False, l0f=False, l1a=True)
    mk.dram_out("x3mn", [D, TL])
    mk.alloc_common(l0a=False)
    mk.alloc_l1a()
    if "w1a" not in mk.sb:
        mk.sbuf("w1a", [128, KD, 128], BF16)
        mk.sbuf("w1b", [128, KD, 128], BF16)
    if "w2a" not in mk.sb:
        mk.sbuf("w2a", [128, 32, 128], BF16)
    with nc.body("prog", valid_engines=[ET.Pool, ET.Activation, ET.PE, ET.DVE, ET.SP]) as _:
        B = BB(nc)
        B.conflict("xsn", "vtm")
        mk.emit_setup(B, l0a=False)
        mk.emit_l1a_setup(B)
        mk.emit_predump(moe=False, l0f=False, l1a=True)
        if spine:
            mk.emit_l1a(B, zero_state=True, mask_by_role=False, src_name="xsp", dst_name=None)
            mk.emit_l1a(B, zero_state=False, mask_by_role=True, src_name="xmn", dst_name="x3mn")
        else:
            mk.emit_l1a(B, zero_state=True, mask_by_role=False, src_name="xmn", dst_name="x3mn")
        B.fence_all("sy")
    return mk


def run_l1a_test(p):
    h = prep_host(p)
    mk = build_l1a_launch()
    in_maps = []
    for c in range(8):
        b, half = c // 2, c % 2
        m = {"xmn": h["x_fm"][(b, half)], "role": np.array([[float(half)]], np.float32),
             "rolecol": np.full((128, 1), float(half), np.float32), "scolh": h["scolh"],
             "quads": h["quads"], "pvecT": h["pvecT"], "tm_w2": h["tm_w2"],
             "lnxgrow": h["lnxgrow"], "lnxbrow": h["lnxbrow"]}
        for k in ("tm_w1", "td_w1", "td_w2", "Wr", "Wk", "Wv", "Wg", "Wo",
                  "Wk_ffn", "Wv_ffn", "Wr_ffn"):
            m[k] = h[k]
        in_maps.append(m)
    return run_bass_kernel_spmd(mk.nc, in_maps, core_ids=list(range(8)))


def build_l1f_launch(dbg=False):
    mk = MK("l1f", gw=True, dbg=dbg)
    nc = mk.nc
    mk.dram_in("xmn", [D, TL])
    mk.dram_in("role", [1, 1])
    mk.dram_in("rolecol", [128, 1])
    mk.dram_in("scolh", [128, 1])
    mk.dram_in("quads", [2, 2])
    mk.dram_in("pvecT", [D, NV])
    mk.dram_in("rwT", [D, 8])
    mk.dram_in("rbL", [128, 8])
    mk.dram_in("m_b1L", [128, 512])
    mk.io["m_w1T"] = [mk.nc.dram_tensor("m_w1T_%d" % ex, [64, 128, KD, 128], F32,
                                        kind="ExternalInput").ap() for ex in range(NE)]
    mk.io["m_w2T"] = [mk.nc.dram_tensor("m_w2T_%d" % ex, [KD, 128, 32, 128], F32,
                                        kind="ExternalInput").ap() for ex in range(NE)]
    mk.alloc_predump(moe=True, l0f=False)
    mk.dram_out("x4mn", [D, TL])
    mk.alloc_common(l0a=False)
    if "wbig" not in mk.sb:
        mk.sbuf("wbig", [128, 32, 128], BF16)
    mk.alloc_l1f()
    with nc.body("prog", valid_engines=[ET.Pool, ET.Activation, ET.PE, ET.DVE, ET.SP]) as _:
        B = BB(nc)
        mk.emit_setup(B, l0a=False)
        mk.emit_l1f_setup(B)
        mk.emit_predump(moe=True, l0f=False)
        mk.emit_l1f(B, src_name="xmn", dst_name="x4mn")
        B.fence_all("sy")
    return mk


def run_l1f_test(p, dbg=False):
    h = prep_host(p)
    mk = build_l1f_launch(dbg)
    in_maps = []
    for c in range(8):
        b, half = c // 2, c % 2
        m = {"xmn": h["x_fm"][(b, half)], "role": np.array([[float(half)]], np.float32),
             "rolecol": np.full((128, 1), float(half), np.float32), "scolh": h["scolh"],
             "quads": h["quads"], "pvecT": h["pvecT"], "rwT": h["rwT"],
             "rbL": h["rbL"], "m_b1L": h["m_b1L"]}
        for ex in range(NE):
            m["m_w1T_%d" % ex] = h["m_w1T"][ex]
            m["m_w2T_%d" % ex] = h["m_w2T"][ex]
        in_maps.append(m)
    return run_bass_kernel_spmd(mk.nc, in_maps, core_ids=list(range(8)))



def build_l0f_launch(spine=False):
    mk = MK("l0f")
    nc = mk.nc
    mk.dram_in("xmn", [D, TL])
    mk.dram_in("role", [1, 1])
    mk.dram_in("rolecol", [128, 1])
    mk.dram_in("scolh", [128, 1])
    mk.dram_in("quads", [2, 2])
    mk.dram_in("pvecT", [D, NV])
    mk.dram_in("w1f0", [32, 128, KD, 128])
    mk.dram_in("w2f0", [KD, 128, 32, 128])
    mk.alloc_predump(moe=False, l0f=True)
    mk.dram_out("x2mn", [D, TL])
    if spine:
        mk.dram_in("xsp", [D, TL])
        mk.dram_out("x2sp", [D, TL])
    mk.alloc_common(l0a=False)
    mk.alloc_l0f()
    with nc.body("prog", valid_engines=[ET.Pool, ET.Activation, ET.PE, ET.DVE, ET.SP]) as _:
        B = BB(nc)
        mk.emit_setup(B, l0a=False)
        mk.emit_predump(moe=False, l0f=True)
        if spine:
            mk.emit_load_stream(B, "xsp")
            mk.emit_l0f(B)
            mk.emit_store_stream(B, "x2sp")
        mk.emit_load_stream(B, "xmn")
        mk.emit_l0f(B)
        mk.emit_store_stream(B, "x2mn")
        B.fence_all("sy")
    return mk


def run_l0f_test(p):
    h = prep_host(p)
    mk = build_l0f_launch()
    in_maps = []
    for c in range(8):
        b, half = c // 2, c % 2
        m = {"xmn": h["x_fm"][(b, half)], "role": np.array([[float(half)]], np.float32),
             "rolecol": np.full((128, 1), float(half), np.float32), "scolh": h["scolh"],
             "quads": h["quads"], "pvecT": h["pvecT"],
             "w1f0": h["w1f0"], "w2f0": h["w2f0"]}
        in_maps.append(m)
    return run_bass_kernel_spmd(mk.nc, in_maps, core_ids=list(range(8)))




def build_l0a_launch(spine_and_main, dbg=False):
    mk = MK("l0a", dbg=dbg)
    nc = mk.nc
    mk.dram_in("xmn", [D, TL])
    if spine_and_main:
        mk.dram_in("xsp", [D, TL])
    mk.dram_in("role", [1, 1])
    mk.dram_in("rolecol", [128, 1])
    mk.dram_in("scolh", [128, 1])
    mk.dram_in("wqkvT", [128, KD, 3 * D])
    mk.dram_in("woT", [128, KD, D])
    mk.dram_in("omegaT", [64, 1024])
    mk.dram_in("quads", [2, 2])
    mk.dram_in("bqkv", [3, 1024])
    mk.dram_in("pvecT", [D, NV])
    mk.alloc_predump_l0a()
    mk.dram_out("x1mn", [D, TL])
    if spine_and_main:
        mk.dram_out("x1sp", [D, TL])
    mk.alloc_common()
    with nc.body("prog", valid_engines=[ET.Pool, ET.Activation, ET.PE, ET.DVE, ET.SP]) as _:
        B = BB(nc)
        mk.emit_predump_l0a()
        mk.emit_setup(B)
        if spine_and_main:
            mk.emit_load_stream(B, "xsp")
            mk.emit_l0a(B, zero_state=True, mask_state_by_role=False)
            mk.emit_store_stream(B, "x1sp")
            mk.emit_load_stream(B, "xmn")
            mk.emit_l0a(B, zero_state=False, mask_state_by_role=True)
            mk.emit_store_stream(B, "x1mn")
        else:
            mk.emit_load_stream(B, "xmn")
            mk.emit_l0a(B, zero_state=True, mask_state_by_role=False)
            mk.emit_store_stream(B, "x1mn")
        B.fence_all("sy")
    return mk


def run_l0a_test(p, spine_and_main=False, dbg=False):
    h = prep_host(p)
    mk = build_l0a_launch(spine_and_main, dbg=dbg)
    in_maps = []
    for c in range(8):
        b, half = c // 2, c % 2
        m = {"xmn": h["x_fm"][(b, half)], "role": np.array([[float(half)]], np.float32),
             "rolecol": np.full((128, 1), float(half), np.float32), "scolh": h["scolh"],
             "wqkvT": h["wqkvT"], "woT": h["woT"], "omegaT": h["omegaT"],
             "quads": h["quads"], "bqkv": h["bqkv"], "pvecT": h["pvecT"]}
        if spine_and_main:
            m["xsp"] = h["x_fm"][(b, 0)]
        in_maps.append(m)
    res = run_bass_kernel_spmd(mk.nc, in_maps, core_ids=list(range(8)))
    return res




def build_fused(dbg_streams=False):
    mk = MK("fused", gw=True)
    nc = mk.nc
    mk.dram_in("xmn", [D, TL])
    mk.dram_in("xsp", [D, TL])
    mk.dram_in("role", [1, 1])
    mk.dram_in("rolecol", [128, 1])
    mk.dram_in("scolh", [128, 1])
    mk.dram_in("quads", [2, 2])
    mk.dram_in("pvecT", [D, NV])
    mk.dram_in("wqkvT", [128, KD, 3 * D])
    mk.dram_in("woT", [128, KD, D])
    mk.dram_in("omegaT", [64, 1024])
    mk.dram_in("bqkv", [3, 1024])
    mk.dram_in("w1f0", [32, 128, KD, 128])
    mk.dram_in("w2f0", [KD, 128, 32, 128])
    for k in ("tm_w1", "td_w1", "td_w2", "Wr", "Wk", "Wv", "Wg", "Wo",
              "Wk_ffn", "Wv_ffn", "Wr_ffn"):
        mk.dram_in(k, list(L1A_W_SHAPES[k]))
    mk.dram_in("tm_w2", [5, 32, D])
    mk.dram_in("lnxgrow", [128, D])
    mk.dram_in("lnxbrow", [128, D])
    mk.dram_in("rwT", [D, 8])
    mk.dram_in("rbL", [128, 8])
    mk.dram_in("m_b1L", [128, 512])
    mk.io["m_w1T"] = [nc.dram_tensor("m_w1T_%d" % ex, [64, 128, KD, 128], F32,
                                     kind="ExternalInput").ap() for ex in range(NE)]
    mk.io["m_w2T"] = [nc.dram_tensor("m_w2T_%d" % ex, [KD, 128, 32, 128], F32,
                                     kind="ExternalInput").ap() for ex in range(NE)]
    mk.alloc_predump(moe=True, l0f=True, l1a=True)
    mk.alloc_predump_l0a()
    mk.dram_out("x4mn", [D, TL])
    kind = "ExternalOutput" if dbg_streams else "Internal"
    mk.io["s1"] = nc.dram_tensor("s1", [D, TL], F32, kind=kind).ap()
    mk.io["s2"] = nc.dram_tensor("s2", [D, TL], F32, kind=kind).ap()
    mk.io["s3"] = nc.dram_tensor("s3", [D, TL], F32, kind=kind).ap()

    mk.alloc_common(l0a=True)
    addr = nc.lookup_mls(mk.sb["wqkv"]).memorylocations[0].addr
    mk.sb["hbuf"] = nc.alloc_sbuf_tensor_at("hbuf", [128, 32, G], BF16, offset=addr)
    mk.sb["accf"] = nc.alloc_sbuf_tensor_at("accf", [128, KD, G], F32, offset=addr + 32768)
    mk.sbuf("w1a", [128, KD, 128], BF16)
    mk.sbuf("w1b", [128, KD, 128], BF16)
    mk.sbuf("w2a", [128, 32, 128], BF16)
    addr2 = nc.lookup_mls(mk.sb["w2a"]).memorylocations[0].addr
    mk.sb["gtm"] = nc.alloc_sbuf_tensor_at("gtm", [128, NCH, 1024], BF16, offset=addr2)
    aA = nc.lookup_mls(mk.sb["omA"]).memorylocations[0].addr
    aB = nc.lookup_mls(mk.sb["omB"]).memorylocations[0].addr
    mk.sb["lnxg_r"] = nc.alloc_sbuf_tensor_at("lnxg_r", [128, 1024], BF16, offset=aA)
    mk.sb["lnxb_r"] = nc.alloc_sbuf_tensor_at("lnxb_r", [128, 1024], BF16, offset=aB)
    mk.alloc_l1a()
    mk.alloc_l0f_fused()
    aK = nc.lookup_mls(mk.sb["kft"]).memorylocations[0].addr
    aO = nc.lookup_mls(mk.sb["otm"]).memorylocations[0].addr
    aL = nc.lookup_mls(mk.sb["loraf"]).memorylocations[0].addr
    mk.sb["gater"] = nc.alloc_sbuf_tensor_at("gater", [1, G], F32, offset=aK)
    mk.sb["gb"] = nc.alloc_sbuf_tensor_at("gb", [128, G], BF16, offset=aO)
    mk.sb["b1s"] = nc.alloc_sbuf_tensor_at("b1s", [128, 64, 8], F32, offset=aL)
    aV = nc.lookup_mls(mk.sb["vtm"]).memorylocations[0].addr
    for _i, _nm in enumerate(("w1c", "w1d", "w1e", "w1f")):
        mk.sb[_nm] = nc.alloc_sbuf_tensor_at(_nm, [128, KD, 128], BF16,
                                             offset=aV + _i * 2048)
    mk.alloc_l1f()
    with nc.body("prog", valid_engines=[ET.Pool, ET.Activation, ET.PE, ET.DVE, ET.SP]) as _:
        B = BB(nc)
        B.conflict("hbuf", "wqkv")
        B.conflict("accf", "wqkv")
        B.conflict("gtm", "w2a")
        B.conflict("lnxg_r", "omA")
        B.conflict("lnxb_r", "omB")
        B.conflict("gater", "kft")
        B.conflict("gb", "otm")
        B.conflict("b1s", "loraf")
        for _nm in ("w1c", "w1d", "w1e", "w1f"):
            B.conflict(_nm, "vtm")
            B.conflict(_nm, "xsn")
        B.conflict("xsn", "vtm")
        with nc.named_scope("setup"):
            mk.emit_predump_l0a()
            mk.emit_setup(B, l0a=True)
            mk.emit_l1a_setup(B)
            mk.emit_l1f_setup(B)
            mk.emit_predump(moe=False, l0f=True, l1a=True)
        with nc.named_scope("sL0A"):
            mk.emit_l0a(B, zero_state=True, mask_state_by_role=False,
                        src_name="xsp", dst_name="s1")
        with nc.named_scope("sL0F"):
            mk.emit_l0f(B, src_name="s1", dst_name="s2")
        with nc.named_scope("sL1A"):
            mk.emit_l1a(B, zero_state=True, mask_by_role=False, src_name="s2", dst_name=None,
                        reduced=True)
        with nc.named_scope("reload"):
            mk.emit_reload_wqkv(B)
            mk.emit_predump(moe=True, l0f=False, l1a=False)
        with nc.named_scope("mL0A"):
            mk.emit_l0a(B, zero_state=False, mask_state_by_role=True,
                        src_name="xmn", dst_name="s1")
        with nc.named_scope("mL0F"):
            mk.emit_l0f(B, src_name="s1", dst_name="s2")
        with nc.named_scope("mL1A"):
            mk.emit_l1a(B, zero_state=False, mask_by_role=True, src_name="s2", dst_name="s3")
        with nc.named_scope("mL1F"):
            mk.emit_l1f(B, src_name="s3", dst_name="x4mn")
        B.fence_all("sy")
    return mk


_PROGS = {}
FUSED = True


def _get_progs():
    if "p" not in _PROGS:
        _PROGS["p"] = (build_l0a_launch(True), build_l0f_launch(True),
                       build_l1a_launch(True), build_l1f_launch())
    return _PROGS["p"]


def _get_fused():
    if "f" not in _PROGS:
        _PROGS["f"] = build_fused()
    return _PROGS["f"]


def kernel(**inputs):
    h = prep_host(inputs)
    if FUSED:
        mk = _get_fused()
        im = []
        for c in range(8):
            b, half = c // 2, c % 2
            m = {"xmn": h["x_fm"][(b, half)], "xsp": h["x_fm"][(b, 0)],
                 "role": np.array([[float(half)]], np.float32),
                 "rolecol": np.full((128, 1), float(half), np.float32),
                 "scolh": h["scolh"], "quads": h["quads"], "pvecT": h["pvecT"],
                 "wqkvT": h["wqkvT"], "woT": h["woT"], "omegaT": h["omegaT"],
                 "bqkv": h["bqkv"], "w1f0": h["w1f0"], "w2f0": h["w2f0"],
                 "tm_w2": h["tm_w2"], "lnxgrow": h["lnxgrow"], "lnxbrow": h["lnxbrow"],
                 "rwT": h["rwT"], "rbL": h["rbL"], "m_b1L": h["m_b1L"]}
            for k in ("tm_w1", "td_w1", "td_w2", "Wr", "Wk", "Wv", "Wg", "Wo",
                      "Wk_ffn", "Wv_ffn", "Wr_ffn"):
                m[k] = h[k]
            for ex in range(NE):
                m["m_w1T_%d" % ex] = h["m_w1T"][ex]
                m["m_w2T_%d" % ex] = h["m_w2T"][ex]
            im.append(m)
        rr = run_bass_kernel_spmd(mk.nc, im, core_ids=list(range(8))).results
        out = np.zeros((4, 4096, 1024), np.float32)
        for c in range(8):
            b, half = c // 2, c % 2
            out[b, half * 2048:(half + 1) * 2048] = rr[c]["x4mn"].T
        return out
    mka, mkf0, mk1a, mk1f = _get_progs()
    base = []
    for c in range(8):
        b, half = c // 2, c % 2
        base.append({"role": np.array([[float(half)]], np.float32),
                     "rolecol": np.full((128, 1), float(half), np.float32),
                     "scolh": h["scolh"], "quads": h["quads"], "pvecT": h["pvecT"]})
    im = []
    for c in range(8):
        b, half = c // 2, c % 2
        m = dict(base[c])
        m.update({"xmn": h["x_fm"][(b, half)], "xsp": h["x_fm"][(b, 0)],
                  "wqkvT": h["wqkvT"], "woT": h["woT"], "omegaT": h["omegaT"],
                  "bqkv": h["bqkv"]})
        im.append(m)
    r1 = run_bass_kernel_spmd(mka.nc, im, core_ids=list(range(8))).results
    im = []
    for c in range(8):
        m = dict(base[c])
        m.update({"xmn": r1[c]["x1mn"], "xsp": r1[c]["x1sp"],
                  "w1f0": h["w1f0"], "w2f0": h["w2f0"]})
        im.append(m)
    r2 = run_bass_kernel_spmd(mkf0.nc, im, core_ids=list(range(8))).results
    im = []
    for c in range(8):
        m = dict(base[c])
        m.update({"xmn": r2[c]["x2mn"], "xsp": r2[c]["x2sp"], "tm_w2": h["tm_w2"],
                  "lnxgrow": h["lnxgrow"], "lnxbrow": h["lnxbrow"]})
        for k in ("tm_w1", "td_w1", "td_w2", "Wr", "Wk", "Wv", "Wg", "Wo",
                  "Wk_ffn", "Wv_ffn", "Wr_ffn"):
            m[k] = h[k]
        im.append(m)
    r3 = run_bass_kernel_spmd(mk1a.nc, im, core_ids=list(range(8))).results
    im = []
    for c in range(8):
        m = dict(base[c])
        m.update({"xmn": r3[c]["x3mn"], "rwT": h["rwT"], "rbL": h["rbL"],
                  "m_b1L": h["m_b1L"]})
        for ex in range(NE):
            m["m_w1T_%d" % ex] = h["m_w1T"][ex]
            m["m_w2T_%d" % ex] = h["m_w2T"][ex]
        im.append(m)
    r4 = run_bass_kernel_spmd(mk1f.nc, im, core_ids=list(range(8))).results
    out = np.zeros((4, 4096, 1024), np.float32)
    for c in range(8):
        b, half = c // 2, c % 2
        out[b, half * 2048:(half + 1) * 2048] = r4[c]["x4mn"].T
    return out



